# version 17
# baseline (speedup 1.0000x reference)
import contextlib
import numpy as np
import ml_dtypes

import concourse.bass as bass
import concourse.mybir as mybir
from concourse.bass_utils import run_bass_kernel_spmd

F32 = mybir.dt.float32
BF16 = mybir.dt.bfloat16
I32 = mybir.dt.int32
AX = mybir.AxisListType
OP = mybir.AluOpType
ACTF = mybir.ActivationFunctionType

NT = 18
NOWN = 16
TW = NT * 128
NHEAD = 9
INV32 = 1.0 / 32.0
EPS = 1e-5


class Ctr:
    __slots__ = ("h", "n")
    def __init__(self, h):
        self.h = h
        self.n = 0


def _load_act_set(e, idx):
    return e.add_instruction(mybir.InstLoadActFuncSet(
        name=e.bass.get_next_instruction_name(),
        act_func_set_id=idx, ins=[], outs=[]))


class Em:
    def __init__(self, live, e):
        self.live = live
        self.e = e
        self.ctrs = None
        self.cur = None
        self.mute = False

    @contextlib.contextmanager
    def on(self, eng):
        prev = self.cur
        self.cur = eng
        yield
        self.cur = prev

    def w(self, ctr, val):
        if self.mute:
            return
        if self.cur == self.live and val > 0:
            self.e.wait_ge(ctr.h, val)

    def i(self, fn, inc=1):
        if self.mute:
            return
        c = self.ctrs[self.cur]
        if self.cur == self.live:
            inst = fn(self.e)
            if inc:
                inst.then_inc(c.h, inc)
        c.n += inc

    def d(self, ctr, fn):
        if self.mute:
            return
        if self.cur == self.live and ctr.n > 0:
            self.e.wait_ge(ctr.h, ctr.n)
        if self.cur == self.live:
            fn(self.e).then_inc(ctr.h, 16)
        ctr.n += 16

    def fence(self):
        c = self.ctrs[self.cur]
        self.w(c, c.n)


def build_program(phases="ABCDE", nheads=NHEAD, noload=False):
    nc = bass.Bass("TRN2", target_bir_lowering=False)

    d_xtm = nc.dram_tensor("xw_tm", [NT, 128, 1024], F32, kind="ExternalInput").ap()
    d_xT = nc.dram_tensor("xw_T", [128, 8, TW], F32, kind="ExternalInput").ap()
    d_wq = nc.dram_tensor("wq", [128, 8, 1024], F32, kind="ExternalInput").ap()
    d_wk = nc.dram_tensor("wk", [128, 8, 1024], F32, kind="ExternalInput").ap()
    d_wv = nc.dram_tensor("wv", [128, 8, 1024], F32, kind="ExternalInput").ap()
    d_wo = nc.dram_tensor("wo", [128, 8, 1024], F32, kind="ExternalInput").ap()
    d_w1T = nc.dram_tensor("w1T", [128, 8, 4096], F32, kind="ExternalInput").ap()
    d_w2T = nc.dram_tensor("w2T", [128, 32, 1024], F32, kind="ExternalInput").ap()
    d_bq = nc.dram_tensor("bq_col", [128, 8], F32, kind="ExternalInput").ap()
    d_bk = nc.dram_tensor("bk_col", [128, 8], F32, kind="ExternalInput").ap()
    d_bv = nc.dram_tensor("bv_col", [128, 8], F32, kind="ExternalInput").ap()
    d_borow = nc.dram_tensor("bo_row", [1, 1024], F32, kind="ExternalInput").ap()
    d_cwa = nc.dram_tensor("cw3a", [128, 32], F32, kind="ExternalInput").ap()
    d_cwb = nc.dram_tensor("cw3b", [128, 32], F32, kind="ExternalInput").ap()
    d_cwc = nc.dram_tensor("cw3c", [128, 32], F32, kind="ExternalInput").ap()
    d_cb = nc.dram_tensor("cb32", [128, 32], F32, kind="ExternalInput").ap()
    d_ones = nc.dram_tensor("ones128", [128, 128], F32, kind="ExternalInput").ap()
    d_onesw = nc.dram_tensor("ones_row", [1, 384], F32, kind="ExternalInput").ap()
    d_ident = nc.dram_tensor("ident", [128, 128], F32, kind="ExternalInput").ap()
    d_nh = nc.dram_tensor("neghalf", [128, 1], F32, kind="ExternalInput").ap()
    d_mask = nc.dram_tensor("maskLR", [128, 2], F32, kind="ExternalInput").ap()
    d_out = nc.dram_tensor("out", [NOWN * 128, 1024], F32, kind="ExternalOutput").ap()

    d_qs = nc.dram_tensor("q_scr", [128, 8, TW], BF16, kind="Internal").ap()
    d_ks = nc.dram_tensor("k_scr", [128, 8, TW], BF16, kind="Internal").ap()
    d_vs = nc.dram_tensor("v_scr", [128, 8, TW], BF16, kind="Internal").ap()
    d_us = nc.dram_tensor("u_scr", [NT, 128, 1024], F32, kind="Internal").ap()
    d_xqs = nc.dram_tensor("xqT_scr", [NT, 128, 8, 128], BF16, kind="Internal").ap()
    d_oss = nc.dram_tensor("os_scr", [NHEAD, 1, 4096], F32, kind="Internal").ap()
    d_ys = nc.dram_tensor("y_scr", [NOWN, 128, 1024], F32, kind="Internal").ap()

    with contextlib.ExitStack() as stk:
        SBT = lambda st, name, shape, dt=F32: st.enter_context(
            nc.sbuf_tensor(name, shape, dt))
        sb = lambda name, shape, dt=F32: SBT(stk, name, shape, dt)
        psum = lambda name, shape: stk.enter_context(nc.psum_tensor(name, shape, F32))

        ones = sb("ones", [128, 128])
        onesw = sb("onesw", [1, 384])
        ident = sb("ident_sb", [128, 128])
        cwa = sb("cwa", [128, 32]); cwb = sb("cwb", [128, 32])
        cwc = sb("cwc", [128, 32]); cb32 = sb("cb32sb", [128, 32])
        neghalf = sb("neghalfsb", [128, 1])
        masklr = sb("masklrsb", [128, 2])
        sw1 = sb("sw1", [1, 1]); sw2 = sb("sw2", [1, 1])
        sw2col = sb("sw2col", [128, 1])
        thr1 = sb("thr1", [128, 1]); nthr1 = sb("nthr1", [128, 1])
        thr2 = sb("thr2", [128, 1]); nthr2 = sb("nthr2", [128, 1])
        wsum = sb("wsum", [128, 1])
        stat = sb("stat", [128, 8])
        q2st = sb("q2st", [128, 4])
        zq4 = sb("zq4", [128, 8])
        Zcol = sb("Zcol", [128, 1])
        rcol = sb("rcol", [128, 1])
        rb16 = sb("rb16", [128, 2], BF16)
        m1row = sb("m1row", [1, TW])
        borow = sb("borow", [1, 1024])
        s2row = sb("s2row", [1, 129])
        scal = sb("scal", [1, 8])
        mhrow = sb("mhrow", [1, 2])
        sw1row = sb("sw1row", [1, 128])
        w1qT = sb("w1qT", [128, 8, 4096], BF16)
        w2qT = sb("w2qT", [128, 32, 1024], BF16)

        PT0 = psum("PT0", [128, 1024])
        PT1 = psum("PT1", [128, 1024])
        PT2 = psum("PT2", [128, 1024])
        PT3 = psum("PT3", [128, 512])
        PT4 = psum("PT4", [128, 512])

        h_ld = stk.enter_context(nc.semaphore("s_ld"))
        h_st = stk.enter_context(nc.semaphore("s_st"))
        h_pe = stk.enter_context(nc.semaphore("s_pe"))
        h_ac = stk.enter_context(nc.semaphore("s_ac"))
        h_dv = stk.enter_context(nc.semaphore("s_dv"))
        h_l2 = stk.enter_context(nc.semaphore("s_l2"))

        ten = {}
        with contextlib.ExitStack() as sA:
            with contextlib.ExitStack() as sWQ:
                ten["big0"] = SBT(sWQ, "big0", [128, 4096])
                ten["c1b"] = SBT(sWQ, "c1b", [128, 1024])
                ten["c2b"] = SBT(sWQ, "c2b", [128, 1024])
            ten["wb"] = SBT(sA, "wb", [128, 8, 1024], BF16)
            ten["wld"] = SBT(sA, "wld", [128, 1024])
            ten["xtf"] = SBT(sA, "xtf", [128, 8, 384])
            ten["xts0"] = SBT(sA, "xts0", [128, 8, 384], BF16)
            ten["xts1"] = SBT(sA, "xts1", [128, 8, 384], BF16)
            for i in range(2):
                ten[f"stg{i}"] = SBT(sA, f"stg{i}", [128, 512], BF16)
            ten["bqr"] = SBT(sA, "bqr", [128, 8])
            ten["bkr"] = SBT(sA, "bkr", [128, 8])
            ten["bvr"] = SBT(sA, "bvr", [128, 8])
        with contextlib.ExitStack() as sBC:
            with contextlib.ExitStack() as sB:
                ten["Ebuf0"] = SBT(sB, "Ebuf0", [128, 4096], BF16)
                ten["Ebuf1"] = SBT(sB, "Ebuf1", [128, 4096], BF16)
                ten["Ebuf2"] = SBT(sB, "Ebuf2", [128, 4096], BF16)
                ten["K2"] = SBT(sB, "K2", [128, 4096], BF16)
                ten["Qh0"] = SBT(sB, "Qh0", [128, 8, 256], BF16)
                ten["Qh1"] = SBT(sB, "Qh1", [128, 8, 256], BF16)
                ten["ossb"] = SBT(sB, "ossb", [128, 1024])
                ten["esb"] = SBT(sB, "esb", [128, 4, 1024])
            with contextlib.ExitStack() as sC:
                ten["wob"] = SBT(sC, "wob", [128, 8, 1024], BF16)
                ten["osrow"] = SBT(sC, "osrow", [1, 4096])
                ten["Xt0"] = SBT(sC, "Xt0", [128, 1024])
                ten["lnin"] = SBT(sC, "lnin", [128, 1024])
                ten["utile"] = SBT(sC, "utile", [128, 1024])
                ten["qsc"] = SBT(sC, "qsc", [128, 1024])
                ten["zich"] = SBT(sC, "zich", [128, 512], I32)
                ten["Vt"] = SBT(sC, "Vt", [128, 8, 128], BF16)
                ten["Ot"] = SBT(sC, "Ot", [128, 8, 128], BF16)
                ten["xqtm"] = SBT(sC, "xqtm", [128, 1024])
                ten["xqTt"] = SBT(sC, "xqTt", [128, 8, 128], BF16)
        with contextlib.ExitStack() as sD:
            ten["zti"] = SBT(sD, "zti", [128, 32, 130])
            ten["htile"] = SBT(sD, "htile", [128, 32, 128])
            ten["x2q"] = SBT(sD, "x2q", [128, 16, 128], BF16)
            ten["x2qb"] = SBT(sD, "x2qb", [128, 16, 128], BF16)
            ten["zi16"] = SBT(sD, "zi16", [128, 16, 128], I32)
            ten["uld"] = SBT(sD, "uld", [128, 1024])
            ten["ypre"] = SBT(sD, "ypre", [128, 1024])
            ten["xqt0"] = SBT(sD, "xqt0", [128, 8, 132], BF16)
            ten["xqt1"] = SBT(sD, "xqt1", [128, 8, 132], BF16)
            ten["xqth"] = SBT(sD, "xqth", [128, 8, 1], BF16)
            ten["m12sb"] = SBT(sD, "m12sb", [128, 129])
            ten["s2sb"] = SBT(sD, "s2sb", [128, 128])
            ten["Mp"] = SBT(sD, "Mp", [128, 128])
            ten["mhsb"] = SBT(sD, "mhsb", [128, 1])
            ten["zedge"] = SBT(sD, "zedge", [128, 32, 1])
            ten["est"] = SBT(sD, "est", [128, 8])

        class T:
            pass
        for k, v in ten.items():
            setattr(T, k, v)

        def program(em):
            LD = Ctr(h_ld); ST = Ctr(h_st)
            PE = Ctr(h_pe); AC = Ctr(h_ac); DV = Ctr(h_dv)
            L2 = Ctr(h_l2)
            em.ctrs = {"tensor": PE, "vector": DV, "scalar": AC, "sync": None,
                       "gpsimd": None}

            def barrier():
                for eng in ("sync", "tensor", "vector", "scalar", "gpsimd"):
                    with em.on(eng):
                        for c in (LD, ST, PE, AC, DV, L2):
                            em.w(c, c.n)

            em.mute = "A" not in phases
            with em.on("sync"):
                for dst, src in ((ones, d_ones), (onesw, d_onesw),
                                 (ident, d_ident),
                                 (borow, d_borow), (cwa, d_cwa), (cwb, d_cwb),
                                 (cwc, d_cwc), (cb32, d_cb), (neghalf, d_nh),
                                 (masklr, d_mask)):
                    em.d(LD, lambda e, dst=dst, src=src: e.dma_start(dst[...], src))
            consts_ld = LD.n

            def wq_quant(dram, si, sw, thr, nthr, dstq, swcol=None):
                big0 = T.big0; c1b = T.c1b; c2b = T.c2b
                halves = [(big0[:, 0:2048], LD, "sync"),
                          (big0[:, 2048:4096], L2, "gpsimd")]

                def load_half(j):
                    buf, _, _ = halves[j % 2]
                    if dram is d_w1T:
                        return lambda e, j=j, buf=buf: e.dma_start(
                            buf, dram[:, j // 2,
                                      2048 * (j % 2):2048 * (j % 2) + 2048])
                    return lambda e, j=j, buf=buf: e.dma_start(
                        buf.rearrange("p (a b) -> p a b", a=2),
                        dram[:, 2 * j:2 * (j + 1), :])

                with em.on("vector"):
                    em.fence()
                    em.i(lambda e: e.memset(wsum[...], 0.0))
                buf_dv = [DV.n, DV.n]
                for j in range(16):
                    buf, ctr, eng = halves[j % 2]
                    with em.on(eng):
                        em.w(DV, buf_dv[j % 2])
                        em.d(ctr, load_half(j))
                    with em.on("vector"):
                        em.w(ctr, ctr.n)
                        em.fence()
                        em.i(lambda e, buf=buf: e.tensor_reduce(
                            c1b[:, 0:1], buf, AX.X, OP.add,
                            apply_absolute_value=True))
                        em.fence()
                        em.i(lambda e: e.tensor_add(wsum[...], wsum[...],
                                                    c1b[:, 0:1]))
                    buf_dv[j % 2] = DV.n
                with em.on("tensor"):
                    em.w(DV, DV.n)
                    em.i(lambda e: e.matmul(PT3[0:1, si:si + 1], ones[:, 0:1],
                                            wsum[...], start=True, stop=True))
                with em.on("vector"):
                    em.w(PE, PE.n)
                    em.fence()
                    em.i(lambda e: e.tensor_scalar(scal[0:1, si:si + 1],
                                                   PT3[0:1, si:si + 1],
                                                   1.0 / (4096.0 * 1024.0), 1e-5,
                                                   OP.mult, OP.max))
                    em.fence()
                    em.i(lambda e: e.tensor_copy(sw[...], scal[0:1, si:si + 1]))
                    em.fence()
                    em.i(lambda e: e.tensor_scalar(scal[0:1, si + 1:si + 2],
                                                   scal[0:1, si:si + 1], 0.5, None,
                                                   OP.mult))
                with em.on("tensor"):
                    em.w(DV, DV.n)
                    em.i(lambda e: e.matmul(PT3[:, 8 + si:9 + si], ones[0:1, :],
                                            scal[0:1, si + 1:si + 2],
                                            start=True, stop=True))
                with em.on("vector"):
                    em.w(PE, PE.n)
                    em.fence()
                    em.i(lambda e: e.tensor_copy(thr[...], PT3[:, 8 + si:9 + si]))
                    em.fence()
                    em.i(lambda e: e.tensor_scalar(nthr[...], thr[...], -1.0, None,
                                                   OP.mult))
                    if swcol is not None:
                        em.fence()
                        em.i(lambda e: e.tensor_scalar(swcol[...], thr[...], 2.0,
                                                       None, OP.mult))
                for j in range(16):
                    buf, ctr, eng = halves[j % 2]
                    with em.on(eng):
                        em.w(DV, buf_dv[j % 2])
                        em.d(ctr, load_half(j))
                    with em.on("vector"):
                        em.w(ctr, ctr.n)
                        for qq in range(2):
                            sl = slice(1024 * qq, 1024 * (qq + 1))
                            em.fence()
                            em.i(lambda e, sl=sl, buf=buf: e.tensor_scalar(
                                c1b[...], buf[:, sl], thr[...], None, OP.is_gt))
                            em.fence()
                            em.i(lambda e, sl=sl, buf=buf: e.tensor_scalar(
                                c2b[...], buf[:, sl], nthr[...], None, OP.is_lt))
                            em.fence()
                            if dstq is w1qT:
                                off = 2048 * (j % 2) + 1024 * qq
                                em.i(lambda e, j=j, off=off: e.tensor_sub(
                                    w1qT[:, j // 2, off:off + 1024],
                                    c1b[...], c2b[...]))
                            else:
                                g = 2 * j + qq
                                em.i(lambda e, g=g: e.tensor_sub(
                                    w2qT[...].rearrange("p a b -> p (a b)")
                                    [:, 1024 * g:1024 * (g + 1)],
                                    c1b[...], c2b[...]))
                    buf_dv[j % 2] = DV.n

            wq_quant(d_w1T, 0, sw1, thr1, nthr1, w1qT)
            wq_quant(d_w2T, 2, sw2, thr2, nthr2, w2qT, swcol=sw2col)
            with em.on("vector"):
                em.fence()
                em.i(lambda e: e.tensor_scalar(sw1row[...], ones[0:1, :], sw1[...],
                                               None, OP.mult))
            barrier()

            for dsrc, dst, scale in ((d_bq, T.bqr, INV32), (d_bk, T.bkr, INV32),
                                     (d_bv, T.bvr, 1.0)):
                with em.on("sync"):
                    em.w(DV, DV.n)
                    em.d(LD, lambda e, dsrc=dsrc, dst=dst:
                         e.dma_start(dst[...], dsrc))
                with em.on("vector"):
                    em.w(LD, LD.n)
                    em.fence()
                    em.i(lambda e, dst=dst, scale=scale:
                         e.tensor_scalar(dst[...], dst[...], scale, None, OP.mult))

            nslab = TW // 128
            psl = [PT0[:, 0:512], PT0[:, 512:1024], PT1[:, 0:512], PT1[:, 512:1024],
                   PT2[:, 0:512], PT2[:, 512:1024]]
            psl_free = [0] * 6
            stg_free = [0] * 2
            slot_i = 0
            stg_i = 0
            stgs = [ten["stg0"], ten["stg1"]]
            xtss = [T.xts0, T.xts1]
            for (dwsrc, bias, dscr, scale) in (
                    (d_wq, T.bqr, d_qs, INV32), (d_wk, T.bkr, d_ks, INV32),
                    (d_wv, T.bvr, d_vs, 1.0)):
                for kch in range(8):
                    with em.on("sync"):
                        em.w(DV, DV.n)
                        em.w(PE, PE.n)
                        em.d(LD, lambda e, dwsrc=dwsrc, kch=kch:
                             e.dma_start(T.wld[...], dwsrc[:, kch, :]))
                    with em.on("vector"):
                        em.w(LD, LD.n)
                        em.fence()
                        em.i(lambda e, kch=kch, scale=scale:
                             e.tensor_scalar(T.wb[:, kch, :], T.wld[...], scale,
                                             None, OP.mult))
                wcast = DV.n
                batches = [(384 * i, 384) for i in range(6)]
                slab_pe = [0] * len(batches)
                slab_cast = [0, 0]
                for si, (s0, wdt) in enumerate(batches):
                    xb = xtss[si % 2]
                    with em.on("gpsimd"):
                        em.w(DV, slab_cast[si % 2])
                        if si >= 1:
                            em.w(PE, slab_pe[si - 1])
                        em.d(L2, lambda e, s0=s0, wdt=wdt: e.dma_start(
                            T.xtf[:, :, 0:wdt], d_xT[:, :, s0:s0 + wdt]))
                    with em.on("vector"):
                        em.w(L2, L2.n)
                        em.fence()
                        em.i(lambda e, xb=xb, wdt=wdt: e.tensor_copy(
                            xb[:, :, 0:wdt], T.xtf[:, :, 0:wdt]))
                    slab_cast[si % 2] = DV.n
                    xcast = DV.n
                    for mch in range(8):
                        slot = slot_i % 6
                        slot_i += 1
                        pslot = psl[slot]
                        with em.on("tensor"):
                            em.w(DV, max(xcast, wcast))
                            em.w(AC, psl_free[slot])
                            for kch in range(8):
                                em.i(lambda e, pslot=pslot, kch=kch, mch=mch,
                                     xb=xb, wdt=wdt:
                                     e.matmul(pslot[:, 0:wdt],
                                              T.wb[:, kch, 128 * mch:128 * (mch + 1)],
                                              xb[:, kch, 0:wdt],
                                              start=(kch == 0), stop=(kch == 7)),
                                     inc=(1 if kch == 7 else 0))

                        sgi = stg_i % 2
                        sgb = stgs[sgi]
                        stg_i += 1
                        with em.on("scalar"):
                            em.w(PE, PE.n)
                            em.w(ST, stg_free[sgi])
                            em.fence()
                            em.i(lambda e, sgb=sgb, pslot=pslot, wdt=wdt,
                                 bias=bias, mch=mch:
                                 e.activation(sgb[:, 0:wdt], pslot[:, 0:wdt],
                                              ACTF.Identity,
                                              bias=bias[:, mch:mch + 1]))
                        psl_free[slot] = AC.n
                        with em.on("sync"):
                            em.w(AC, AC.n)
                            em.d(ST, lambda e, sgb=sgb, dscr=dscr, mch=mch,
                                 s0=s0, wdt=wdt:
                                 e.dma_start(dscr[:, mch, s0:s0 + wdt],
                                             sgb[:, 0:wdt]))
                        stg_free[sgi] = ST.n
                    slab_pe[si] = PE.n
            barrier()
            em.mute = False
            if "B" not in phases:
                return
            bi = phases.index("B")
            BL = int(phases[bi + 1]) if (bi + 1 < len(phases)
                                         and phases[bi + 1].isdigit()) else 4

            slot_ps = [PT0, PT1, PT2]
            slot_free_dv = [0, 0, 0]
            esb_free_ac = [0, 0, 0, 0]
            ebuf_free_pe = [0, 0, 0]
            zq4_free_dv = [0, 0]
            mv_evac_dv = DV.n
            Ebufs = [T.Ebuf0, T.Ebuf1, T.Ebuf2]
            Qhs = [T.Qh0, T.Qh1]
            osst_st = [0] * NHEAD
            prev_mv_pe = 0
            prev_r_dv = 0
            prev_osst = 0
            head_score_pe = 0
            for h in range(nheads):
                if not noload:
                    with em.on("sync"):
                        em.w(PE, head_score_pe if h >= 1 else PE.n)
                        t0 = 256 * h
                        em.d(LD, lambda e, t0=t0: e.dma_start(
                            T.K2[0:64, 0:2048], d_ks[0:64, :, t0:t0 + 256]))
                        em.d(LD, lambda e, t0=t0: e.dma_start(
                            T.K2[0:64, 2048:4096], d_ks[64:128, :, t0:t0 + 256]))
                        em.d(LD, lambda e, t0=t0: e.dma_start(
                            T.K2[64:128, 0:2048], d_ks[0:64, :, t0:t0 + 256]))
                        em.d(LD, lambda e, t0=t0: e.dma_start(
                            T.K2[64:128, 2048:4096],
                            d_ks[64:128, :, t0:t0 + 256]))
                        em.d(LD, lambda e, t0=t0, h=h: e.dma_start(
                            Qhs[h % 2][...], d_qs[:, :, t0:t0 + 256]))
                ld_done = LD.n
                qh = Qhs[h % 2]

                def emit_chain(jj, dv_gate):
                    ebc = Ebufs[jj % 3]
                    with em.on("tensor"):
                        em.w(DV, dv_gate)
                        if jj == 0:
                            em.w(DV, mv_evac_dv)
                        for q in range(8):
                            pt = PT3 if q < 4 else PT4
                            em.i(lambda e, pt=pt, q=q, ebc=ebc, jj=jj:
                                 e.matmul(pt[32 * (q % 4):32 * (q % 4) + 1,
                                             0:512],
                                          rb16[:, jj % 2:jj % 2 + 1],
                                          ebc[:, 512 * q:512 * (q + 1)],
                                          start=(jj == 0), stop=(jj == 31),
                                          tile_position=(0, 32 * (q % 4))),
                                 inc=(1 if q == 7 else 0))
                    return PE.n

                r_dv_arr = [0] * 32
                for j in range(32 if BL >= 2 else 0):
                    jslots = [(4 * j + g) % 3 for g in range(4)]
                    order = [0, 1, 2, 3] if j % 2 == 0 else [2, 3, 0, 1]
                    first_pos = min(order.index(0), order.index(3))
                    later_g = order[max(order.index(0), order.index(3))]
                    mm_pe = {}
                    ev_start = DV.n

                    def score_mm(g):
                        s = jslots[g]
                        em.w(DV, slot_free_dv[s])
                        if g == later_g:
                            em.w(DV, ev_start + first_pos + 1)
                        par = 0 if g < 2 else 64
                        tp = (0, 0) if g < 2 else (64, 0)
                        base = 1024 * (g % 2)
                        for half in range(2):
                            em.i(lambda e, s=s, half=half, par=par, tp=tp,
                                 base=base, j=j, qh=qh:
                                 e.matmul(
                                     slot_ps[s][:, 512 * half:512 * (half + 1)],
                                     T.K2[par:par + 64, 128 * j:128 * (j + 1)],
                                     qh[par:par + 64, :, :]
                                     .rearrange("p a b -> p (a b)")
                                     [:, base + 512 * half:base + 512 * (half + 1)],
                                     start=True, stop=True, tile_position=tp),
                                 inc=(1 if half == 1 else 0))
                        mm_pe[g] = PE.n

                    with em.on("tensor"):
                        em.w(LD, ld_done)
                        score_mm(order[0])
                        score_mm(order[1])
                    if BL >= 3 and j >= 2:
                        prev_mv_pe = emit_chain(j - 2, r_dv_arr[j - 2])
                        ebuf_free_pe[(j - 2) % 3] = prev_mv_pe
                    with em.on("tensor"):
                        score_mm(order[2])
                        score_mm(order[3])
                    eb = Ebufs[j % 3]
                    zo = 4 * (j % 2)
                    evd = {}
                    with em.on("vector"):
                        for g in order:
                            em.w(PE, mm_pe[g])
                            em.w(AC, esb_free_ac[g])
                            em.fence()
                            em.i(lambda e, g=g, s=jslots[g]: e.tensor_copy(
                                T.esb[:, g, :], slot_ps[s][:, :]))
                            slot_free_dv[jslots[g]] = DV.n
                            evd[g] = DV.n
                    with em.on("scalar"):
                        em.w(PE, ebuf_free_pe[j % 3])
                        em.w(DV, zq4_free_dv[j % 2])
                        for g in order:
                            em.w(DV, evd[g])
                            em.i(lambda e, g=g, eb=eb, zo=zo:
                                 e.activation(eb[:, 1024 * g:1024 * (g + 1)],
                                              T.esb[:, g, :], ACTF.Exp,
                                              accum_out=zq4[:, zo + g:
                                                            zo + g + 1]))
                            esb_free_ac[g] = AC.n
                    with em.on("vector"):
                        em.w(AC, AC.n)
                        em.w(PE, prev_mv_pe)
                        em.fence()
                        em.i(lambda e, zo=zo: e.tensor_add(
                            Zcol[...], zq4[:, zo:zo + 1], zq4[:, zo + 1:zo + 2]))
                        em.fence()
                        em.i(lambda e, zo=zo: e.tensor_add(
                            Zcol[...], Zcol[...], zq4[:, zo + 2:zo + 3]))
                        em.fence()
                        em.i(lambda e, zo=zo: e.tensor_add(
                            Zcol[...], Zcol[...], zq4[:, zo + 3:zo + 4]))
                        em.fence()
                        em.i(lambda e: e.reciprocal(rcol[...], Zcol[...]))
                        em.fence()
                        em.i(lambda e, j=j: e.tensor_copy(
                            rb16[:, j % 2:j % 2 + 1], rcol[...]))
                    prev_r_dv = DV.n
                    r_dv_arr[j] = DV.n
                    zq4_free_dv[j % 2] = DV.n
                    if j == 31:
                        head_score_pe = mm_pe[order[3]]
                if BL >= 3:
                    prev_mv_pe = emit_chain(30, r_dv_arr[30])
                    ebuf_free_pe[30 % 3] = prev_mv_pe
                    prev_mv_pe = emit_chain(31, r_dv_arr[31])
                    ebuf_free_pe[31 % 3] = prev_mv_pe
                if BL >= 4:
                    with em.on("vector"):
                        em.w(PE, PE.n)
                        em.w(ST, prev_osst)
                        for q in range(8):
                            pt = PT3 if q < 4 else PT4
                            em.fence()
                            em.i(lambda e, pt=pt, q=q: e.tensor_copy(
                                T.ossb[32 * (q % 4):32 * (q % 4) + 1,
                                       512 * (q // 4):512 * (q // 4) + 512],
                                pt[32 * (q % 4):32 * (q % 4) + 1, 0:512]))
                    mv_evac_dv = DV.n
                    with em.on("sync"):
                        em.w(DV, mv_evac_dv)
                        for q in range(8):
                            em.d(ST, lambda e, h=h, q=q: e.dma_start(
                                d_oss[h, 0:1, 512 * q:512 * (q + 1)],
                                T.ossb[32 * (q % 4):32 * (q % 4) + 1,
                                       512 * (q // 4):512 * (q // 4) + 512]))
                osst_st[h] = ST.n
                prev_osst = ST.n
            barrier()
            if "C" not in phases:
                return

            for kch in range(8):
                with em.on("gpsimd"):
                    em.w(DV, DV.n)
                    em.d(L2, lambda e, kch=kch: e.dma_start(T.Xt0[...],
                                                            d_wo[:, kch, :]))
                with em.on("vector"):
                    em.w(L2, L2.n)
                    em.fence()
                    em.i(lambda e, kch=kch: e.tensor_copy(T.wob[:, kch, :],
                                                          T.Xt0[...]))
            wo_free_dv = [DV.n, DV.n]
            wt_slot_free = [0, 0]
            prev_vsc = 0
            prev_lnin = [DV.n, 0]
            prev_sq_ac = 0
            prev_m1tr = 0
            prev_ust = 0
            prev_xqst = 0
            prev_wbc = 0
            osld = 0
            for t in range(NT):
                h = t // 2
                hb = t % 2
                if hb == 0:
                    with em.on("sync"):
                        em.w(ST, osst_st[h])
                        em.w(PE, prev_wbc)
                        em.d(LD, lambda e, h=h: e.dma_start(T.osrow[...],
                                                            d_oss[h, :, :]))
                    osld = LD.n
                with em.on("gpsimd"):
                    em.w(DV, max(prev_vsc, prev_lnin[0]))
                    em.d(L2, lambda e, t=t: e.dma_start(
                        T.Vt[...], d_vs[:, :, 128 * t:128 * (t + 1)]))
                    em.d(L2, lambda e, t=t: e.dma_start(T.Xt0[...],
                                                        d_xtm[t, :, :]))
                vx_ld = L2.n
                os3 = T.osrow[0:1, :].rearrange("o (q c) -> o q c", c=256)
                with em.on("tensor"):
                    em.w(LD, osld)
                    em.w(DV, prev_vsc)
                    for b4 in range(2):
                        em.i(lambda e, b4=b4, hb=hb, os3=os3:
                             e.matmul(PT2[0:64, 512 * b4:512 * b4 + 512],
                                      ones[0:1, 0:64],
                                      os3[0:1, 4 * b4:4 * b4 + 4,
                                          128 * hb:128 * hb + 128],
                                      start=True, stop=True,
                                      tile_position=(0, 0)),
                             inc=0)
                    for b4 in range(2):
                        em.i(lambda e, b4=b4, hb=hb, os3=os3:
                             e.matmul(PT2[64:128, 512 * b4:512 * b4 + 512],
                                      ones[0:1, 0:64],
                                      os3[0:1, 8 + 4 * b4:8 + 4 * b4 + 4,
                                          128 * hb:128 * hb + 128],
                                      start=True, stop=True,
                                      tile_position=(0, 64)),
                             inc=(1 if b4 == 1 else 0))
                with em.on("vector"):
                    em.w(PE, PE.n)
                    em.w(L2, vx_ld)
                    em.fence()
                    em.i(lambda e: e.tensor_mul(
                        T.Ot[...], T.Vt[...],
                        PT2[:, 0:1024].rearrange("p (a b) -> p a b", b=128)))
                prev_wbc = PE.n
                prev_vsc = DV.n
                wps = [PT0, PT1][t % 2]
                with em.on("tensor"):
                    em.w(DV, max(prev_vsc, wo_free_dv[t % 2]))
                    for half in range(2):
                        for kch in range(8):
                            em.i(lambda e, wps=wps, half=half, kch=kch:
                                 e.matmul(wps[:, 512 * half:512 * (half + 1)],
                                          T.Ot[:, kch, :],
                                          T.wob[:, kch, 512 * half:512 * (half + 1)],
                                          start=(kch == 0), stop=False),
                                 inc=0)
                        em.i(lambda e, wps=wps, half=half:
                             e.matmul(wps[:, 512 * half:512 * (half + 1)],
                                      ones[0:1, :],
                                      borow[0:1, 512 * half:512 * (half + 1)],
                                      start=False, stop=True),
                             inc=(1 if half == 1 else 0))
                wo_done = PE.n
                with em.on("vector"):
                    em.w(PE, wo_done)
                    em.w(AC, prev_sq_ac)
                    em.fence()
                    em.i(lambda e, wps=wps: e.tensor_add(
                        T.lnin[...], wps[...], T.Xt0[...]))
                wo_free_dv[t % 2] = DV.n
                prev_lnin[0] = DV.n
                lnin_dv = DV.n
                with em.on("scalar"):
                    em.w(DV, lnin_dv)
                    em.fence()
                    em.i(lambda e: e.activation(T.qsc[...], T.lnin[...], ACTF.Square,
                                                accum_out=stat[:, 1:2]))
                prev_sq_ac = AC.n
                with em.on("vector"):
                    em.w(AC, AC.n)
                    em.w(PE, prev_m1tr)
                    em.fence()
                    em.i(lambda e: e.tensor_scalar(T.qsc[...], T.lnin[...], 1.0,
                                                   None, OP.mult, OP.add,
                                                   accum_out=stat[:, 0:1]))
                    em.fence()
                    em.i(lambda e: e.tensor_scalar(stat[:, 2:3], stat[:, 0:1],
                                                   -1.0 / 1024.0, None, OP.mult))
                    em.fence()
                    em.i(lambda e: e.tensor_scalar(stat[:, 3:4], stat[:, 1:2],
                                                   1.0 / 1024.0, EPS, OP.mult,
                                                   OP.add))
                    em.fence()
                    em.i(lambda e: e.tensor_mul(stat[:, 4:5], stat[:, 2:3],
                                                stat[:, 2:3]))
                    em.fence()
                    em.i(lambda e: e.tensor_sub(stat[:, 3:4], stat[:, 3:4],
                                                stat[:, 4:5]))
                with em.on("scalar"):
                    em.w(DV, DV.n)
                    em.fence()
                    em.i(lambda e: e.activation(stat[:, 4:5], stat[:, 3:4], ACTF.Ln))
                    em.fence()
                    em.i(lambda e: e.activation(stat[:, 5:6], stat[:, 4:5],
                                                ACTF.Exp, scale=neghalf[...]))
                rstd_ac = AC.n
                with em.on("vector"):
                    em.w(AC, rstd_ac)
                    em.w(ST, prev_ust)
                    em.fence()
                    em.i(lambda e: e.tensor_scalar(T.utile[...], T.lnin[...],
                                                   stat[:, 2:3], stat[:, 5:6],
                                                   OP.add, OP.mult))
                    em.fence()
                    em.i(lambda e: e.tensor_reduce(stat[:, 6:7], T.utile[...],
                                                   AX.X, OP.max,
                                                   apply_absolute_value=True))
                    em.fence()
                    em.i(lambda e: e.tensor_scalar(stat[:, 6:7], stat[:, 6:7], 1e-5,
                                                   None, OP.max))
                    em.fence()
                    em.i(lambda e: e.reciprocal(stat[:, 7:8], stat[:, 6:7]))
                    em.fence()
                    em.i(lambda e: e.tensor_scalar(stat[:, 7:8], stat[:, 7:8],
                                                   127.0, None, OP.mult))
                    em.fence()
                    em.i(lambda e: e.tensor_scalar(stat[:, 6:7], stat[:, 6:7],
                                                   1.0 / 127.0, None, OP.mult))
                ufin_dv = DV.n
                with em.on("sync"):
                    em.w(DV, ufin_dv)
                    em.d(ST, lambda e, t=t: e.dma_start(d_us[t, :, :], T.utile[...]))
                prev_ust = ST.n
                with em.on("tensor"):
                    em.w(DV, ufin_dv)
                    em.i(lambda e: e.transpose(PT3[0:1, 0:128], stat[:, 6:7],
                                               ident[...]))
                prev_m1tr = PE.n
                with em.on("vector"):
                    em.w(PE, prev_m1tr)
                    em.fence()
                    em.i(lambda e, t=t: e.tensor_copy(
                        m1row[0:1, 128 * t:128 * (t + 1)], PT3[0:1, 0:128]))
                with em.on("vector"):
                    zv = T.qsc[:, 0:512]
                    cv = T.qsc[:, 512:1024]
                    for hf in range(2):
                        sl = slice(512 * hf, 512 * (hf + 1))
                        em.fence()
                        em.i(lambda e, sl=sl, zv=zv: e.tensor_scalar(
                            zv, T.utile[:, sl], stat[:, 7:8], 0.5,
                            OP.mult, OP.add))
                        em.fence()
                        em.i(lambda e, zv=zv: e.tensor_copy(T.zich[...], zv))
                        em.fence()
                        em.i(lambda e, zv=zv, cv=cv: e.tensor_tensor(
                            cv, T.zich[...], zv, OP.is_gt))
                        em.fence()
                        em.i(lambda e, sl=sl, cv=cv: e.tensor_sub(
                            T.xqtm[:, sl], T.zich[...], cv))
                q1_dv = DV.n
                with em.on("tensor"):
                    em.w(DV, q1_dv)
                    for ch in range(8):
                        em.i(lambda e, ch=ch, wps=wps: e.transpose(
                            wps[:, 128 * ch:128 * (ch + 1)],
                            T.xqtm[:, 128 * ch:128 * (ch + 1)], ident[...]))
                xqtr_pe = PE.n
                with em.on("vector"):
                    em.w(PE, xqtr_pe)
                    em.w(ST, prev_xqst)
                    em.fence()
                    em.i(lambda e, wps=wps: e.tensor_copy(
                        T.xqTt[...],
                        wps[...].rearrange("p (a b) -> p a b", b=128)))
                wo_free_dv[t % 2] = DV.n
                with em.on("sync"):
                    em.w(DV, DV.n)
                    em.d(ST, lambda e, t=t: e.dma_start(d_xqs[t, :, :, :],
                                                        T.xqTt[...]))
                prev_xqst = ST.n
            barrier()
            if "D" not in phases:
                return

            with em.on("scalar"):
                em.i(lambda e: _load_act_set(e, 18), inc=0)
            xqts = [T.xqt0, T.xqt1]
            with em.on("vector"):
                em.fence()
                em.i(lambda e: e.tensor_copy(mhrow[0:1, 0:1],
                                             m1row[0:1, 2303:2304]))
                em.fence()
                em.i(lambda e: e.tensor_scalar(mhrow[0:1, 0:1], mhrow[0:1, 0:1],
                                               sw1[...], None, OP.mult))
            m12h_dv = DV.n
            with em.on("sync"):
                em.d(LD, lambda e: e.dma_start(T.xqt0[:, :, 0:128],
                                               d_xqs[17, :, :, :]))
            halo_ld = LD.n
            with em.on("vector"):
                em.w(LD, halo_ld)
                em.fence()
                em.i(lambda e: e.tensor_copy(T.xqth[...], T.xqt0[:, :, 127:128]))
            halo_dv = DV.n
            with em.on("tensor"):
                em.w(DV, halo_dv)
                for och in range(32):
                    for kch in range(8):
                        em.i(lambda e, och=och, kch=kch:
                             e.matmul(PT4[:, och:och + 1],
                                      w1qT[:, kch, 128 * och:128 * (och + 1)],
                                      T.xqth[:, kch, 0:1],
                                      start=(kch == 0), stop=(kch == 7)),
                             inc=(1 if (och == 31 and kch == 7) else 0))
            with em.on("tensor"):
                em.w(DV, m12h_dv)
                em.i(lambda e: e.matmul(PT3[:, 0:1], ones[0:1, :], mhrow[0:1, 0:1],
                                        start=True, stop=True))
            mh_pe = PE.n
            with em.on("vector"):
                em.w(PE, mh_pe)
                em.fence()
                em.i(lambda e: e.tensor_copy(T.mhsb[...], PT3[:, 0:1]))
                em.fence()
                em.i(lambda e: e.tensor_mul(
                    T.zedge[...],
                    PT4[:, 0:32].rearrange("p (a b) -> p a b", b=1),
                    T.mhsb[...].rearrange("p (o b) -> p o b", o=1)
                    .broadcast_to([128, 32, 1])))
                em.fence()
                em.i(lambda e: e.tensor_mul(
                    T.zedge[...], T.zedge[...],
                    masklr[:, 0:1].rearrange("p (o b) -> p o b", o=1)
                    .broadcast_to([128, 32, 1])))
            zedge_dv = DV.n

            f1_slot_free = zedge_dv
            f2_free_dv = DV.n
            htile_free_dv = 0
            ldx_pe_wait = 0
            yst_st = {}
            uld_free_dv = 0
            q2_pe = [0, 0]
            prev_small = DV.n
            fused_sq_ac = 0
            ogroups = [(0, 3), (3, 6), (6, 9), (9, 12), (12, 15), (15, 18),
                       (18, 21), (21, 24), (24, 27), (27, 30), (30, 32)]
            with em.on("sync"):
                em.w(DV, halo_dv)
                em.d(LD, lambda e: e.dma_start(xqts[0][:, :, 0:128],
                                               d_xqs[0, :, :, :]))
            xq_ld_t = [LD.n] + [0] * NT
            for t in range(NOWN):
                with em.on("sync"):
                    if t >= 1:
                        em.w(PE, ldx_pe_wait)
                    em.w(DV, uld_free_dv)
                    em.d(LD, lambda e, t=t: e.dma_start(
                        xqts[(t + 1) % 2][:, :, 0:128], d_xqs[t + 1, :, :, :]))
                    em.d(LD, lambda e, t=t: e.dma_start(T.uld[...], d_us[t, :, :]))
                xq_ld_t[t + 1] = LD.n
                xq_ld = LD.n
                with em.on("vector"):
                    em.w(LD, xq_ld)
                    em.fence()
                    em.i(lambda e, t=t: e.tensor_copy(
                        xqts[t % 2][:, :, 128:129], xqts[(t + 1) % 2][:, :, 0:1]))
                xcol_dv = DV.n
                with em.on("tensor"):
                    em.w(DV, max(m12h_dv, prev_small))
                    em.i(lambda e, t=t: e.matmul(PT2[:, 0:129], sw1row[...],
                                                 m1row[0:1, 128 * t:128 * t + 129],
                                                 start=True, stop=True))
                with em.on("vector"):
                    em.w(PE, PE.n)
                    em.fence()
                    em.i(lambda e: e.tensor_copy(T.m12sb[...], PT2[:, 0:129]))
                m12t_dv = DV.n
                with em.on("vector"):
                    em.w(AC, fused_sq_ac)
                    em.fence()
                    em.i(lambda e: e.tensor_copy(T.zti[:, :, 0:1], T.zedge[...]))
                def sweeps(p0, p1, pac):
                    with em.on("vector"):
                        em.w(AC, pac)
                        em.fence()
                        for och in range(p0, p1):
                            em.i(lambda e, och=och: e.scalar_tensor_tensor(
                                T.htile[:, och, :], T.zti[:, och, 0:128],
                                cwa[:, och:och + 1], T.htile[:, och, :],
                                OP.mult, OP.add))
                        em.fence()
                        for och in range(p0, p1):
                            em.i(lambda e, och=och: e.scalar_tensor_tensor(
                                T.htile[:, och, :], T.zti[:, och, 2:130],
                                cwc[:, och:och + 1], T.htile[:, och, :],
                                OP.mult, OP.add))

                prev_grp = None
                for (o0, o1) in ogroups:
                    ng = o1 - o0
                    with em.on("tensor"):
                        em.w(LD, xq_ld)
                        em.w(DV, max(f1_slot_free, xcol_dv))
                        for oo in range(ng):
                            och = o0 + oo
                            for kch in range(8):
                                em.i(lambda e, oo=oo, och=och, kch=kch, t=t:
                                     e.matmul(PT3[:, 129 * oo:129 * oo + 129],
                                              w1qT[:, kch, 128 * och:128 * (och + 1)],
                                              xqts[t % 2][:, kch, 0:129],
                                              start=(kch == 0), stop=(kch == 7)),
                                     inc=(1 if (oo == ng - 1 and kch == 7) else 0))
                    with em.on("vector"):
                        em.w(PE, PE.n)
                        em.w(DV, m12t_dv)
                        em.fence()
                        em.i(lambda e, o0=o0, ng=ng: e.tensor_mul(
                            T.zti[:, o0:o0 + ng, 1:130],
                            PT3[:, 0:129 * ng].rearrange("p (a b) -> p a b", b=129),
                            T.m12sb[...].rearrange("p (o b) -> p o b", o=1)
                            .broadcast_to([128, ng, 129])))
                        if t == NOWN - 1:
                            em.fence()
                            em.i(lambda e, o0=o0, ng=ng: e.tensor_mul(
                                T.zti[:, o0:o0 + ng, 129:130],
                                T.zti[:, o0:o0 + ng, 129:130],
                                masklr[:, 1:2].rearrange("p (o b) -> p o b", o=1)
                                .broadcast_to([128, ng, 1])))
                        em.fence()
                        em.i(lambda e, o0=o0, ng=ng: e.tensor_copy(
                            T.zedge[:, o0:o0 + ng, :],
                            T.zti[:, o0:o0 + ng, 128:129]))
                    f1_slot_free = DV.n
                    with em.on("scalar"):
                        em.w(DV, f1_slot_free)
                        if o0 == 0:
                            em.w(DV, htile_free_dv)
                        for och in range(o0, o1):
                            em.i(lambda e, och=och: e.activation(
                                T.htile[:, och, :], T.zti[:, och, 1:129],
                                ACTF.Identity, scale=cwb[:, och:och + 1],
                                bias=cb32[:, och:och + 1]))
                    if prev_grp is not None:
                        sweeps(*prev_grp)
                    prev_grp = (o0, o1, AC.n)
                ldx_pe_wait = PE.n
                sweeps(*prev_grp)
                conv_dv = DV.n
                with em.on("scalar"):
                    em.w(DV, conv_dv)
                    em.fence()
                    em.i(lambda e: e.activation(T.htile[...], T.htile[...],
                                                ACTF.Silu))
                silu_ac = AC.n
                with em.on("vector"):
                    em.w(AC, silu_ac)
                    em.fence()
                    em.i(lambda e: e.tensor_reduce(
                        T.Mp[...], T.htile[...].rearrange("p a b -> p b a"),
                        AX.X, OP.max, apply_absolute_value=True))
                mnmx_dv = DV.n
                with em.on("tensor"):
                    em.w(DV, mnmx_dv)
                    em.i(lambda e: e.transpose(PT4[0:128, 0:128], T.Mp[...],
                                               ident[...]))
                mtr_pe = PE.n
                with em.on("vector"):
                    em.w(PE, mtr_pe)
                    em.fence()
                    em.i(lambda e: e.tensor_reduce(q2st[:, 0:1], PT4[:, 0:128],
                                                   AX.X, OP.max))
                    em.fence()
                    em.i(lambda e: e.tensor_scalar(q2st[:, 0:1], q2st[:, 0:1],
                                                   1e-5, None, OP.max))
                    em.fence()
                    em.i(lambda e: e.reciprocal(q2st[:, 1:2], q2st[:, 0:1]))
                    em.fence()
                    em.i(lambda e: e.tensor_scalar(q2st[:, 1:2], q2st[:, 1:2],
                                                   127.0, None, OP.mult))
                    em.fence()
                    em.i(lambda e: e.tensor_scalar(q2st[:, 2:3], q2st[:, 0:1],
                                                   sw2col[...], None, OP.mult))
                    em.fence()
                    em.i(lambda e: e.tensor_scalar(q2st[:, 2:3], q2st[:, 2:3],
                                                   1.0 / 127.0, None, OP.mult))
                q2s_dv = DV.n
                with em.on("tensor"):
                    em.w(DV, q2s_dv)
                    em.i(lambda e: e.transpose(PT4[0:1, 256:384], q2st[:, 1:2],
                                               ident[...]))
                with em.on("vector"):
                    em.w(PE, PE.n)
                    em.fence()
                    em.i(lambda e: e.tensor_copy(s2row[0:1, 0:128],
                                                 PT4[0:1, 256:384]))
                with em.on("tensor"):
                    em.w(DV, DV.n)
                    em.i(lambda e: e.matmul(PT4[:, 384:512], ones[0:1, :],
                                            s2row[0:1, 0:128],
                                            start=True, stop=True))
                with em.on("vector"):
                    em.w(PE, PE.n)
                    em.fence()
                    em.i(lambda e: e.tensor_copy(T.s2sb[...], PT4[:, 384:512]))
                prev_small = DV.n
                fps = [PT0, PT1][t % 2]
                for hf in range(2):
                    x2b = [T.x2q, T.x2qb][hf]
                    xs = T.zti[:, 0:16, 1:129]
                    cs = T.zti[:, 16:32, 1:129]
                    with em.on("vector"):
                        em.w(PE, q2_pe[hf])
                        em.fence()
                        em.i(lambda e, hf=hf, xs=xs: e.tensor_mul(
                            xs, T.htile[:, 16 * hf:16 * (hf + 1), :],
                            T.s2sb[...].rearrange("p (o b) -> p o b", o=1)
                            .broadcast_to([128, 16, 128])))
                        em.fence()
                        em.i(lambda e, xs=xs: e.tensor_scalar(
                            xs, xs, 1.0, 0.5, OP.mult, OP.add))
                        em.fence()
                        em.i(lambda e, xs=xs: e.tensor_copy(T.zi16[...], xs))
                        em.fence()
                        em.i(lambda e, xs=xs, cs=cs: e.tensor_tensor(
                            cs, T.zi16[...], xs, OP.is_gt))
                        em.fence()
                        em.i(lambda e, x2b=x2b, cs=cs: e.tensor_sub(
                            x2b[...], T.zi16[...], cs))
                    x2q_dv = DV.n
                    with em.on("tensor"):
                        em.w(DV, max(x2q_dv, f2_free_dv))
                        for half in range(2):
                            for kk in range(16):
                                kch = 16 * hf + kk
                                em.i(lambda e, fps=fps, half=half, kk=kk, kch=kch,
                                     x2b=x2b:
                                     e.matmul(fps[:, 512 * half:512 * (half + 1)],
                                              x2b[:, kk, :],
                                              w2qT[:, kch, 512 * half:512 * (half + 1)],
                                              start=(kch == 0),
                                              stop=(kch == 31)),
                                     inc=(1 if (half == 1 and kk == 15) else 0))
                    q2_pe[hf] = PE.n
                f2_pe = PE.n
                htile_free_dv = DV.n
                with em.on("vector"):
                    em.w(PE, f2_pe)
                    em.w(LD, xq_ld)
                    em.w(ST, yst_st.get(t - 1, 0))
                    em.fence()
                    em.i(lambda e, fps=fps: e.tensor_scalar(
                        T.ypre[...], fps[...], q2st[:, 2:3], None, OP.mult))
                    em.fence()
                    em.i(lambda e: e.tensor_add(T.ypre[...], T.ypre[...],
                                                T.uld[...]))
                f2_free_dv = DV.n
                uld_free_dv = DV.n
                ypre_dv = DV.n
                sink = T.zti[:, 8:16, 1:129]
                with em.on("scalar"):
                    em.w(DV, ypre_dv)
                    em.i(lambda e: _load_act_set(e, 6), inc=0)
                    em.fence()
                    em.i(lambda e, sink=sink: e.activation(
                        sink, T.ypre[...], ACTF.Square,
                        accum_out=T.est[:, 1:2]))
                fused_sq_ac = AC.n
                with em.on("vector"):
                    em.w(AC, fused_sq_ac)
                    em.fence()
                    em.i(lambda e, sink=sink: e.tensor_scalar(
                        sink, T.ypre[...], 1.0, None, OP.mult, OP.add,
                        accum_out=T.est[:, 0:1]))
                    em.fence()
                    em.i(lambda e: e.tensor_scalar(T.est[:, 2:3], T.est[:, 0:1],
                                                   -1.0 / 1024.0, None, OP.mult))
                    em.fence()
                    em.i(lambda e: e.tensor_scalar(T.est[:, 3:4], T.est[:, 1:2],
                                                   1.0 / 1024.0, EPS, OP.mult,
                                                   OP.add))
                    em.fence()
                    em.i(lambda e: e.tensor_mul(T.est[:, 4:5], T.est[:, 2:3],
                                                T.est[:, 2:3]))
                    em.fence()
                    em.i(lambda e: e.tensor_sub(T.est[:, 3:4], T.est[:, 3:4],
                                                T.est[:, 4:5]))
                est_dv = DV.n
                with em.on("scalar"):
                    em.w(DV, est_dv)
                    em.fence()
                    em.i(lambda e: e.activation(T.est[:, 4:5], T.est[:, 3:4],
                                                ACTF.Ln))
                    em.fence()
                    em.i(lambda e: e.activation(T.est[:, 5:6], T.est[:, 4:5],
                                                ACTF.Exp, scale=neghalf[...]))
                    em.i(lambda e: _load_act_set(e, 18), inc=0)
                rs_ac = AC.n
                with em.on("vector"):
                    em.w(AC, rs_ac)
                    em.fence()
                    em.i(lambda e: e.tensor_scalar(T.ypre[...], T.ypre[...],
                                                   T.est[:, 2:3], T.est[:, 5:6],
                                                   OP.add, OP.mult))
                with em.on("sync"):
                    em.w(DV, DV.n)
                    em.d(ST, lambda e, t=t: e.dma_start(
                        d_out[128 * t:128 * (t + 1), :], T.ypre[...]))
                yst_st[t] = ST.n
            with em.on("sync"):
                em.w(ST, ST.n)
                em.w(L2, L2.n)

        with nc.Block() as blk:
            @blk.sync
            def _(e):
                program(Em("sync", e))

            @blk.tensor
            def _(e):
                program(Em("tensor", e))

            @blk.vector
            def _(e):
                program(Em("vector", e))

            @blk.scalar
            def _(e):
                _load_act_set(e, 6)
                program(Em("scalar", e))

            @blk.gpsimd
            def _(e):
                program(Em("gpsimd", e))
    return nc


_NC_CACHE = {}


def _get_program():
    if "nc" not in _NC_CACHE:
        _NC_CACHE["nc"] = build_program()
    return _NC_CACHE["nc"]


def _stage_core(x, c, consts):
    n, half = c // 2, c % 2
    own = x[n, 2048 * half:2048 * half + 2048]
    if half == 0:
        foreign = x[n, 2048:2048 + 256]
        maskL, maskR = 0.0, 1.0
    else:
        foreign = x[n, 2048 - 256:2048]
        maskL, maskR = 1.0, 0.0
    win = np.concatenate([own, foreign], axis=0)
    m = dict(consts)
    m["xw_tm"] = np.ascontiguousarray(win.reshape(NT, 128, 1024))
    m["xw_T"] = np.ascontiguousarray(win.T.reshape(8, 128, TW).transpose(1, 0, 2))
    m["maskLR"] = np.broadcast_to(np.array([maskL, maskR], np.float32),
                                  (128, 2)).copy()
    return m


def kernel(L0_attn_X_embed, L0_attn_Wq, L0_attn_bq, L0_attn_Wk, L0_attn_bk,
           L0_attn_Wv, L0_attn_bv, L0_attn_Wo, L0_attn_bo, L0_ffn_w1,
           L0_ffn_conv_w, L0_ffn_conv_b, L0_ffn_w2):
    x = np.ascontiguousarray(np.asarray(L0_attn_X_embed, np.float32))
    f32 = lambda a: np.ascontiguousarray(np.asarray(a, np.float32))

    def chunk_major(w):
        w = f32(w)
        return np.ascontiguousarray(w.reshape(8, 128, w.shape[1]).transpose(1, 0, 2))

    consts = {
        "wq": chunk_major(L0_attn_Wq), "wk": chunk_major(L0_attn_Wk),
        "wv": chunk_major(L0_attn_Wv), "wo": chunk_major(L0_attn_Wo),
        "w1T": chunk_major(f32(L0_ffn_w1).T),
        "w2T": np.ascontiguousarray(
            f32(L0_ffn_w2).T.reshape(32, 128, 1024).transpose(1, 0, 2)),
        "bq_col": np.ascontiguousarray(f32(L0_attn_bq).reshape(8, 128).T),
        "bk_col": np.ascontiguousarray(f32(L0_attn_bk).reshape(8, 128).T),
        "bv_col": np.ascontiguousarray(f32(L0_attn_bv).reshape(8, 128).T),
        "bo_row": f32(L0_attn_bo).reshape(1, 1024),
        "cw3a": np.ascontiguousarray(f32(L0_ffn_conv_w)[:, 0, 0].reshape(32, 128).T),
        "cw3b": np.ascontiguousarray(f32(L0_ffn_conv_w)[:, 0, 1].reshape(32, 128).T),
        "cw3c": np.ascontiguousarray(f32(L0_ffn_conv_w)[:, 0, 2].reshape(32, 128).T),
        "cb32": np.ascontiguousarray(f32(L0_ffn_conv_b).reshape(32, 128).T),
        "ones128": np.ones((128, 128), np.float32),
        "ones_row": np.ones((1, 384), np.float32),
        "ident": np.eye(128, dtype=np.float32),
        "neghalf": np.full((128, 1), -0.5, np.float32),
    }
    in_maps = [_stage_core(x, c, consts) for c in range(8)]
    nc = _get_program()
    res = run_bass_kernel_spmd(nc, in_maps, core_ids=list(range(8)))
    out = np.empty((4, 4096, 1024), np.float32)
    for c in range(8):
        n, half = c // 2, c % 2
        out[n, 2048 * half:2048 * half + 2048] = res.results[c]["out"]
    return out


# version 21
# speedup vs baseline: 1.1179x; 1.1179x over previous
import contextlib
import numpy as np
import ml_dtypes

import concourse.bass as bass
import concourse.mybir as mybir
from concourse.bass_utils import run_bass_kernel_spmd

F32 = mybir.dt.float32
BF16 = mybir.dt.bfloat16
I32 = mybir.dt.int32
AX = mybir.AxisListType
OP = mybir.AluOpType
ACTF = mybir.ActivationFunctionType

NT = 18
NOWN = 16
TW = NT * 128
NHEAD = 9
INV32 = 1.0 / 32.0
EPS = 1e-5


class Ctr:
    __slots__ = ("h", "n")
    def __init__(self, h):
        self.h = h
        self.n = 0


def _load_act_set(e, idx):
    return e.add_instruction(mybir.InstLoadActFuncSet(
        name=e.bass.get_next_instruction_name(),
        act_func_set_id=idx, ins=[], outs=[]))


class Em:
    def __init__(self, live, e):
        self.live = live
        self.e = e
        self.ctrs = None
        self.cur = None
        self.mute = False

    @contextlib.contextmanager
    def on(self, eng):
        prev = self.cur
        self.cur = eng
        yield
        self.cur = prev

    def w(self, ctr, val):
        if self.mute:
            return
        if self.cur == self.live and val > 0:
            self.e.wait_ge(ctr.h, val)

    def i(self, fn, inc=1):
        if self.mute:
            return
        c = self.ctrs[self.cur]
        if self.cur == self.live:
            inst = fn(self.e)
            if inc:
                inst.then_inc(c.h, inc)
        c.n += inc

    def d(self, ctr, fn):
        if self.mute:
            return
        if self.cur == self.live and ctr.n > 0:
            self.e.wait_ge(ctr.h, ctr.n)
        if self.cur == self.live:
            fn(self.e).then_inc(ctr.h, 16)
        ctr.n += 16

    def fence(self):
        c = self.ctrs[self.cur]
        self.w(c, c.n)


def build_program(phases="ABCDE", nheads=NHEAD, noload=False):
    nc = bass.Bass("TRN2", target_bir_lowering=False)

    d_xtm = nc.dram_tensor("xw_tm", [NT, 128, 1024], F32, kind="ExternalInput").ap()
    d_xT = nc.dram_tensor("xw_T", [128, 8, TW], F32, kind="ExternalInput").ap()
    d_wq = nc.dram_tensor("wq", [128, 8, 1024], F32, kind="ExternalInput").ap()
    d_wk = nc.dram_tensor("wk", [128, 8, 1024], F32, kind="ExternalInput").ap()
    d_wv = nc.dram_tensor("wv", [128, 8, 1024], F32, kind="ExternalInput").ap()
    d_wo = nc.dram_tensor("wo", [128, 8, 1024], F32, kind="ExternalInput").ap()
    d_w1T = nc.dram_tensor("w1T", [128, 8, 4096], F32, kind="ExternalInput").ap()
    d_w2T = nc.dram_tensor("w2T", [128, 32, 1024], F32, kind="ExternalInput").ap()
    d_bq = nc.dram_tensor("bq_col", [128, 8], F32, kind="ExternalInput").ap()
    d_bk = nc.dram_tensor("bk_col", [128, 8], F32, kind="ExternalInput").ap()
    d_bv = nc.dram_tensor("bv_col", [128, 8], F32, kind="ExternalInput").ap()
    d_borow = nc.dram_tensor("bo_row", [1, 1024], F32, kind="ExternalInput").ap()
    d_cwa = nc.dram_tensor("cw3a", [128, 32], F32, kind="ExternalInput").ap()
    d_cwb = nc.dram_tensor("cw3b", [128, 32], F32, kind="ExternalInput").ap()
    d_cwc = nc.dram_tensor("cw3c", [128, 32], F32, kind="ExternalInput").ap()
    d_cb = nc.dram_tensor("cb32", [128, 32], F32, kind="ExternalInput").ap()
    d_ones = nc.dram_tensor("ones128", [128, 128], F32, kind="ExternalInput").ap()
    d_onesw = nc.dram_tensor("ones_row", [1, 384], F32, kind="ExternalInput").ap()
    d_ident = nc.dram_tensor("ident", [128, 128], F32, kind="ExternalInput").ap()
    d_nh = nc.dram_tensor("neghalf", [128, 1], F32, kind="ExternalInput").ap()
    d_mask = nc.dram_tensor("maskLR", [128, 2], F32, kind="ExternalInput").ap()
    d_out = nc.dram_tensor("out", [NOWN * 128, 1024], F32, kind="ExternalOutput").ap()

    d_qs = nc.dram_tensor("q_scr", [128, 8, TW], BF16, kind="Internal").ap()
    d_ks = nc.dram_tensor("k_scr", [128, 8, TW], BF16, kind="Internal").ap()
    d_vs = nc.dram_tensor("v_scr", [128, 8, TW], BF16, kind="Internal").ap()
    d_us = nc.dram_tensor("u_scr", [NT, 128, 1024], F32, kind="Internal").ap()
    d_xqs = nc.dram_tensor("xqT_scr", [NT, 128, 8, 128], BF16, kind="Internal").ap()
    d_oss = nc.dram_tensor("os_scr", [NHEAD, 1, 4096], F32, kind="Internal").ap()
    d_ys = nc.dram_tensor("y_scr", [NOWN, 128, 1024], F32, kind="Internal").ap()

    with contextlib.ExitStack() as stk:
        SBT = lambda st, name, shape, dt=F32: st.enter_context(
            nc.sbuf_tensor(name, shape, dt))
        sb = lambda name, shape, dt=F32: SBT(stk, name, shape, dt)
        psum = lambda name, shape: stk.enter_context(nc.psum_tensor(name, shape, F32))

        ones = sb("ones", [128, 128])
        onesw = sb("onesw", [1, 384])
        ident = sb("ident_sb", [128, 128])
        cwa = sb("cwa", [128, 32]); cwb = sb("cwb", [128, 32])
        cwc = sb("cwc", [128, 32]); cb32 = sb("cb32sb", [128, 32])
        neghalf = sb("neghalfsb", [128, 1])
        masklr = sb("masklrsb", [128, 2])
        sw1 = sb("sw1", [1, 1]); sw2 = sb("sw2", [1, 1])
        sw2col = sb("sw2col", [128, 1])
        thr1 = sb("thr1", [128, 1]); nthr1 = sb("nthr1", [128, 1])
        thr2 = sb("thr2", [128, 1]); nthr2 = sb("nthr2", [128, 1])
        wsum = sb("wsum", [128, 1])
        stat = sb("stat", [128, 8])
        q2st = sb("q2st", [128, 4])
        zq4 = sb("zq4", [128, 8])
        Zcol = sb("Zcol", [128, 1])
        rcol = sb("rcol", [128, 1])
        rb16 = sb("rb16", [128, 2], BF16)
        m1row = sb("m1row", [1, TW])
        borow = sb("borow", [1, 1024])
        s2row = sb("s2row", [1, 129])
        scal = sb("scal", [1, 8])
        mhrow = sb("mhrow", [1, 2])
        sw1row = sb("sw1row", [1, 128])
        w1qT = sb("w1qT", [128, 8, 4096], BF16)
        w2qT = sb("w2qT", [128, 32, 1024], BF16)

        PT0 = psum("PT0", [128, 1024])
        PT1 = psum("PT1", [128, 1024])
        PT2 = psum("PT2", [128, 1024])
        PT3 = psum("PT3", [128, 512])
        PT4 = psum("PT4", [128, 512])

        h_ld = stk.enter_context(nc.semaphore("s_ld"))
        h_st = stk.enter_context(nc.semaphore("s_st"))
        h_pe = stk.enter_context(nc.semaphore("s_pe"))
        h_ac = stk.enter_context(nc.semaphore("s_ac"))
        h_dv = stk.enter_context(nc.semaphore("s_dv"))
        h_l2 = stk.enter_context(nc.semaphore("s_l2"))

        ten = {}
        with contextlib.ExitStack() as sA:
            with contextlib.ExitStack() as sWQ:
                ten["big0"] = SBT(sWQ, "big0", [128, 4096])
                ten["c1b"] = SBT(sWQ, "c1b", [128, 1024])
                ten["c2b"] = SBT(sWQ, "c2b", [128, 1024])
            ten["wb"] = SBT(sA, "wb", [128, 8, 1024], BF16)
            ten["xtf"] = SBT(sA, "xtf", [128, 8, 288])
            ten["xall"] = SBT(sA, "xall", [128, 8, TW], BF16)
            for i in range(2):
                ten[f"stg{i}"] = SBT(sA, f"stg{i}", [128, 288], BF16)
            ten["bqr"] = SBT(sA, "bqr", [128, 8])
            ten["bkr"] = SBT(sA, "bkr", [128, 8])
            ten["bvr"] = SBT(sA, "bvr", [128, 8])
        with contextlib.ExitStack() as sBC:
            ten["wob"] = SBT(sBC, "wob", [128, 8, 1024], BF16)
            with contextlib.ExitStack() as sB:
                with contextlib.ExitStack() as sWo:
                    ten["wotmp"] = SBT(sWo, "wotmp", [128, 1024])
                ten["Ebuf0"] = SBT(sB, "Ebuf0", [128, 4096], BF16)
                ten["Ebuf1"] = SBT(sB, "Ebuf1", [128, 4096], BF16)
                ten["Ebuf2"] = SBT(sB, "Ebuf2", [128, 4096], BF16)
                ten["K2"] = SBT(sB, "K2", [128, 4096], BF16)
                ten["Qh0"] = SBT(sB, "Qh0", [128, 8, 256], BF16)
                ten["Qh1"] = SBT(sB, "Qh1", [128, 8, 256], BF16)
                ten["ossb"] = SBT(sB, "ossb", [128, 1024])
            with contextlib.ExitStack() as sC:
                ten["osrow"] = SBT(sC, "osrow", [1, 4096])
                ten["Xt0"] = SBT(sC, "Xt0", [128, 1024])
                ten["lnin"] = SBT(sC, "lnin", [128, 1024])
                ten["utile"] = SBT(sC, "utile", [128, 1024])
                ten["qsc"] = SBT(sC, "qsc", [128, 1024])
                ten["zich"] = SBT(sC, "zich", [128, 512], I32)
                ten["Vt"] = SBT(sC, "Vt", [128, 8, 128], BF16)
                ten["Vt1"] = SBT(sC, "Vt1", [128, 8, 128], BF16)
                ten["Ot"] = SBT(sC, "Ot", [128, 8, 128], BF16)
                ten["xqtm"] = SBT(sC, "xqtm", [128, 1024])
                ten["xqTt"] = SBT(sC, "xqTt", [128, 8, 128], BF16)
        with contextlib.ExitStack() as sD:
            ten["zti"] = SBT(sD, "zti", [128, 32, 130])
            ten["htile"] = SBT(sD, "htile", [128, 32, 128])
            ten["x2q"] = SBT(sD, "x2q", [128, 16, 128], BF16)
            ten["x2qb"] = SBT(sD, "x2qb", [128, 16, 128], BF16)
            ten["zi16"] = SBT(sD, "zi16", [128, 16, 128], I32)
            ten["uld"] = SBT(sD, "uld", [128, 1024])
            ten["ypre"] = SBT(sD, "ypre", [128, 1024])
            ten["xqt0"] = SBT(sD, "xqt0", [128, 8, 132], BF16)
            ten["xqt1"] = SBT(sD, "xqt1", [128, 8, 132], BF16)
            ten["xqth"] = SBT(sD, "xqth", [128, 8, 1], BF16)
            ten["m12sb"] = SBT(sD, "m12sb", [128, 129])
            ten["s2sb"] = SBT(sD, "s2sb", [128, 128])
            ten["Mp"] = SBT(sD, "Mp", [128, 128])
            ten["mhsb"] = SBT(sD, "mhsb", [128, 1])
            ten["zedge"] = SBT(sD, "zedge", [128, 32, 1])
            ten["est"] = SBT(sD, "est", [128, 8])

        class T:
            pass
        for k, v in ten.items():
            setattr(T, k, v)

        def program(em):
            LD = Ctr(h_ld); ST = Ctr(h_st)
            PE = Ctr(h_pe); AC = Ctr(h_ac); DV = Ctr(h_dv)
            L2 = Ctr(h_l2)
            em.ctrs = {"tensor": PE, "vector": DV, "scalar": AC, "sync": None,
                       "gpsimd": None}

            def barrier():
                for eng in ("sync", "tensor", "vector", "scalar", "gpsimd"):
                    with em.on(eng):
                        for c in (LD, ST, PE, AC, DV, L2):
                            em.w(c, c.n)

            em.mute = "A" not in phases
            with em.on("sync"):
                for dst, src in ((ones, d_ones), (onesw, d_onesw),
                                 (ident, d_ident),
                                 (borow, d_borow), (cwa, d_cwa), (cwb, d_cwb),
                                 (cwc, d_cwc), (cb32, d_cb), (neghalf, d_nh),
                                 (masklr, d_mask)):
                    em.d(LD, lambda e, dst=dst, src=src: e.dma_start(dst[...], src))
            consts_ld = LD.n

            def wq_quant(dram, si, sw, thr, nthr, dstq, swcol=None):
                big0 = T.big0; c1b = T.c1b; c2b = T.c2b
                halves = [(big0[:, 0:2048], LD, "sync"),
                          (big0[:, 2048:4096], L2, "gpsimd")]

                def load_half(j):
                    buf, _, _ = halves[j % 2]
                    if dram is d_w1T:
                        return lambda e, j=j, buf=buf: e.dma_start(
                            buf, dram[:, j // 2,
                                      2048 * (j % 2):2048 * (j % 2) + 2048])
                    return lambda e, j=j, buf=buf: e.dma_start(
                        buf.rearrange("p (a b) -> p a b", a=2),
                        dram[:, 2 * j:2 * (j + 1), :])

                with em.on("vector"):
                    em.fence()
                    em.i(lambda e: e.memset(wsum[...], 0.0))
                buf_dv = [DV.n, DV.n]
                for j in range(16):
                    buf, ctr, eng = halves[j % 2]
                    with em.on(eng):
                        em.w(DV, buf_dv[j % 2])
                        em.d(ctr, load_half(j))
                    with em.on("vector"):
                        em.w(ctr, ctr.n)
                        em.fence()
                        em.i(lambda e, buf=buf: e.tensor_reduce(
                            c1b[:, 0:1], buf, AX.X, OP.add,
                            apply_absolute_value=True))
                        em.fence()
                        em.i(lambda e: e.tensor_add(wsum[...], wsum[...],
                                                    c1b[:, 0:1]))
                    buf_dv[j % 2] = DV.n
                with em.on("tensor"):
                    em.w(DV, DV.n)
                    em.i(lambda e: e.matmul(PT3[0:1, si:si + 1], ones[:, 0:1],
                                            wsum[...], start=True, stop=True))
                with em.on("vector"):
                    em.w(PE, PE.n)
                    em.fence()
                    em.i(lambda e: e.tensor_scalar(scal[0:1, si:si + 1],
                                                   PT3[0:1, si:si + 1],
                                                   1.0 / (4096.0 * 1024.0), 1e-5,
                                                   OP.mult, OP.max))
                    em.fence()
                    em.i(lambda e: e.tensor_copy(sw[...], scal[0:1, si:si + 1]))
                    em.fence()
                    em.i(lambda e: e.tensor_scalar(scal[0:1, si + 1:si + 2],
                                                   scal[0:1, si:si + 1], 0.5, None,
                                                   OP.mult))
                with em.on("tensor"):
                    em.w(DV, DV.n)
                    em.i(lambda e: e.matmul(PT3[:, 8 + si:9 + si], ones[0:1, :],
                                            scal[0:1, si + 1:si + 2],
                                            start=True, stop=True))
                with em.on("vector"):
                    em.w(PE, PE.n)
                    em.fence()
                    em.i(lambda e: e.tensor_copy(thr[...], PT3[:, 8 + si:9 + si]))
                    em.fence()
                    em.i(lambda e: e.tensor_scalar(nthr[...], thr[...], -1.0, None,
                                                   OP.mult))
                    if swcol is not None:
                        em.fence()
                        em.i(lambda e: e.tensor_scalar(swcol[...], thr[...], 2.0,
                                                       None, OP.mult))
                for j in range(16):
                    buf, ctr, eng = halves[j % 2]
                    with em.on(eng):
                        em.w(DV, buf_dv[j % 2])
                        em.d(ctr, load_half(j))
                    with em.on("vector"):
                        em.w(ctr, ctr.n)
                        for qq in range(2):
                            sl = slice(1024 * qq, 1024 * (qq + 1))
                            em.fence()
                            em.i(lambda e, sl=sl, buf=buf: e.tensor_scalar(
                                c1b[...], buf[:, sl], thr[...], None, OP.is_gt))
                            em.fence()
                            em.i(lambda e, sl=sl, buf=buf: e.tensor_scalar(
                                c2b[...], buf[:, sl], nthr[...], None, OP.is_lt))
                            em.fence()
                            if dstq is w1qT:
                                off = 2048 * (j % 2) + 1024 * qq
                                em.i(lambda e, j=j, off=off: e.tensor_sub(
                                    w1qT[:, j // 2, off:off + 1024],
                                    c1b[...], c2b[...]))
                            else:
                                g = 2 * j + qq
                                em.i(lambda e, g=g: e.tensor_sub(
                                    w2qT[...].rearrange("p a b -> p (a b)")
                                    [:, 1024 * g:1024 * (g + 1)],
                                    c1b[...], c2b[...]))
                    buf_dv[j % 2] = DV.n

            wq_quant(d_w1T, 0, sw1, thr1, nthr1, w1qT)
            wq_quant(d_w2T, 2, sw2, thr2, nthr2, w2qT, swcol=sw2col)
            with em.on("vector"):
                em.fence()
                em.i(lambda e: e.tensor_scalar(sw1row[...], ones[0:1, :], sw1[...],
                                               None, OP.mult))
            barrier()

            for dsrc, dst, scale in ((d_bq, T.bqr, INV32), (d_bk, T.bkr, INV32),
                                     (d_bv, T.bvr, 1.0)):
                with em.on("sync"):
                    em.w(DV, DV.n)
                    em.d(LD, lambda e, dsrc=dsrc, dst=dst:
                         e.dma_start(dst[...], dsrc))
                with em.on("vector"):
                    em.w(LD, LD.n)
                    em.fence()
                    em.i(lambda e, dst=dst, scale=scale:
                         e.tensor_scalar(dst[...], dst[...], scale, None, OP.mult))

            nslab = TW // 128
            psl = [PT0[:, 0:512], PT0[:, 512:1024], PT1[:, 0:512], PT1[:, 512:1024],
                   PT2[:, 0:512], PT2[:, 512:1024]]
            psl_free = [0] * 6
            stg_free = [0] * 2
            slot_i = 0
            stg_i = 0
            stgs = [ten["stg0"], ten["stg1"]]
            batches = [(288 * i, 288) for i in range(8)]
            xall_cast = [0] * 8
            wldv = ten["xtf"][...].rearrange("p a b -> p (a b)")[:, 0:1024]
            for wi, (dwsrc, bias, dscr, scale) in enumerate((
                    (d_wq, T.bqr, d_qs, INV32), (d_wk, T.bkr, d_ks, INV32),
                    (d_wv, T.bvr, d_vs, 1.0))):
                for kch in range(8):
                    with em.on("sync"):
                        em.w(DV, DV.n)
                        em.w(PE, PE.n)
                        em.d(LD, lambda e, dwsrc=dwsrc, kch=kch:
                             e.dma_start(wldv, dwsrc[:, kch, :]))
                    with em.on("vector"):
                        em.w(LD, LD.n)
                        em.fence()
                        em.i(lambda e, kch=kch, scale=scale:
                             e.tensor_scalar(T.wb[:, kch, :], wldv, scale,
                                             None, OP.mult))
                wcast = DV.n
                for si, (s0, wdt) in enumerate(batches):
                    if wi == 0:
                        with em.on("gpsimd"):
                            if si >= 1:
                                em.w(DV, xall_cast[si - 1])
                            else:
                                em.w(DV, wcast)
                            em.d(L2, lambda e, s0=s0, wdt=wdt: e.dma_start(
                                T.xtf[:, :, 0:wdt], d_xT[:, :, s0:s0 + wdt]))
                        with em.on("vector"):
                            em.w(L2, L2.n)
                            em.fence()
                            em.i(lambda e, s0=s0, wdt=wdt: e.tensor_copy(
                                T.xall[:, :, s0:s0 + wdt], T.xtf[:, :, 0:wdt]))
                        xall_cast[si] = DV.n
                    xcast = xall_cast[si]
                    for mch in range(8):
                        slot = slot_i % 6
                        slot_i += 1
                        pslot = psl[slot]
                        with em.on("tensor"):
                            em.w(DV, max(xcast, wcast))
                            em.w(AC, psl_free[slot])
                            for kch in range(8):
                                em.i(lambda e, pslot=pslot, kch=kch, mch=mch,
                                     s0=s0, wdt=wdt:
                                     e.matmul(pslot[:, 0:wdt],
                                              T.wb[:, kch, 128 * mch:128 * (mch + 1)],
                                              T.xall[:, kch, s0:s0 + wdt],
                                              start=(kch == 0), stop=(kch == 7)),
                                     inc=(1 if kch == 7 else 0))

                        sgi = stg_i % 2
                        sgb = stgs[sgi]
                        stg_i += 1
                        with em.on("scalar"):
                            em.w(PE, PE.n)
                            em.w(ST, stg_free[sgi])
                            em.fence()
                            em.i(lambda e, sgb=sgb, pslot=pslot, wdt=wdt,
                                 bias=bias, mch=mch:
                                 e.activation(sgb[:, 0:wdt], pslot[:, 0:wdt],
                                              ACTF.Identity,
                                              bias=bias[:, mch:mch + 1]))
                        psl_free[slot] = AC.n
                        with em.on("sync"):
                            em.w(AC, AC.n)
                            em.d(ST, lambda e, sgb=sgb, dscr=dscr, mch=mch,
                                 s0=s0, wdt=wdt:
                                 e.dma_start(dscr[:, mch, s0:s0 + wdt],
                                             sgb[:, 0:wdt]))
                        stg_free[sgi] = ST.n
            barrier()
            for kch in range(8):
                with em.on("sync"):
                    em.w(DV, DV.n)
                    em.d(LD, lambda e, kch=kch: e.dma_start(T.wotmp[...],
                                                            d_wo[:, kch, :]))
                with em.on("vector"):
                    em.w(LD, LD.n)
                    em.fence()
                    em.i(lambda e, kch=kch: e.tensor_copy(T.wob[:, kch, :],
                                                          T.wotmp[...]))
            barrier()
            em.mute = False
            if "B" not in phases:
                return
            bi = phases.index("B")
            BL = int(phases[bi + 1]) if (bi + 1 < len(phases)
                                         and phases[bi + 1].isdigit()) else 4

            slot_ps = [PT0, PT1, PT2]
            slot_free_ac = [0, 0, 0]
            ebuf_free_pe = [0, 0, 0]
            zq4_free_dv = [0, 0]
            mv_evac_dv = DV.n
            Ebufs = [T.Ebuf0, T.Ebuf1, T.Ebuf2]
            Qhs = [T.Qh0, T.Qh1]
            osst_st = [0] * NHEAD
            prev_mv_pe = 0
            prev_r_dv = 0
            prev_osst = 0
            head_score_pe = 0
            for h in range(nheads):
                if not noload:
                    with em.on("sync"):
                        em.w(PE, head_score_pe if h >= 1 else PE.n)
                        t0 = 256 * h
                        em.d(LD, lambda e, t0=t0: e.dma_start(
                            T.K2[0:64, 0:2048], d_ks[0:64, :, t0:t0 + 256]))
                        em.d(LD, lambda e, t0=t0: e.dma_start(
                            T.K2[0:64, 2048:4096], d_ks[64:128, :, t0:t0 + 256]))
                        em.d(LD, lambda e, t0=t0: e.dma_start(
                            T.K2[64:128, 0:2048], d_ks[0:64, :, t0:t0 + 256]))
                        em.d(LD, lambda e, t0=t0: e.dma_start(
                            T.K2[64:128, 2048:4096],
                            d_ks[64:128, :, t0:t0 + 256]))
                        em.d(LD, lambda e, t0=t0, h=h: e.dma_start(
                            Qhs[h % 2][...], d_qs[:, :, t0:t0 + 256]))
                ld_done = LD.n
                qh = Qhs[h % 2]

                def emit_chain(jj, dv_gate):
                    ebc = Ebufs[jj % 3]
                    with em.on("tensor"):
                        em.w(DV, dv_gate)
                        if jj == 0:
                            em.w(DV, mv_evac_dv)
                        for q in range(8):
                            pt = PT3 if q < 4 else PT4
                            em.i(lambda e, pt=pt, q=q, ebc=ebc, jj=jj:
                                 e.matmul(pt[32 * (q % 4):32 * (q % 4) + 1,
                                             0:512],
                                          rb16[:, jj % 2:jj % 2 + 1],
                                          ebc[:, 512 * q:512 * (q + 1)],
                                          start=(jj == 0), stop=(jj == 31),
                                          tile_position=(0, 32 * (q % 4))),
                                 inc=(1 if q == 7 else 0))
                    return PE.n

                r_dv_arr = [0] * 32
                for j in range(32 if BL >= 2 else 0):
                    jslots = [(4 * j + g) % 3 for g in range(4)]
                    order = [0, 1, 2, 3] if j % 2 == 0 else [2, 3, 0, 1]
                    first_pos = min(order.index(0), order.index(3))
                    later_g = order[max(order.index(0), order.index(3))]
                    mm_pe = {}
                    ac_start = AC.n

                    def score_mm(g):
                        s = jslots[g]
                        em.w(AC, slot_free_ac[s])
                        if g == later_g:
                            em.w(AC, ac_start + first_pos + 1)
                        par = 0 if g < 2 else 64
                        tp = (0, 0) if g < 2 else (64, 0)
                        base = 1024 * (g % 2)
                        for half in range(2):
                            em.i(lambda e, s=s, half=half, par=par, tp=tp,
                                 base=base, j=j, qh=qh:
                                 e.matmul(
                                     slot_ps[s][:, 512 * half:512 * (half + 1)],
                                     T.K2[par:par + 64, 128 * j:128 * (j + 1)],
                                     qh[par:par + 64, :, :]
                                     .rearrange("p a b -> p (a b)")
                                     [:, base + 512 * half:base + 512 * (half + 1)],
                                     start=True, stop=True, tile_position=tp),
                                 inc=(1 if half == 1 else 0))
                        mm_pe[g] = PE.n

                    with em.on("tensor"):
                        em.w(LD, ld_done)
                        score_mm(order[0])
                        score_mm(order[1])
                    if BL >= 3 and j >= 2:
                        prev_mv_pe = emit_chain(j - 2, r_dv_arr[j - 2])
                        ebuf_free_pe[(j - 2) % 3] = prev_mv_pe
                    with em.on("tensor"):
                        score_mm(order[2])
                        score_mm(order[3])
                    eb = Ebufs[j % 3]
                    zo = 4 * (j % 2)
                    with em.on("scalar"):
                        em.w(PE, ebuf_free_pe[j % 3])
                        em.w(DV, zq4_free_dv[j % 2])
                        for g in order:
                            em.w(PE, mm_pe[g])
                            kw = ({"accum_out": zq4[:, zo + g:zo + g + 1]}
                                  if g < 2 else {})
                            em.i(lambda e, g=g, eb=eb, s=jslots[g], kw=kw:
                                 e.activation(eb[:, 1024 * g:1024 * (g + 1)],
                                              slot_ps[s][:, :], ACTF.Exp, **kw))
                            slot_free_ac[jslots[g]] = AC.n
                    with em.on("vector"):
                        em.w(AC, AC.n)
                        em.w(PE, prev_mv_pe)
                        for g in range(2, 4):
                            em.fence()
                            em.i(lambda e, g=g, eb=eb, zo=zo: e.tensor_reduce(
                                zq4[:, zo + g:zo + g + 1],
                                eb[:, 1024 * g:1024 * (g + 1)], AX.X, OP.add))
                        em.fence()
                        em.i(lambda e, zo=zo: e.tensor_add(
                            Zcol[...], zq4[:, zo:zo + 1], zq4[:, zo + 1:zo + 2]))
                        em.fence()
                        em.i(lambda e, zo=zo: e.tensor_add(
                            Zcol[...], Zcol[...], zq4[:, zo + 2:zo + 3]))
                        em.fence()
                        em.i(lambda e, zo=zo: e.tensor_add(
                            Zcol[...], Zcol[...], zq4[:, zo + 3:zo + 4]))
                        em.fence()
                        em.i(lambda e: e.reciprocal(rcol[...], Zcol[...]))
                        em.fence()
                        em.i(lambda e, j=j: e.tensor_copy(
                            rb16[:, j % 2:j % 2 + 1], rcol[...]))
                    prev_r_dv = DV.n
                    r_dv_arr[j] = DV.n
                    zq4_free_dv[j % 2] = DV.n
                    if j == 31:
                        head_score_pe = mm_pe[order[3]]
                if BL >= 3:
                    prev_mv_pe = emit_chain(30, r_dv_arr[30])
                    ebuf_free_pe[30 % 3] = prev_mv_pe
                    prev_mv_pe = emit_chain(31, r_dv_arr[31])
                    ebuf_free_pe[31 % 3] = prev_mv_pe
                if BL >= 4:
                    with em.on("vector"):
                        em.w(PE, PE.n)
                        em.w(ST, prev_osst)
                        for q in range(8):
                            pt = PT3 if q < 4 else PT4
                            em.fence()
                            em.i(lambda e, pt=pt, q=q: e.tensor_copy(
                                T.ossb[32 * (q % 4):32 * (q % 4) + 1,
                                       512 * (q // 4):512 * (q // 4) + 512],
                                pt[32 * (q % 4):32 * (q % 4) + 1, 0:512]))
                    mv_evac_dv = DV.n
                    with em.on("sync"):
                        em.w(DV, mv_evac_dv)
                        for q in range(8):
                            em.d(ST, lambda e, h=h, q=q: e.dma_start(
                                d_oss[h, 0:1, 512 * q:512 * (q + 1)],
                                T.ossb[32 * (q % 4):32 * (q % 4) + 1,
                                       512 * (q // 4):512 * (q // 4) + 512]))
                osst_st[h] = ST.n
                prev_osst = ST.n
            barrier()
            if "C" not in phases:
                return

            wo_free_dv = [DV.n, DV.n]
            wt_slot_free = [0, 0]
            prev_vsc = 0
            prev_lnin = [0, 0]
            prev_sq_ac = 0
            prev_m1tr = 0
            prev_ust = 0
            prev_xqst = 0
            prev_wbc = 0
            osld = 0
            Vts = [T.Vt, T.Vt1]
            with em.on("gpsimd"):
                em.d(L2, lambda e: e.dma_start(T.Vt[...], d_vs[:, :, 0:128]))
            vx_ld = [L2.n, 0]
            vt_free = [0, 0]
            for t in range(NT):
                h = t // 2
                hb = t % 2
                vtb = Vts[t % 2]
                xtb = T.Xt0
                if hb == 0:
                    with em.on("sync"):
                        em.w(ST, osst_st[h])
                        em.w(PE, prev_wbc)
                        em.d(LD, lambda e, h=h: e.dma_start(T.osrow[...],
                                                            d_oss[h, :, :]))
                    osld = LD.n
                if t + 1 < NT:
                    with em.on("gpsimd"):
                        em.w(DV, vt_free[(t + 1) % 2])
                        em.d(L2, lambda e, t=t: e.dma_start(
                            Vts[(t + 1) % 2][...],
                            d_vs[:, :, 128 * (t + 1):128 * (t + 2)]))
                    vx_ld[(t + 1) % 2] = L2.n
                with em.on("gpsimd"):
                    em.w(DV, prev_lnin[0])
                    em.d(L2, lambda e, t=t: e.dma_start(T.Xt0[...],
                                                        d_xtm[t, :, :]))
                x_ld = L2.n
                os3 = T.osrow[0:1, :].rearrange("o (q c) -> o q c", c=256)
                with em.on("tensor"):
                    em.w(LD, osld)
                    em.w(DV, prev_vsc)
                    for b4 in range(2):
                        em.i(lambda e, b4=b4, hb=hb, os3=os3:
                             e.matmul(PT2[0:64, 512 * b4:512 * b4 + 512],
                                      ones[0:1, 0:64],
                                      os3[0:1, 4 * b4:4 * b4 + 4,
                                          128 * hb:128 * hb + 128],
                                      start=True, stop=True,
                                      tile_position=(0, 0)),
                             inc=0)
                    for b4 in range(2):
                        em.i(lambda e, b4=b4, hb=hb, os3=os3:
                             e.matmul(PT2[64:128, 512 * b4:512 * b4 + 512],
                                      ones[0:1, 0:64],
                                      os3[0:1, 8 + 4 * b4:8 + 4 * b4 + 4,
                                          128 * hb:128 * hb + 128],
                                      start=True, stop=True,
                                      tile_position=(0, 64)),
                             inc=(1 if b4 == 1 else 0))
                with em.on("vector"):
                    em.w(PE, PE.n)
                    em.w(L2, vx_ld[t % 2])
                    em.fence()
                    em.i(lambda e, vtb=vtb: e.tensor_mul(
                        T.Ot[...], vtb[...],
                        PT2[:, 0:1024].rearrange("p (a b) -> p a b", b=128)))
                prev_wbc = PE.n
                prev_vsc = DV.n
                wps = [PT0, PT1][t % 2]
                with em.on("tensor"):
                    em.w(DV, max(prev_vsc, wo_free_dv[t % 2]))
                    for half in range(2):
                        for kch in range(8):
                            em.i(lambda e, wps=wps, half=half, kch=kch:
                                 e.matmul(wps[:, 512 * half:512 * (half + 1)],
                                          T.Ot[:, kch, :],
                                          T.wob[:, kch, 512 * half:512 * (half + 1)],
                                          start=(kch == 0), stop=False),
                                 inc=0)
                        em.i(lambda e, wps=wps, half=half:
                             e.matmul(wps[:, 512 * half:512 * (half + 1)],
                                      ones[0:1, :],
                                      borow[0:1, 512 * half:512 * (half + 1)],
                                      start=False, stop=True),
                             inc=(1 if half == 1 else 0))
                wo_done = PE.n
                with em.on("vector"):
                    em.w(PE, wo_done)
                    em.w(AC, prev_sq_ac)
                    em.w(L2, x_ld)
                    em.fence()
                    em.i(lambda e, wps=wps, xtb=xtb: e.tensor_add(
                        T.lnin[...], wps[...], xtb[...]))
                wo_free_dv[t % 2] = DV.n
                prev_lnin[0] = DV.n
                vt_free[t % 2] = DV.n
                lnin_dv = DV.n
                with em.on("scalar"):
                    em.w(DV, lnin_dv)
                    em.fence()
                    em.i(lambda e: e.activation(T.qsc[...], T.lnin[...], ACTF.Square,
                                                accum_out=stat[:, 1:2]))
                prev_sq_ac = AC.n
                with em.on("vector"):
                    em.w(AC, AC.n)
                    em.w(PE, prev_m1tr)
                    em.fence()
                    em.i(lambda e: e.tensor_scalar(T.qsc[...], T.lnin[...], 1.0,
                                                   None, OP.mult, OP.add,
                                                   accum_out=stat[:, 0:1]))
                    em.fence()
                    em.i(lambda e: e.tensor_scalar(stat[:, 2:3], stat[:, 0:1],
                                                   -1.0 / 1024.0, None, OP.mult))
                    em.fence()
                    em.i(lambda e: e.tensor_scalar(stat[:, 3:4], stat[:, 1:2],
                                                   1.0 / 1024.0, EPS, OP.mult,
                                                   OP.add))
                    em.fence()
                    em.i(lambda e: e.tensor_mul(stat[:, 4:5], stat[:, 2:3],
                                                stat[:, 2:3]))
                    em.fence()
                    em.i(lambda e: e.tensor_sub(stat[:, 3:4], stat[:, 3:4],
                                                stat[:, 4:5]))
                with em.on("scalar"):
                    em.w(DV, DV.n)
                    em.fence()
                    em.i(lambda e: e.activation(stat[:, 4:5], stat[:, 3:4], ACTF.Ln))
                    em.fence()
                    em.i(lambda e: e.activation(stat[:, 5:6], stat[:, 4:5],
                                                ACTF.Exp, scale=neghalf[...]))
                rstd_ac = AC.n
                with em.on("vector"):
                    em.w(AC, rstd_ac)
                    em.w(ST, prev_ust)
                    em.fence()
                    em.i(lambda e: e.tensor_scalar(T.utile[...], T.lnin[...],
                                                   stat[:, 2:3], stat[:, 5:6],
                                                   OP.add, OP.mult))
                    em.fence()
                    em.i(lambda e: e.tensor_reduce(stat[:, 6:7], T.utile[...],
                                                   AX.X, OP.max,
                                                   apply_absolute_value=True))
                    em.fence()
                    em.i(lambda e: e.tensor_scalar(stat[:, 6:7], stat[:, 6:7], 1e-5,
                                                   None, OP.max))
                    em.fence()
                    em.i(lambda e: e.reciprocal(stat[:, 7:8], stat[:, 6:7]))
                    em.fence()
                    em.i(lambda e: e.tensor_scalar(stat[:, 7:8], stat[:, 7:8],
                                                   127.0, None, OP.mult))
                    em.fence()
                    em.i(lambda e: e.tensor_scalar(stat[:, 6:7], stat[:, 6:7],
                                                   1.0 / 127.0, None, OP.mult))
                ufin_dv = DV.n
                with em.on("sync"):
                    em.w(DV, ufin_dv)
                    em.d(ST, lambda e, t=t: e.dma_start(d_us[t, :, :], T.utile[...]))
                prev_ust = ST.n
                with em.on("tensor"):
                    em.w(DV, ufin_dv)
                    em.i(lambda e: e.transpose(PT3[0:1, 0:128], stat[:, 6:7],
                                               ident[...]))
                prev_m1tr = PE.n
                with em.on("vector"):
                    em.w(PE, prev_m1tr)
                    em.fence()
                    em.i(lambda e, t=t: e.tensor_copy(
                        m1row[0:1, 128 * t:128 * (t + 1)], PT3[0:1, 0:128]))
                with em.on("vector"):
                    zv = T.qsc[:, 0:512]
                    cv = T.qsc[:, 512:1024]
                    for hf in range(2):
                        sl = slice(512 * hf, 512 * (hf + 1))
                        em.fence()
                        em.i(lambda e, sl=sl, zv=zv: e.tensor_scalar(
                            zv, T.utile[:, sl], stat[:, 7:8], 0.5,
                            OP.mult, OP.add))
                        em.fence()
                        em.i(lambda e, zv=zv: e.tensor_copy(T.zich[...], zv))
                        em.fence()
                        em.i(lambda e, zv=zv, cv=cv: e.tensor_tensor(
                            cv, T.zich[...], zv, OP.is_gt))
                        em.fence()
                        em.i(lambda e, sl=sl, cv=cv: e.tensor_sub(
                            T.xqtm[:, sl], T.zich[...], cv))
                q1_dv = DV.n
                with em.on("tensor"):
                    em.w(DV, q1_dv)
                    for ch in range(8):
                        em.i(lambda e, ch=ch, wps=wps: e.transpose(
                            wps[:, 128 * ch:128 * (ch + 1)],
                            T.xqtm[:, 128 * ch:128 * (ch + 1)], ident[...]))
                xqtr_pe = PE.n
                with em.on("vector"):
                    em.w(PE, xqtr_pe)
                    em.w(ST, prev_xqst)
                    em.fence()
                    em.i(lambda e, wps=wps: e.tensor_copy(
                        T.xqTt[...],
                        wps[...].rearrange("p (a b) -> p a b", b=128)))
                wo_free_dv[t % 2] = DV.n
                with em.on("sync"):
                    em.w(DV, DV.n)
                    em.d(ST, lambda e, t=t: e.dma_start(d_xqs[t, :, :, :],
                                                        T.xqTt[...]))
                prev_xqst = ST.n
            barrier()
            if "D" not in phases:
                return

            with em.on("scalar"):
                em.i(lambda e: _load_act_set(e, 18), inc=0)
            xqts = [T.xqt0, T.xqt1]
            with em.on("vector"):
                em.fence()
                em.i(lambda e: e.tensor_copy(mhrow[0:1, 0:1],
                                             m1row[0:1, 2303:2304]))
                em.fence()
                em.i(lambda e: e.tensor_scalar(mhrow[0:1, 0:1], mhrow[0:1, 0:1],
                                               sw1[...], None, OP.mult))
            m12h_dv = DV.n
            with em.on("sync"):
                em.d(LD, lambda e: e.dma_start(T.xqt0[:, :, 0:128],
                                               d_xqs[17, :, :, :]))
            halo_ld = LD.n
            with em.on("vector"):
                em.w(LD, halo_ld)
                em.fence()
                em.i(lambda e: e.tensor_copy(T.xqth[...], T.xqt0[:, :, 127:128]))
            halo_dv = DV.n
            with em.on("tensor"):
                em.w(DV, halo_dv)
                for och in range(32):
                    for kch in range(8):
                        em.i(lambda e, och=och, kch=kch:
                             e.matmul(PT4[:, och:och + 1],
                                      w1qT[:, kch, 128 * och:128 * (och + 1)],
                                      T.xqth[:, kch, 0:1],
                                      start=(kch == 0), stop=(kch == 7)),
                             inc=(1 if (och == 31 and kch == 7) else 0))
            with em.on("tensor"):
                em.w(DV, m12h_dv)
                em.i(lambda e: e.matmul(PT3[:, 0:1], ones[0:1, :], mhrow[0:1, 0:1],
                                        start=True, stop=True))
            mh_pe = PE.n
            with em.on("vector"):
                em.w(PE, mh_pe)
                em.fence()
                em.i(lambda e: e.tensor_copy(T.mhsb[...], PT3[:, 0:1]))
                em.fence()
                em.i(lambda e: e.tensor_mul(
                    T.zedge[...],
                    PT4[:, 0:32].rearrange("p (a b) -> p a b", b=1),
                    T.mhsb[...].rearrange("p (o b) -> p o b", o=1)
                    .broadcast_to([128, 32, 1])))
                em.fence()
                em.i(lambda e: e.tensor_mul(
                    T.zedge[...], T.zedge[...],
                    masklr[:, 0:1].rearrange("p (o b) -> p o b", o=1)
                    .broadcast_to([128, 32, 1])))
            zedge_dv = DV.n

            f1_slot_free = zedge_dv
            f2_free_dv = DV.n
            htile_free_dv = 0
            ldx_pe_wait = 0
            yst_st = {}
            uld_free_dv = 0
            q2_pe = [0, 0]
            prev_small = DV.n
            fused_sq_ac = 0
            ogroups = [(0, 3), (3, 6), (6, 9), (9, 12), (12, 15), (15, 18),
                       (18, 21), (21, 24), (24, 27), (27, 30), (30, 32)]
            with em.on("sync"):
                em.w(DV, halo_dv)
                em.d(LD, lambda e: e.dma_start(xqts[0][:, :, 0:128],
                                               d_xqs[0, :, :, :]))
            xq_ld_t = [LD.n] + [0] * NT
            for t in range(NOWN):
                with em.on("sync"):
                    if t >= 1:
                        em.w(PE, ldx_pe_wait)
                    em.w(DV, uld_free_dv)
                    em.d(LD, lambda e, t=t: e.dma_start(
                        xqts[(t + 1) % 2][:, :, 0:128], d_xqs[t + 1, :, :, :]))
                    em.d(LD, lambda e, t=t: e.dma_start(T.uld[...], d_us[t, :, :]))
                xq_ld_t[t + 1] = LD.n
                xq_ld = LD.n
                with em.on("vector"):
                    em.w(LD, xq_ld)
                    em.fence()
                    em.i(lambda e, t=t: e.tensor_copy(
                        xqts[t % 2][:, :, 128:129], xqts[(t + 1) % 2][:, :, 0:1]))
                xcol_dv = DV.n
                with em.on("tensor"):
                    em.w(DV, max(m12h_dv, prev_small))
                    em.i(lambda e, t=t: e.matmul(PT2[:, 0:129], sw1row[...],
                                                 m1row[0:1, 128 * t:128 * t + 129],
                                                 start=True, stop=True))
                with em.on("vector"):
                    em.w(PE, PE.n)
                    em.fence()
                    em.i(lambda e: e.tensor_copy(T.m12sb[...], PT2[:, 0:129]))
                m12t_dv = DV.n
                with em.on("vector"):
                    em.w(AC, fused_sq_ac)
                    em.fence()
                    em.i(lambda e: e.tensor_copy(T.zti[:, :, 0:1], T.zedge[...]))
                def sweeps(p0, p1, pac):
                    with em.on("vector"):
                        em.w(AC, pac)
                        em.fence()
                        for och in range(p0, p1):
                            em.i(lambda e, och=och: e.scalar_tensor_tensor(
                                T.htile[:, och, :], T.zti[:, och, 0:128],
                                cwa[:, och:och + 1], T.htile[:, och, :],
                                OP.mult, OP.add))
                        em.fence()
                        for och in range(p0, p1):
                            em.i(lambda e, och=och: e.scalar_tensor_tensor(
                                T.htile[:, och, :], T.zti[:, och, 2:130],
                                cwc[:, och:och + 1], T.htile[:, och, :],
                                OP.mult, OP.add))

                prev_grp = None
                for (o0, o1) in ogroups:
                    ng = o1 - o0
                    with em.on("tensor"):
                        em.w(LD, xq_ld)
                        em.w(DV, max(f1_slot_free, xcol_dv))
                        for oo in range(ng):
                            och = o0 + oo
                            for kch in range(8):
                                em.i(lambda e, oo=oo, och=och, kch=kch, t=t:
                                     e.matmul(PT3[:, 129 * oo:129 * oo + 129],
                                              w1qT[:, kch, 128 * och:128 * (och + 1)],
                                              xqts[t % 2][:, kch, 0:129],
                                              start=(kch == 0), stop=(kch == 7)),
                                     inc=(1 if (oo == ng - 1 and kch == 7) else 0))
                    with em.on("vector"):
                        em.w(PE, PE.n)
                        em.w(DV, m12t_dv)
                        em.fence()
                        em.i(lambda e, o0=o0, ng=ng: e.tensor_mul(
                            T.zti[:, o0:o0 + ng, 1:130],
                            PT3[:, 0:129 * ng].rearrange("p (a b) -> p a b", b=129),
                            T.m12sb[...].rearrange("p (o b) -> p o b", o=1)
                            .broadcast_to([128, ng, 129])))
                        if t == NOWN - 1:
                            em.fence()
                            em.i(lambda e, o0=o0, ng=ng: e.tensor_mul(
                                T.zti[:, o0:o0 + ng, 129:130],
                                T.zti[:, o0:o0 + ng, 129:130],
                                masklr[:, 1:2].rearrange("p (o b) -> p o b", o=1)
                                .broadcast_to([128, ng, 1])))
                        em.fence()
                        em.i(lambda e, o0=o0, ng=ng: e.tensor_copy(
                            T.zedge[:, o0:o0 + ng, :],
                            T.zti[:, o0:o0 + ng, 128:129]))
                    f1_slot_free = DV.n
                    with em.on("scalar"):
                        em.w(DV, f1_slot_free)
                        if o0 == 0:
                            em.w(DV, htile_free_dv)
                        for och in range(o0, o1):
                            em.i(lambda e, och=och: e.activation(
                                T.htile[:, och, :], T.zti[:, och, 1:129],
                                ACTF.Identity, scale=cwb[:, och:och + 1],
                                bias=cb32[:, och:och + 1]))
                    if prev_grp is not None:
                        sweeps(*prev_grp)
                    prev_grp = (o0, o1, AC.n)
                ldx_pe_wait = PE.n
                sweeps(*prev_grp)
                conv_dv = DV.n
                with em.on("scalar"):
                    em.w(DV, conv_dv)
                    em.fence()
                    em.i(lambda e: e.activation(T.htile[...], T.htile[...],
                                                ACTF.Silu))
                silu_ac = AC.n
                with em.on("vector"):
                    em.w(AC, silu_ac)
                    em.fence()
                    em.i(lambda e: e.tensor_reduce(
                        T.Mp[...], T.htile[...].rearrange("p a b -> p b a"),
                        AX.X, OP.max, apply_absolute_value=True))
                mnmx_dv = DV.n
                with em.on("tensor"):
                    em.w(DV, mnmx_dv)
                    em.i(lambda e: e.transpose(PT4[0:128, 0:128], T.Mp[...],
                                               ident[...]))
                mtr_pe = PE.n
                with em.on("vector"):
                    em.w(PE, mtr_pe)
                    em.fence()
                    em.i(lambda e: e.tensor_reduce(q2st[:, 0:1], PT4[:, 0:128],
                                                   AX.X, OP.max))
                    em.fence()
                    em.i(lambda e: e.tensor_scalar(q2st[:, 0:1], q2st[:, 0:1],
                                                   1e-5, None, OP.max))
                    em.fence()
                    em.i(lambda e: e.reciprocal(q2st[:, 1:2], q2st[:, 0:1]))
                    em.fence()
                    em.i(lambda e: e.tensor_scalar(q2st[:, 1:2], q2st[:, 1:2],
                                                   127.0, None, OP.mult))
                    em.fence()
                    em.i(lambda e: e.tensor_scalar(q2st[:, 2:3], q2st[:, 0:1],
                                                   sw2col[...], None, OP.mult))
                    em.fence()
                    em.i(lambda e: e.tensor_scalar(q2st[:, 2:3], q2st[:, 2:3],
                                                   1.0 / 127.0, None, OP.mult))
                q2s_dv = DV.n
                with em.on("tensor"):
                    em.w(DV, q2s_dv)
                    em.i(lambda e: e.transpose(PT4[0:1, 256:384], q2st[:, 1:2],
                                               ident[...]))
                with em.on("vector"):
                    em.w(PE, PE.n)
                    em.fence()
                    em.i(lambda e: e.tensor_copy(s2row[0:1, 0:128],
                                                 PT4[0:1, 256:384]))
                with em.on("tensor"):
                    em.w(DV, DV.n)
                    em.i(lambda e: e.matmul(PT4[:, 384:512], ones[0:1, :],
                                            s2row[0:1, 0:128],
                                            start=True, stop=True))
                with em.on("vector"):
                    em.w(PE, PE.n)
                    em.fence()
                    em.i(lambda e: e.tensor_copy(T.s2sb[...], PT4[:, 384:512]))
                prev_small = DV.n
                fps = [PT0, PT1][t % 2]
                for hf in range(2):
                    x2b = [T.x2q, T.x2qb][hf]
                    xs = T.zti[:, 0:16, 1:129]
                    cs = T.zti[:, 16:32, 1:129]
                    with em.on("vector"):
                        em.w(PE, q2_pe[hf])
                        em.fence()
                        em.i(lambda e, hf=hf, xs=xs: e.tensor_mul(
                            xs, T.htile[:, 16 * hf:16 * (hf + 1), :],
                            T.s2sb[...].rearrange("p (o b) -> p o b", o=1)
                            .broadcast_to([128, 16, 128])))
                        em.fence()
                        em.i(lambda e, xs=xs: e.tensor_scalar(
                            xs, xs, 1.0, 0.5, OP.mult, OP.add))
                        em.fence()
                        em.i(lambda e, xs=xs: e.tensor_copy(T.zi16[...], xs))
                        em.fence()
                        em.i(lambda e, xs=xs, cs=cs: e.tensor_tensor(
                            cs, T.zi16[...], xs, OP.is_gt))
                        em.fence()
                        em.i(lambda e, x2b=x2b, cs=cs: e.tensor_sub(
                            x2b[...], T.zi16[...], cs))
                    x2q_dv = DV.n
                    with em.on("tensor"):
                        em.w(DV, max(x2q_dv, f2_free_dv))
                        for half in range(2):
                            for kk in range(16):
                                kch = 16 * hf + kk
                                em.i(lambda e, fps=fps, half=half, kk=kk, kch=kch,
                                     x2b=x2b:
                                     e.matmul(fps[:, 512 * half:512 * (half + 1)],
                                              x2b[:, kk, :],
                                              w2qT[:, kch, 512 * half:512 * (half + 1)],
                                              start=(kch == 0),
                                              stop=(kch == 31)),
                                     inc=(1 if (half == 1 and kk == 15) else 0))
                    q2_pe[hf] = PE.n
                f2_pe = PE.n
                htile_free_dv = DV.n
                with em.on("vector"):
                    em.w(PE, f2_pe)
                    em.w(LD, xq_ld)
                    em.w(ST, yst_st.get(t - 1, 0))
                    em.fence()
                    em.i(lambda e, fps=fps: e.tensor_scalar(
                        T.ypre[...], fps[...], q2st[:, 2:3], None, OP.mult))
                    em.fence()
                    em.i(lambda e: e.tensor_add(T.ypre[...], T.ypre[...],
                                                T.uld[...]))
                f2_free_dv = DV.n
                uld_free_dv = DV.n
                ypre_dv = DV.n
                sink = T.zti[:, 8:16, 1:129]
                with em.on("scalar"):
                    em.w(DV, ypre_dv)
                    em.i(lambda e: _load_act_set(e, 6), inc=0)
                    em.fence()
                    em.i(lambda e, sink=sink: e.activation(
                        sink, T.ypre[...], ACTF.Square,
                        accum_out=T.est[:, 1:2]))
                fused_sq_ac = AC.n
                with em.on("vector"):
                    em.w(AC, fused_sq_ac)
                    em.fence()
                    em.i(lambda e, sink=sink: e.tensor_scalar(
                        sink, T.ypre[...], 1.0, None, OP.mult, OP.add,
                        accum_out=T.est[:, 0:1]))
                    em.fence()
                    em.i(lambda e: e.tensor_scalar(T.est[:, 2:3], T.est[:, 0:1],
                                                   -1.0 / 1024.0, None, OP.mult))
                    em.fence()
                    em.i(lambda e: e.tensor_scalar(T.est[:, 3:4], T.est[:, 1:2],
                                                   1.0 / 1024.0, EPS, OP.mult,
                                                   OP.add))
                    em.fence()
                    em.i(lambda e: e.tensor_mul(T.est[:, 4:5], T.est[:, 2:3],
                                                T.est[:, 2:3]))
                    em.fence()
                    em.i(lambda e: e.tensor_sub(T.est[:, 3:4], T.est[:, 3:4],
                                                T.est[:, 4:5]))
                est_dv = DV.n
                with em.on("scalar"):
                    em.w(DV, est_dv)
                    em.fence()
                    em.i(lambda e: e.activation(T.est[:, 4:5], T.est[:, 3:4],
                                                ACTF.Ln))
                    em.fence()
                    em.i(lambda e: e.activation(T.est[:, 5:6], T.est[:, 4:5],
                                                ACTF.Exp, scale=neghalf[...]))
                    em.i(lambda e: _load_act_set(e, 18), inc=0)
                rs_ac = AC.n
                with em.on("vector"):
                    em.w(AC, rs_ac)
                    em.fence()
                    em.i(lambda e: e.tensor_scalar(T.ypre[...], T.ypre[...],
                                                   T.est[:, 2:3], T.est[:, 5:6],
                                                   OP.add, OP.mult))
                with em.on("sync"):
                    em.w(DV, DV.n)
                    em.d(ST, lambda e, t=t: e.dma_start(
                        d_out[128 * t:128 * (t + 1), :], T.ypre[...]))
                yst_st[t] = ST.n
            with em.on("sync"):
                em.w(ST, ST.n)
                em.w(L2, L2.n)

        with nc.Block() as blk:
            @blk.sync
            def _(e):
                program(Em("sync", e))

            @blk.tensor
            def _(e):
                program(Em("tensor", e))

            @blk.vector
            def _(e):
                program(Em("vector", e))

            @blk.scalar
            def _(e):
                _load_act_set(e, 6)
                program(Em("scalar", e))

            @blk.gpsimd
            def _(e):
                program(Em("gpsimd", e))
    return nc


_NC_CACHE = {}


def _get_program():
    if "nc" not in _NC_CACHE:
        _NC_CACHE["nc"] = build_program()
    return _NC_CACHE["nc"]


def _stage_core(x, c, consts):
    n, half = c // 2, c % 2
    own = x[n, 2048 * half:2048 * half + 2048]
    if half == 0:
        foreign = x[n, 2048:2048 + 256]
        maskL, maskR = 0.0, 1.0
    else:
        foreign = x[n, 2048 - 256:2048]
        maskL, maskR = 1.0, 0.0
    win = np.concatenate([own, foreign], axis=0)
    m = dict(consts)
    m["xw_tm"] = np.ascontiguousarray(win.reshape(NT, 128, 1024))
    m["xw_T"] = np.ascontiguousarray(win.T.reshape(8, 128, TW).transpose(1, 0, 2))
    m["maskLR"] = np.broadcast_to(np.array([maskL, maskR], np.float32),
                                  (128, 2)).copy()
    return m


def kernel(L0_attn_X_embed, L0_attn_Wq, L0_attn_bq, L0_attn_Wk, L0_attn_bk,
           L0_attn_Wv, L0_attn_bv, L0_attn_Wo, L0_attn_bo, L0_ffn_w1,
           L0_ffn_conv_w, L0_ffn_conv_b, L0_ffn_w2):
    x = np.ascontiguousarray(np.asarray(L0_attn_X_embed, np.float32))
    f32 = lambda a: np.ascontiguousarray(np.asarray(a, np.float32))

    def chunk_major(w):
        w = f32(w)
        return np.ascontiguousarray(w.reshape(8, 128, w.shape[1]).transpose(1, 0, 2))

    consts = {
        "wq": chunk_major(L0_attn_Wq), "wk": chunk_major(L0_attn_Wk),
        "wv": chunk_major(L0_attn_Wv), "wo": chunk_major(L0_attn_Wo),
        "w1T": chunk_major(f32(L0_ffn_w1).T),
        "w2T": np.ascontiguousarray(
            f32(L0_ffn_w2).T.reshape(32, 128, 1024).transpose(1, 0, 2)),
        "bq_col": np.ascontiguousarray(f32(L0_attn_bq).reshape(8, 128).T),
        "bk_col": np.ascontiguousarray(f32(L0_attn_bk).reshape(8, 128).T),
        "bv_col": np.ascontiguousarray(f32(L0_attn_bv).reshape(8, 128).T),
        "bo_row": f32(L0_attn_bo).reshape(1, 1024),
        "cw3a": np.ascontiguousarray(f32(L0_ffn_conv_w)[:, 0, 0].reshape(32, 128).T),
        "cw3b": np.ascontiguousarray(f32(L0_ffn_conv_w)[:, 0, 1].reshape(32, 128).T),
        "cw3c": np.ascontiguousarray(f32(L0_ffn_conv_w)[:, 0, 2].reshape(32, 128).T),
        "cb32": np.ascontiguousarray(f32(L0_ffn_conv_b).reshape(32, 128).T),
        "ones128": np.ones((128, 128), np.float32),
        "ones_row": np.ones((1, 384), np.float32),
        "ident": np.eye(128, dtype=np.float32),
        "neghalf": np.full((128, 1), -0.5, np.float32),
    }
    in_maps = [_stage_core(x, c, consts) for c in range(8)]
    nc = _get_program()
    res = run_bass_kernel_spmd(nc, in_maps, core_ids=list(range(8)))
    out = np.empty((4, 4096, 1024), np.float32)
    for c in range(8):
        n, half = c // 2, c % 2
        out[n, 2048 * half:2048 * half + 2048] = res.results[c]["out"]
    return out


# version 22 (final)
# speedup vs baseline: 1.1878x; 1.0626x over previous
import contextlib
import numpy as np
import ml_dtypes

import concourse.bass as bass
import concourse.mybir as mybir
from concourse.bass_utils import run_bass_kernel_spmd

F32 = mybir.dt.float32
BF16 = mybir.dt.bfloat16
I32 = mybir.dt.int32
AX = mybir.AxisListType
OP = mybir.AluOpType
ACTF = mybir.ActivationFunctionType

NT = 18
NOWN = 16
TW = NT * 128
NHEAD = 9
INV32 = 1.0 / 32.0
EPS = 1e-5


class Ctr:
    __slots__ = ("h", "n")
    def __init__(self, h):
        self.h = h
        self.n = 0


def _load_act_set(e, idx):
    return e.add_instruction(mybir.InstLoadActFuncSet(
        name=e.bass.get_next_instruction_name(),
        act_func_set_id=idx, ins=[], outs=[]))


class Em:
    def __init__(self, live, e):
        self.live = live
        self.e = e
        self.ctrs = None
        self.cur = None
        self.mute = False

    @contextlib.contextmanager
    def on(self, eng):
        prev = self.cur
        self.cur = eng
        yield
        self.cur = prev

    def w(self, ctr, val):
        if self.mute:
            return
        if self.cur == self.live and val > 0:
            self.e.wait_ge(ctr.h, val)

    def i(self, fn, inc=1):
        if self.mute:
            return
        c = self.ctrs[self.cur]
        if self.cur == self.live:
            inst = fn(self.e)
            if inc:
                inst.then_inc(c.h, inc)
        c.n += inc

    def d(self, ctr, fn):
        if self.mute:
            return
        if self.cur == self.live and ctr.n > 0:
            self.e.wait_ge(ctr.h, ctr.n)
        if self.cur == self.live:
            fn(self.e).then_inc(ctr.h, 16)
        ctr.n += 16

    def fence(self):
        c = self.ctrs[self.cur]
        self.w(c, c.n)


def build_program(phases="ABCDE", nheads=NHEAD, noload=False):
    nc = bass.Bass("TRN2", target_bir_lowering=False)

    d_xtm = nc.dram_tensor("xw_tm", [NT, 128, 1024], F32, kind="ExternalInput").ap()
    d_xT = nc.dram_tensor("xw_T", [128, 8, TW], F32, kind="ExternalInput").ap()
    d_wq = nc.dram_tensor("wq", [128, 8, 1024], F32, kind="ExternalInput").ap()
    d_wk = nc.dram_tensor("wk", [128, 8, 1024], F32, kind="ExternalInput").ap()
    d_wv = nc.dram_tensor("wv", [128, 8, 1024], F32, kind="ExternalInput").ap()
    d_wo = nc.dram_tensor("wo", [128, 8, 1024], F32, kind="ExternalInput").ap()
    d_w1T = nc.dram_tensor("w1T", [128, 8, 4096], F32, kind="ExternalInput").ap()
    d_w2T = nc.dram_tensor("w2T", [128, 32, 1024], F32, kind="ExternalInput").ap()
    d_bq = nc.dram_tensor("bq_col", [128, 8], F32, kind="ExternalInput").ap()
    d_bk = nc.dram_tensor("bk_col", [128, 8], F32, kind="ExternalInput").ap()
    d_bv = nc.dram_tensor("bv_col", [128, 8], F32, kind="ExternalInput").ap()
    d_borow = nc.dram_tensor("bo_row", [1, 1024], F32, kind="ExternalInput").ap()
    d_cwa = nc.dram_tensor("cw3a", [128, 32], F32, kind="ExternalInput").ap()
    d_cwb = nc.dram_tensor("cw3b", [128, 32], F32, kind="ExternalInput").ap()
    d_cwc = nc.dram_tensor("cw3c", [128, 32], F32, kind="ExternalInput").ap()
    d_cb = nc.dram_tensor("cb32", [128, 32], F32, kind="ExternalInput").ap()
    d_ones = nc.dram_tensor("ones128", [128, 128], F32, kind="ExternalInput").ap()
    d_onesw = nc.dram_tensor("ones_row", [1, 384], F32, kind="ExternalInput").ap()
    d_ident = nc.dram_tensor("ident", [128, 128], F32, kind="ExternalInput").ap()
    d_nh = nc.dram_tensor("neghalf", [128, 1], F32, kind="ExternalInput").ap()
    d_mask = nc.dram_tensor("maskLR", [128, 2], F32, kind="ExternalInput").ap()
    d_out = nc.dram_tensor("out", [NOWN * 128, 1024], F32, kind="ExternalOutput").ap()

    d_qs = nc.dram_tensor("q_scr", [128, 8, TW], BF16, kind="Internal").ap()
    d_ks = nc.dram_tensor("k_scr", [128, 8, TW], BF16, kind="Internal").ap()
    d_vs = nc.dram_tensor("v_scr", [128, 8, TW], BF16, kind="Internal").ap()
    d_us = nc.dram_tensor("u_scr", [NT, 128, 1024], F32, kind="Internal").ap()
    d_xqs = nc.dram_tensor("xqT_scr", [NT, 128, 8, 128], BF16, kind="Internal").ap()
    d_oss = nc.dram_tensor("os_scr", [NHEAD, 1, 4096], F32, kind="Internal").ap()
    d_ys = nc.dram_tensor("y_scr", [NOWN, 128, 1024], F32, kind="Internal").ap()

    with contextlib.ExitStack() as stk:
        SBT = lambda st, name, shape, dt=F32: st.enter_context(
            nc.sbuf_tensor(name, shape, dt))
        sb = lambda name, shape, dt=F32: SBT(stk, name, shape, dt)
        psum = lambda name, shape: stk.enter_context(nc.psum_tensor(name, shape, F32))

        ones = sb("ones", [128, 128])
        onesw = sb("onesw", [1, 384])
        ident = sb("ident_sb", [128, 128])
        cwa = sb("cwa", [128, 32]); cwb = sb("cwb", [128, 32])
        cwc = sb("cwc", [128, 32]); cb32 = sb("cb32sb", [128, 32])
        neghalf = sb("neghalfsb", [128, 1])
        masklr = sb("masklrsb", [128, 2])
        sw1 = sb("sw1", [1, 1]); sw2 = sb("sw2", [1, 1])
        sw2col = sb("sw2col", [128, 1])
        thr1 = sb("thr1", [128, 1]); nthr1 = sb("nthr1", [128, 1])
        thr2 = sb("thr2", [128, 1]); nthr2 = sb("nthr2", [128, 1])
        wsum = sb("wsum", [128, 1])
        stat = sb("stat", [128, 8])
        q2st = sb("q2st", [128, 4])
        zq4 = sb("zq4", [128, 8])
        Zcol = sb("Zcol", [128, 1])
        rcol = sb("rcol", [128, 1])
        rb16 = sb("rb16", [128, 2], BF16)
        m1row = sb("m1row", [1, TW])
        borow = sb("borow", [1, 1024])
        s2row = sb("s2row", [1, 129])
        scal = sb("scal", [1, 8])
        mhrow = sb("mhrow", [1, 2])
        sw1row = sb("sw1row", [1, 128])
        w1qT = sb("w1qT", [128, 8, 4096], BF16)
        w2qT = sb("w2qT", [128, 32, 1024], BF16)

        PT0 = psum("PT0", [128, 1024])
        PT1 = psum("PT1", [128, 1024])
        PT2 = psum("PT2", [128, 1024])
        PT3 = psum("PT3", [128, 512])
        PT4 = psum("PT4", [128, 512])

        h_ld = stk.enter_context(nc.semaphore("s_ld"))
        h_st = stk.enter_context(nc.semaphore("s_st"))
        h_pe = stk.enter_context(nc.semaphore("s_pe"))
        h_ac = stk.enter_context(nc.semaphore("s_ac"))
        h_dv = stk.enter_context(nc.semaphore("s_dv"))
        h_l2 = stk.enter_context(nc.semaphore("s_l2"))

        ten = {}
        with contextlib.ExitStack() as sA:
            with contextlib.ExitStack() as sWQ:
                ten["big0"] = SBT(sWQ, "big0", [128, 4096])
                ten["c1b"] = SBT(sWQ, "c1b", [128, 1024])
                ten["c2b"] = SBT(sWQ, "c2b", [128, 1024])
            ten["wb"] = SBT(sA, "wb", [128, 8, 1024], BF16)
            ten["wld"] = SBT(sA, "wld", [128, 1024])
            ten["xtf"] = SBT(sA, "xtf", [128, 8, 384])
            ten["xts0"] = SBT(sA, "xts0", [128, 8, 384], BF16)
            ten["xts1"] = SBT(sA, "xts1", [128, 8, 384], BF16)
            for i in range(2):
                ten[f"stg{i}"] = SBT(sA, f"stg{i}", [128, 512], BF16)
            ten["bqr"] = SBT(sA, "bqr", [128, 8])
            ten["bkr"] = SBT(sA, "bkr", [128, 8])
            ten["bvr"] = SBT(sA, "bvr", [128, 8])
        with contextlib.ExitStack() as sBC:
            ten["wob"] = SBT(sBC, "wob", [128, 8, 1024], BF16)
            with contextlib.ExitStack() as sB:
                with contextlib.ExitStack() as sWo:
                    ten["wotmp"] = SBT(sWo, "wotmp", [128, 1024])
                ten["Ebuf0"] = SBT(sB, "Ebuf0", [128, 4096], BF16)
                ten["Ebuf1"] = SBT(sB, "Ebuf1", [128, 4096], BF16)
                ten["Ebuf2"] = SBT(sB, "Ebuf2", [128, 4096], BF16)
                ten["K2"] = SBT(sB, "K2", [128, 4096], BF16)
                ten["Qh0"] = SBT(sB, "Qh0", [128, 8, 256], BF16)
                ten["Qh1"] = SBT(sB, "Qh1", [128, 8, 256], BF16)
                ten["ossb"] = SBT(sB, "ossb", [128, 1024])
            with contextlib.ExitStack() as sC:
                ten["osrow"] = SBT(sC, "osrow", [1, 4096])
                ten["Xt0"] = SBT(sC, "Xt0", [128, 1024])
                ten["lnin"] = SBT(sC, "lnin", [128, 1024])
                ten["utile"] = SBT(sC, "utile", [128, 1024])
                ten["qsc"] = SBT(sC, "qsc", [128, 1024])
                ten["zich"] = SBT(sC, "zich", [128, 512], I32)
                ten["Vt"] = SBT(sC, "Vt", [128, 8, 128], BF16)
                ten["Ot"] = SBT(sC, "Ot", [128, 8, 128], BF16)
                ten["xqtm"] = SBT(sC, "xqtm", [128, 1024])
                ten["xqTt"] = SBT(sC, "xqTt", [128, 8, 128], BF16)
        with contextlib.ExitStack() as sD:
            ten["zti"] = SBT(sD, "zti", [128, 32, 130])
            ten["htile"] = SBT(sD, "htile", [128, 32, 128])
            ten["x2q"] = SBT(sD, "x2q", [128, 16, 128], BF16)
            ten["x2qb"] = SBT(sD, "x2qb", [128, 16, 128], BF16)
            ten["zi16"] = SBT(sD, "zi16", [128, 16, 128], I32)
            ten["uld"] = SBT(sD, "uld", [128, 1024])
            ten["ypre"] = SBT(sD, "ypre", [128, 1024])
            ten["xqt0"] = SBT(sD, "xqt0", [128, 8, 132], BF16)
            ten["xqt1"] = SBT(sD, "xqt1", [128, 8, 132], BF16)
            ten["xqth"] = SBT(sD, "xqth", [128, 8, 1], BF16)
            ten["m12sb"] = SBT(sD, "m12sb", [128, 129])
            ten["s2sb"] = SBT(sD, "s2sb", [128, 128])
            ten["Mp"] = SBT(sD, "Mp", [128, 128])
            ten["mhsb"] = SBT(sD, "mhsb", [128, 1])
            ten["zedge"] = SBT(sD, "zedge", [128, 32, 1])
            ten["est"] = SBT(sD, "est", [128, 8])

        class T:
            pass
        for k, v in ten.items():
            setattr(T, k, v)

        def program(em):
            LD = Ctr(h_ld); ST = Ctr(h_st)
            PE = Ctr(h_pe); AC = Ctr(h_ac); DV = Ctr(h_dv)
            L2 = Ctr(h_l2)
            em.ctrs = {"tensor": PE, "vector": DV, "scalar": AC, "sync": None,
                       "gpsimd": None}

            def barrier():
                for eng in ("sync", "tensor", "vector", "scalar", "gpsimd"):
                    with em.on(eng):
                        for c in (LD, ST, PE, AC, DV, L2):
                            em.w(c, c.n)

            em.mute = "A" not in phases
            with em.on("sync"):
                for dst, src in ((ones, d_ones), (onesw, d_onesw),
                                 (ident, d_ident),
                                 (borow, d_borow), (cwa, d_cwa), (cwb, d_cwb),
                                 (cwc, d_cwc), (cb32, d_cb), (neghalf, d_nh),
                                 (masklr, d_mask)):
                    em.d(LD, lambda e, dst=dst, src=src: e.dma_start(dst[...], src))
            consts_ld = LD.n

            def wq_quant(dram, si, sw, thr, nthr, dstq, swcol=None):
                big0 = T.big0; c1b = T.c1b; c2b = T.c2b
                halves = [(big0[:, 0:2048], LD, "sync"),
                          (big0[:, 2048:4096], L2, "gpsimd")]

                def load_half(j):
                    buf, _, _ = halves[j % 2]
                    if dram is d_w1T:
                        return lambda e, j=j, buf=buf: e.dma_start(
                            buf, dram[:, j // 2,
                                      2048 * (j % 2):2048 * (j % 2) + 2048])
                    return lambda e, j=j, buf=buf: e.dma_start(
                        buf.rearrange("p (a b) -> p a b", a=2),
                        dram[:, 2 * j:2 * (j + 1), :])

                with em.on("vector"):
                    em.fence()
                    em.i(lambda e: e.memset(wsum[...], 0.0))
                buf_dv = [DV.n, DV.n]
                for j in range(16):
                    buf, ctr, eng = halves[j % 2]
                    with em.on(eng):
                        em.w(DV, buf_dv[j % 2])
                        em.d(ctr, load_half(j))
                    with em.on("vector"):
                        em.w(ctr, ctr.n)
                        em.fence()
                        em.i(lambda e, buf=buf: e.tensor_reduce(
                            c1b[:, 0:1], buf, AX.X, OP.add,
                            apply_absolute_value=True))
                        em.fence()
                        em.i(lambda e: e.tensor_add(wsum[...], wsum[...],
                                                    c1b[:, 0:1]))
                    buf_dv[j % 2] = DV.n
                with em.on("tensor"):
                    em.w(DV, DV.n)
                    em.i(lambda e: e.matmul(PT3[0:1, si:si + 1], ones[:, 0:1],
                                            wsum[...], start=True, stop=True))
                with em.on("vector"):
                    em.w(PE, PE.n)
                    em.fence()
                    em.i(lambda e: e.tensor_scalar(scal[0:1, si:si + 1],
                                                   PT3[0:1, si:si + 1],
                                                   1.0 / (4096.0 * 1024.0), 1e-5,
                                                   OP.mult, OP.max))
                    em.fence()
                    em.i(lambda e: e.tensor_copy(sw[...], scal[0:1, si:si + 1]))
                    em.fence()
                    em.i(lambda e: e.tensor_scalar(scal[0:1, si + 1:si + 2],
                                                   scal[0:1, si:si + 1], 0.5, None,
                                                   OP.mult))
                with em.on("tensor"):
                    em.w(DV, DV.n)
                    em.i(lambda e: e.matmul(PT3[:, 8 + si:9 + si], ones[0:1, :],
                                            scal[0:1, si + 1:si + 2],
                                            start=True, stop=True))
                with em.on("vector"):
                    em.w(PE, PE.n)
                    em.fence()
                    em.i(lambda e: e.tensor_copy(thr[...], PT3[:, 8 + si:9 + si]))
                    em.fence()
                    em.i(lambda e: e.tensor_scalar(nthr[...], thr[...], -1.0, None,
                                                   OP.mult))
                    if swcol is not None:
                        em.fence()
                        em.i(lambda e: e.tensor_scalar(swcol[...], thr[...], 2.0,
                                                       None, OP.mult))
                for j in range(16):
                    buf, ctr, eng = halves[j % 2]
                    with em.on(eng):
                        em.w(DV, buf_dv[j % 2])
                        em.d(ctr, load_half(j))
                    with em.on("vector"):
                        em.w(ctr, ctr.n)
                        for qq in range(2):
                            sl = slice(1024 * qq, 1024 * (qq + 1))
                            em.fence()
                            em.i(lambda e, sl=sl, buf=buf: e.tensor_scalar(
                                c1b[...], buf[:, sl], thr[...], None, OP.is_gt))
                            em.fence()
                            em.i(lambda e, sl=sl, buf=buf: e.tensor_scalar(
                                c2b[...], buf[:, sl], nthr[...], None, OP.is_lt))
                            em.fence()
                            if dstq is w1qT:
                                off = 2048 * (j % 2) + 1024 * qq
                                em.i(lambda e, j=j, off=off: e.tensor_sub(
                                    w1qT[:, j // 2, off:off + 1024],
                                    c1b[...], c2b[...]))
                            else:
                                g = 2 * j + qq
                                em.i(lambda e, g=g: e.tensor_sub(
                                    w2qT[...].rearrange("p a b -> p (a b)")
                                    [:, 1024 * g:1024 * (g + 1)],
                                    c1b[...], c2b[...]))
                    buf_dv[j % 2] = DV.n

            wq_quant(d_w1T, 0, sw1, thr1, nthr1, w1qT)
            wq_quant(d_w2T, 2, sw2, thr2, nthr2, w2qT, swcol=sw2col)
            with em.on("vector"):
                em.fence()
                em.i(lambda e: e.tensor_scalar(sw1row[...], ones[0:1, :], sw1[...],
                                               None, OP.mult))
            barrier()

            for dsrc, dst, scale in ((d_bq, T.bqr, INV32), (d_bk, T.bkr, INV32),
                                     (d_bv, T.bvr, 1.0)):
                with em.on("sync"):
                    em.w(DV, DV.n)
                    em.d(LD, lambda e, dsrc=dsrc, dst=dst:
                         e.dma_start(dst[...], dsrc))
                with em.on("vector"):
                    em.w(LD, LD.n)
                    em.fence()
                    em.i(lambda e, dst=dst, scale=scale:
                         e.tensor_scalar(dst[...], dst[...], scale, None, OP.mult))

            nslab = TW // 128
            psl = [PT0[:, 0:512], PT0[:, 512:1024], PT1[:, 0:512], PT1[:, 512:1024],
                   PT2[:, 0:512], PT2[:, 512:1024]]
            psl_free = [0] * 6
            stg_free = [0] * 2
            slot_i = 0
            stg_i = 0
            stgs = [ten["stg0"], ten["stg1"]]
            xtss = [T.xts0, T.xts1]
            for (dwsrc, bias, dscr, scale) in (
                    (d_wq, T.bqr, d_qs, INV32), (d_wk, T.bkr, d_ks, INV32),
                    (d_wv, T.bvr, d_vs, 1.0)):
                for kch in range(8):
                    with em.on("sync"):
                        em.w(DV, DV.n)
                        em.w(PE, PE.n)
                        em.d(LD, lambda e, dwsrc=dwsrc, kch=kch:
                             e.dma_start(T.wld[...], dwsrc[:, kch, :]))
                    with em.on("vector"):
                        em.w(LD, LD.n)
                        em.fence()
                        em.i(lambda e, kch=kch, scale=scale:
                             e.tensor_scalar(T.wb[:, kch, :], T.wld[...], scale,
                                             None, OP.mult))
                wcast = DV.n
                batches = [(384 * i, 384) for i in range(6)]
                slab_pe = [0] * len(batches)
                slab_cast = [0, 0]
                for si, (s0, wdt) in enumerate(batches):
                    xb = xtss[si % 2]
                    with em.on("gpsimd"):
                        em.w(DV, slab_cast[si % 2])
                        if si >= 1:
                            em.w(PE, slab_pe[si - 1])
                        em.d(L2, lambda e, s0=s0, wdt=wdt: e.dma_start(
                            T.xtf[:, :, 0:wdt], d_xT[:, :, s0:s0 + wdt]))
                    with em.on("vector"):
                        em.w(L2, L2.n)
                        em.fence()
                        em.i(lambda e, xb=xb, wdt=wdt: e.tensor_copy(
                            xb[:, :, 0:wdt], T.xtf[:, :, 0:wdt]))
                    slab_cast[si % 2] = DV.n
                    xcast = DV.n
                    for mch in range(8):
                        slot = slot_i % 6
                        slot_i += 1
                        pslot = psl[slot]
                        with em.on("tensor"):
                            em.w(DV, max(xcast, wcast))
                            em.w(AC, psl_free[slot])
                            for kch in range(8):
                                em.i(lambda e, pslot=pslot, kch=kch, mch=mch,
                                     xb=xb, wdt=wdt:
                                     e.matmul(pslot[:, 0:wdt],
                                              T.wb[:, kch, 128 * mch:128 * (mch + 1)],
                                              xb[:, kch, 0:wdt],
                                              start=(kch == 0), stop=(kch == 7)),
                                     inc=(1 if kch == 7 else 0))

                        sgi = stg_i % 2
                        sgb = stgs[sgi]
                        stg_i += 1
                        with em.on("scalar"):
                            em.w(PE, PE.n)
                            em.w(ST, stg_free[sgi])
                            em.fence()
                            em.i(lambda e, sgb=sgb, pslot=pslot, wdt=wdt,
                                 bias=bias, mch=mch:
                                 e.activation(sgb[:, 0:wdt], pslot[:, 0:wdt],
                                              ACTF.Identity,
                                              bias=bias[:, mch:mch + 1]))
                        psl_free[slot] = AC.n
                        with em.on("sync"):
                            em.w(AC, AC.n)
                            em.d(ST, lambda e, sgb=sgb, dscr=dscr, mch=mch,
                                 s0=s0, wdt=wdt:
                                 e.dma_start(dscr[:, mch, s0:s0 + wdt],
                                             sgb[:, 0:wdt]))
                        stg_free[sgi] = ST.n
                    slab_pe[si] = PE.n
            barrier()
            for kch in range(8):
                with em.on("sync"):
                    em.w(DV, DV.n)
                    em.d(LD, lambda e, kch=kch: e.dma_start(T.wotmp[...],
                                                            d_wo[:, kch, :]))
                with em.on("vector"):
                    em.w(LD, LD.n)
                    em.fence()
                    em.i(lambda e, kch=kch: e.tensor_copy(T.wob[:, kch, :],
                                                          T.wotmp[...]))
            barrier()
            em.mute = False
            if "B" not in phases:
                return
            bi = phases.index("B")
            BL = int(phases[bi + 1]) if (bi + 1 < len(phases)
                                         and phases[bi + 1].isdigit()) else 4

            slot_ps = [PT0, PT1, PT2]
            slot_free_ac = [0, 0, 0]
            ebuf_free_pe = [0, 0, 0]
            zq4_free_dv = [0, 0]
            mv_evac_dv = DV.n
            Ebufs = [T.Ebuf0, T.Ebuf1, T.Ebuf2]
            Qhs = [T.Qh0, T.Qh1]
            osst_st = [0] * NHEAD
            prev_mv_pe = 0
            prev_r_dv = 0
            prev_osst = 0
            head_score_pe = 0
            def k2_load_half(hh, half):
                t0 = 256 * hh
                if half == 0:
                    em.d(LD, lambda e, t0=t0: e.dma_start(
                        T.K2[0:64, 0:2048], d_ks[0:64, :, t0:t0 + 256]))
                    em.d(LD, lambda e, t0=t0: e.dma_start(
                        T.K2[64:128, 0:2048], d_ks[0:64, :, t0:t0 + 256]))
                    em.d(LD, lambda e, t0=t0, hh=hh: e.dma_start(
                        Qhs[hh % 2][...], d_qs[:, :, t0:t0 + 256]))
                else:
                    em.d(LD, lambda e, t0=t0: e.dma_start(
                        T.K2[0:64, 2048:4096], d_ks[64:128, :, t0:t0 + 256]))
                    em.d(LD, lambda e, t0=t0: e.dma_start(
                        T.K2[64:128, 2048:4096], d_ks[64:128, :, t0:t0 + 256]))

            if not noload:
                with em.on("sync"):
                    k2_load_half(0, 0)
            k2_ld = [LD.n, 0]
            for h in range(nheads):
                if not noload:
                    with em.on("sync"):
                        em.w(PE, head_score_pe if h >= 1 else PE.n)
                        k2_load_half(h, 1)
                k2_ld[1] = LD.n
                qh = Qhs[h % 2]

                def emit_chain(jj, dv_gate):
                    ebc = Ebufs[jj % 3]
                    with em.on("tensor"):
                        em.w(DV, dv_gate)
                        if jj == 0:
                            em.w(DV, mv_evac_dv)
                        for q in range(8):
                            pt = PT3 if q < 4 else PT4
                            em.i(lambda e, pt=pt, q=q, ebc=ebc, jj=jj:
                                 e.matmul(pt[32 * (q % 4):32 * (q % 4) + 1,
                                             0:512],
                                          rb16[:, jj % 2:jj % 2 + 1],
                                          ebc[:, 512 * q:512 * (q + 1)],
                                          start=(jj == 0), stop=(jj == 31),
                                          tile_position=(0, 32 * (q % 4))),
                                 inc=(1 if q == 7 else 0))
                    return PE.n

                r_dv_arr = [0] * 32
                for j in range(32 if BL >= 2 else 0):
                    jslots = [(4 * j + g) % 3 for g in range(4)]
                    order = [0, 1, 2, 3] if j % 2 == 0 else [2, 3, 0, 1]
                    first_pos = min(order.index(0), order.index(3))
                    later_g = order[max(order.index(0), order.index(3))]
                    mm_pe = {}
                    ac_start = AC.n

                    def score_mm(g):
                        s = jslots[g]
                        em.w(AC, slot_free_ac[s])
                        if g == later_g:
                            em.w(AC, ac_start + first_pos + 1)
                        par = 0 if g < 2 else 64
                        tp = (0, 0) if g < 2 else (64, 0)
                        base = 1024 * (g % 2)
                        for half in range(2):
                            em.i(lambda e, s=s, half=half, par=par, tp=tp,
                                 base=base, j=j, qh=qh:
                                 e.matmul(
                                     slot_ps[s][:, 512 * half:512 * (half + 1)],
                                     T.K2[par:par + 64, 128 * j:128 * (j + 1)],
                                     qh[par:par + 64, :, :]
                                     .rearrange("p a b -> p (a b)")
                                     [:, base + 512 * half:base + 512 * (half + 1)],
                                     start=True, stop=True, tile_position=tp),
                                 inc=(1 if half == 1 else 0))
                        mm_pe[g] = PE.n

                    with em.on("tensor"):
                        em.w(LD, k2_ld[0] if j < 16 else k2_ld[1])
                        score_mm(order[0])
                        score_mm(order[1])
                    if BL >= 3 and j >= 2:
                        prev_mv_pe = emit_chain(j - 2, r_dv_arr[j - 2])
                        ebuf_free_pe[(j - 2) % 3] = prev_mv_pe
                    with em.on("tensor"):
                        score_mm(order[2])
                        score_mm(order[3])
                    if j == 15 and not noload and h + 1 < nheads:
                        with em.on("sync"):
                            em.w(PE, mm_pe[order[3]])
                            k2_load_half(h + 1, 0)
                        k2_ld[0] = LD.n
                    eb = Ebufs[j % 3]
                    zo = 4 * (j % 2)
                    with em.on("scalar"):
                        em.w(PE, ebuf_free_pe[j % 3])
                        em.w(DV, zq4_free_dv[j % 2])
                        for g in order:
                            em.w(PE, mm_pe[g])
                            kw = ({"accum_out": zq4[:, zo + g:zo + g + 1]}
                                  if g < 2 else {})
                            em.i(lambda e, g=g, eb=eb, s=jslots[g], kw=kw:
                                 e.activation(eb[:, 1024 * g:1024 * (g + 1)],
                                              slot_ps[s][:, :], ACTF.Exp, **kw))
                            slot_free_ac[jslots[g]] = AC.n
                    with em.on("vector"):
                        em.w(AC, AC.n)
                        em.w(PE, prev_mv_pe)
                        for g in range(2, 4):
                            em.fence()
                            em.i(lambda e, g=g, eb=eb, zo=zo: e.tensor_reduce(
                                zq4[:, zo + g:zo + g + 1],
                                eb[:, 1024 * g:1024 * (g + 1)], AX.X, OP.add))
                        em.fence()
                        em.i(lambda e, zo=zo: e.tensor_add(
                            Zcol[...], zq4[:, zo:zo + 1], zq4[:, zo + 1:zo + 2]))
                        em.fence()
                        em.i(lambda e, zo=zo: e.tensor_add(
                            Zcol[...], Zcol[...], zq4[:, zo + 2:zo + 3]))
                        em.fence()
                        em.i(lambda e, zo=zo: e.tensor_add(
                            Zcol[...], Zcol[...], zq4[:, zo + 3:zo + 4]))
                        em.fence()
                        em.i(lambda e: e.reciprocal(rcol[...], Zcol[...]))
                        em.fence()
                        em.i(lambda e, j=j: e.tensor_copy(
                            rb16[:, j % 2:j % 2 + 1], rcol[...]))
                    prev_r_dv = DV.n
                    r_dv_arr[j] = DV.n
                    zq4_free_dv[j % 2] = DV.n
                    if j == 31:
                        head_score_pe = mm_pe[order[3]]
                if BL >= 3:
                    prev_mv_pe = emit_chain(30, r_dv_arr[30])
                    ebuf_free_pe[30 % 3] = prev_mv_pe
                    prev_mv_pe = emit_chain(31, r_dv_arr[31])
                    ebuf_free_pe[31 % 3] = prev_mv_pe
                if BL >= 4:
                    with em.on("vector"):
                        em.w(PE, PE.n)
                        em.w(ST, prev_osst)
                        for q in range(8):
                            pt = PT3 if q < 4 else PT4
                            em.fence()
                            em.i(lambda e, pt=pt, q=q: e.tensor_copy(
                                T.ossb[32 * (q % 4):32 * (q % 4) + 1,
                                       512 * (q // 4):512 * (q // 4) + 512],
                                pt[32 * (q % 4):32 * (q % 4) + 1, 0:512]))
                    mv_evac_dv = DV.n
                    with em.on("sync"):
                        em.w(DV, mv_evac_dv)
                        for q in range(8):
                            em.d(ST, lambda e, h=h, q=q: e.dma_start(
                                d_oss[h, 0:1, 512 * q:512 * (q + 1)],
                                T.ossb[32 * (q % 4):32 * (q % 4) + 1,
                                       512 * (q // 4):512 * (q // 4) + 512]))
                osst_st[h] = ST.n
                prev_osst = ST.n
            barrier()
            if "C" not in phases:
                return

            wo_free_dv = [DV.n, DV.n]
            wt_slot_free = [0, 0]
            prev_vsc = 0
            prev_lnin = [0, 0]
            prev_sq_ac = 0
            prev_m1tr = 0
            prev_ust = 0
            prev_xqst = 0
            prev_wbc = 0
            osld = 0
            for t in range(NT):
                h = t // 2
                hb = t % 2
                if hb == 0:
                    with em.on("sync"):
                        em.w(ST, osst_st[h])
                        em.w(PE, prev_wbc)
                        em.d(LD, lambda e, h=h: e.dma_start(T.osrow[...],
                                                            d_oss[h, :, :]))
                    osld = LD.n
                with em.on("gpsimd"):
                    em.w(DV, max(prev_vsc, prev_lnin[0]))
                    em.d(L2, lambda e, t=t: e.dma_start(
                        T.Vt[...], d_vs[:, :, 128 * t:128 * (t + 1)]))
                    em.d(L2, lambda e, t=t: e.dma_start(T.Xt0[...],
                                                        d_xtm[t, :, :]))
                vx_ld = L2.n
                os3 = T.osrow[0:1, :].rearrange("o (q c) -> o q c", c=256)
                with em.on("tensor"):
                    em.w(LD, osld)
                    em.w(DV, prev_vsc)
                    for b4 in range(2):
                        em.i(lambda e, b4=b4, hb=hb, os3=os3:
                             e.matmul(PT2[0:64, 512 * b4:512 * b4 + 512],
                                      ones[0:1, 0:64],
                                      os3[0:1, 4 * b4:4 * b4 + 4,
                                          128 * hb:128 * hb + 128],
                                      start=True, stop=True,
                                      tile_position=(0, 0)),
                             inc=0)
                    for b4 in range(2):
                        em.i(lambda e, b4=b4, hb=hb, os3=os3:
                             e.matmul(PT2[64:128, 512 * b4:512 * b4 + 512],
                                      ones[0:1, 0:64],
                                      os3[0:1, 8 + 4 * b4:8 + 4 * b4 + 4,
                                          128 * hb:128 * hb + 128],
                                      start=True, stop=True,
                                      tile_position=(0, 64)),
                             inc=(1 if b4 == 1 else 0))
                with em.on("vector"):
                    em.w(PE, PE.n)
                    em.w(L2, vx_ld)
                    em.fence()
                    em.i(lambda e: e.tensor_mul(
                        T.Ot[...], T.Vt[...],
                        PT2[:, 0:1024].rearrange("p (a b) -> p a b", b=128)))
                prev_wbc = PE.n
                prev_vsc = DV.n
                wps = [PT0, PT1][t % 2]
                with em.on("tensor"):
                    em.w(DV, max(prev_vsc, wo_free_dv[t % 2]))
                    for half in range(2):
                        for kch in range(8):
                            em.i(lambda e, wps=wps, half=half, kch=kch:
                                 e.matmul(wps[:, 512 * half:512 * (half + 1)],
                                          T.Ot[:, kch, :],
                                          T.wob[:, kch, 512 * half:512 * (half + 1)],
                                          start=(kch == 0), stop=False),
                                 inc=0)
                        em.i(lambda e, wps=wps, half=half:
                             e.matmul(wps[:, 512 * half:512 * (half + 1)],
                                      ones[0:1, :],
                                      borow[0:1, 512 * half:512 * (half + 1)],
                                      start=False, stop=True),
                             inc=(1 if half == 1 else 0))
                wo_done = PE.n
                with em.on("vector"):
                    em.w(PE, wo_done)
                    em.w(AC, prev_sq_ac)
                    em.fence()
                    em.i(lambda e, wps=wps: e.tensor_add(
                        T.lnin[...], wps[...], T.Xt0[...]))
                wo_free_dv[t % 2] = DV.n
                prev_lnin[0] = DV.n
                lnin_dv = DV.n
                with em.on("scalar"):
                    em.w(DV, lnin_dv)
                    em.fence()
                    em.i(lambda e: e.activation(T.qsc[...], T.lnin[...], ACTF.Square,
                                                accum_out=stat[:, 1:2]))
                prev_sq_ac = AC.n
                with em.on("vector"):
                    em.w(AC, AC.n)
                    em.w(PE, prev_m1tr)
                    em.fence()
                    em.i(lambda e: e.tensor_scalar(T.qsc[...], T.lnin[...], 1.0,
                                                   None, OP.mult, OP.add,
                                                   accum_out=stat[:, 0:1]))
                    em.fence()
                    em.i(lambda e: e.tensor_scalar(stat[:, 2:3], stat[:, 0:1],
                                                   -1.0 / 1024.0, None, OP.mult))
                    em.fence()
                    em.i(lambda e: e.tensor_scalar(stat[:, 3:4], stat[:, 1:2],
                                                   1.0 / 1024.0, EPS, OP.mult,
                                                   OP.add))
                    em.fence()
                    em.i(lambda e: e.tensor_mul(stat[:, 4:5], stat[:, 2:3],
                                                stat[:, 2:3]))
                    em.fence()
                    em.i(lambda e: e.tensor_sub(stat[:, 3:4], stat[:, 3:4],
                                                stat[:, 4:5]))
                with em.on("scalar"):
                    em.w(DV, DV.n)
                    em.fence()
                    em.i(lambda e: e.activation(stat[:, 4:5], stat[:, 3:4], ACTF.Ln))
                    em.fence()
                    em.i(lambda e: e.activation(stat[:, 5:6], stat[:, 4:5],
                                                ACTF.Exp, scale=neghalf[...]))
                rstd_ac = AC.n
                with em.on("vector"):
                    em.w(AC, rstd_ac)
                    em.w(ST, prev_ust)
                    em.fence()
                    em.i(lambda e: e.tensor_scalar(T.utile[...], T.lnin[...],
                                                   stat[:, 2:3], stat[:, 5:6],
                                                   OP.add, OP.mult))
                    em.fence()
                    em.i(lambda e: e.tensor_reduce(stat[:, 6:7], T.utile[...],
                                                   AX.X, OP.max,
                                                   apply_absolute_value=True))
                    em.fence()
                    em.i(lambda e: e.tensor_scalar(stat[:, 6:7], stat[:, 6:7], 1e-5,
                                                   None, OP.max))
                    em.fence()
                    em.i(lambda e: e.reciprocal(stat[:, 7:8], stat[:, 6:7]))
                    em.fence()
                    em.i(lambda e: e.tensor_scalar(stat[:, 7:8], stat[:, 7:8],
                                                   127.0, None, OP.mult))
                    em.fence()
                    em.i(lambda e: e.tensor_scalar(stat[:, 6:7], stat[:, 6:7],
                                                   1.0 / 127.0, None, OP.mult))
                ufin_dv = DV.n
                with em.on("sync"):
                    em.w(DV, ufin_dv)
                    em.d(ST, lambda e, t=t: e.dma_start(d_us[t, :, :], T.utile[...]))
                prev_ust = ST.n
                with em.on("tensor"):
                    em.w(DV, ufin_dv)
                    em.i(lambda e: e.transpose(PT3[0:1, 0:128], stat[:, 6:7],
                                               ident[...]))
                prev_m1tr = PE.n
                with em.on("vector"):
                    em.w(PE, prev_m1tr)
                    em.fence()
                    em.i(lambda e, t=t: e.tensor_copy(
                        m1row[0:1, 128 * t:128 * (t + 1)], PT3[0:1, 0:128]))
                with em.on("vector"):
                    zv = T.qsc[:, 0:512]
                    cv = T.qsc[:, 512:1024]
                    for hf in range(2):
                        sl = slice(512 * hf, 512 * (hf + 1))
                        em.fence()
                        em.i(lambda e, sl=sl, zv=zv: e.tensor_scalar(
                            zv, T.utile[:, sl], stat[:, 7:8], 0.5,
                            OP.mult, OP.add))
                        em.fence()
                        em.i(lambda e, zv=zv: e.tensor_copy(T.zich[...], zv))
                        em.fence()
                        em.i(lambda e, zv=zv, cv=cv: e.tensor_tensor(
                            cv, T.zich[...], zv, OP.is_gt))
                        em.fence()
                        em.i(lambda e, sl=sl, cv=cv: e.tensor_sub(
                            T.xqtm[:, sl], T.zich[...], cv))
                q1_dv = DV.n
                with em.on("tensor"):
                    em.w(DV, q1_dv)
                    for ch in range(8):
                        em.i(lambda e, ch=ch, wps=wps: e.transpose(
                            wps[:, 128 * ch:128 * (ch + 1)],
                            T.xqtm[:, 128 * ch:128 * (ch + 1)], ident[...]))
                xqtr_pe = PE.n
                with em.on("vector"):
                    em.w(PE, xqtr_pe)
                    em.w(ST, prev_xqst)
                    em.fence()
                    em.i(lambda e, wps=wps: e.tensor_copy(
                        T.xqTt[...],
                        wps[...].rearrange("p (a b) -> p a b", b=128)))
                wo_free_dv[t % 2] = DV.n
                with em.on("sync"):
                    em.w(DV, DV.n)
                    em.d(ST, lambda e, t=t: e.dma_start(d_xqs[t, :, :, :],
                                                        T.xqTt[...]))
                prev_xqst = ST.n
            barrier()
            if "D" not in phases:
                return

            with em.on("scalar"):
                em.i(lambda e: _load_act_set(e, 18), inc=0)
            xqts = [T.xqt0, T.xqt1]
            with em.on("vector"):
                em.fence()
                em.i(lambda e: e.tensor_copy(mhrow[0:1, 0:1],
                                             m1row[0:1, 2303:2304]))
                em.fence()
                em.i(lambda e: e.tensor_scalar(mhrow[0:1, 0:1], mhrow[0:1, 0:1],
                                               sw1[...], None, OP.mult))
            m12h_dv = DV.n
            with em.on("sync"):
                em.d(LD, lambda e: e.dma_start(T.xqt0[:, :, 0:128],
                                               d_xqs[17, :, :, :]))
            halo_ld = LD.n
            with em.on("vector"):
                em.w(LD, halo_ld)
                em.fence()
                em.i(lambda e: e.tensor_copy(T.xqth[...], T.xqt0[:, :, 127:128]))
            halo_dv = DV.n
            with em.on("tensor"):
                em.w(DV, halo_dv)
                for och in range(32):
                    for kch in range(8):
                        em.i(lambda e, och=och, kch=kch:
                             e.matmul(PT4[:, och:och + 1],
                                      w1qT[:, kch, 128 * och:128 * (och + 1)],
                                      T.xqth[:, kch, 0:1],
                                      start=(kch == 0), stop=(kch == 7)),
                             inc=(1 if (och == 31 and kch == 7) else 0))
            with em.on("tensor"):
                em.w(DV, m12h_dv)
                em.i(lambda e: e.matmul(PT3[:, 0:1], ones[0:1, :], mhrow[0:1, 0:1],
                                        start=True, stop=True))
            mh_pe = PE.n
            with em.on("vector"):
                em.w(PE, mh_pe)
                em.fence()
                em.i(lambda e: e.tensor_copy(T.mhsb[...], PT3[:, 0:1]))
                em.fence()
                em.i(lambda e: e.tensor_mul(
                    T.zedge[...],
                    PT4[:, 0:32].rearrange("p (a b) -> p a b", b=1),
                    T.mhsb[...].rearrange("p (o b) -> p o b", o=1)
                    .broadcast_to([128, 32, 1])))
                em.fence()
                em.i(lambda e: e.tensor_mul(
                    T.zedge[...], T.zedge[...],
                    masklr[:, 0:1].rearrange("p (o b) -> p o b", o=1)
                    .broadcast_to([128, 32, 1])))
            zedge_dv = DV.n

            f1_slot_free = zedge_dv
            f2_free_dv = DV.n
            htile_free_dv = 0
            ldx_pe_wait = 0
            yst_st = {}
            uld_free_dv = 0
            q2_pe = [0, 0]
            prev_small = DV.n
            fused_sq_ac = 0
            ogroups = [(0, 3), (3, 6), (6, 9), (9, 12), (12, 15), (15, 18),
                       (18, 21), (21, 24), (24, 27), (27, 30), (30, 32)]
            with em.on("sync"):
                em.w(DV, halo_dv)
                em.d(LD, lambda e: e.dma_start(xqts[0][:, :, 0:128],
                                               d_xqs[0, :, :, :]))
            xq_ld_t = [LD.n] + [0] * NT
            for t in range(NOWN):
                with em.on("sync"):
                    if t >= 1:
                        em.w(PE, ldx_pe_wait)
                    em.w(DV, uld_free_dv)
                    em.d(LD, lambda e, t=t: e.dma_start(
                        xqts[(t + 1) % 2][:, :, 0:128], d_xqs[t + 1, :, :, :]))
                    em.d(LD, lambda e, t=t: e.dma_start(T.uld[...], d_us[t, :, :]))
                xq_ld_t[t + 1] = LD.n
                xq_ld = LD.n
                with em.on("vector"):
                    em.w(LD, xq_ld)
                    em.fence()
                    em.i(lambda e, t=t: e.tensor_copy(
                        xqts[t % 2][:, :, 128:129], xqts[(t + 1) % 2][:, :, 0:1]))
                xcol_dv = DV.n
                with em.on("tensor"):
                    em.w(DV, max(m12h_dv, prev_small))
                    em.i(lambda e, t=t: e.matmul(PT2[:, 0:129], sw1row[...],
                                                 m1row[0:1, 128 * t:128 * t + 129],
                                                 start=True, stop=True))
                with em.on("vector"):
                    em.w(PE, PE.n)
                    em.fence()
                    em.i(lambda e: e.tensor_copy(T.m12sb[...], PT2[:, 0:129]))
                m12t_dv = DV.n
                with em.on("vector"):
                    em.w(AC, fused_sq_ac)
                    em.fence()
                    em.i(lambda e: e.tensor_copy(T.zti[:, :, 0:1], T.zedge[...]))
                def sweeps(p0, p1, pac):
                    with em.on("vector"):
                        em.w(AC, pac)
                        em.fence()
                        for och in range(p0, p1):
                            em.i(lambda e, och=och: e.scalar_tensor_tensor(
                                T.htile[:, och, :], T.zti[:, och, 0:128],
                                cwa[:, och:och + 1], T.htile[:, och, :],
                                OP.mult, OP.add))
                        em.fence()
                        for och in range(p0, p1):
                            em.i(lambda e, och=och: e.scalar_tensor_tensor(
                                T.htile[:, och, :], T.zti[:, och, 2:130],
                                cwc[:, och:och + 1], T.htile[:, och, :],
                                OP.mult, OP.add))

                prev_grp = None
                for (o0, o1) in ogroups:
                    ng = o1 - o0
                    with em.on("tensor"):
                        em.w(LD, xq_ld)
                        em.w(DV, max(f1_slot_free, xcol_dv))
                        for oo in range(ng):
                            och = o0 + oo
                            for kch in range(8):
                                em.i(lambda e, oo=oo, och=och, kch=kch, t=t:
                                     e.matmul(PT3[:, 129 * oo:129 * oo + 129],
                                              w1qT[:, kch, 128 * och:128 * (och + 1)],
                                              xqts[t % 2][:, kch, 0:129],
                                              start=(kch == 0), stop=(kch == 7)),
                                     inc=(1 if (oo == ng - 1 and kch == 7) else 0))
                    with em.on("vector"):
                        em.w(PE, PE.n)
                        em.w(DV, m12t_dv)
                        em.fence()
                        em.i(lambda e, o0=o0, ng=ng: e.tensor_mul(
                            T.zti[:, o0:o0 + ng, 1:130],
                            PT3[:, 0:129 * ng].rearrange("p (a b) -> p a b", b=129),
                            T.m12sb[...].rearrange("p (o b) -> p o b", o=1)
                            .broadcast_to([128, ng, 129])))
                        if t == NOWN - 1:
                            em.fence()
                            em.i(lambda e, o0=o0, ng=ng: e.tensor_mul(
                                T.zti[:, o0:o0 + ng, 129:130],
                                T.zti[:, o0:o0 + ng, 129:130],
                                masklr[:, 1:2].rearrange("p (o b) -> p o b", o=1)
                                .broadcast_to([128, ng, 1])))
                        em.fence()
                        em.i(lambda e, o0=o0, ng=ng: e.tensor_copy(
                            T.zedge[:, o0:o0 + ng, :],
                            T.zti[:, o0:o0 + ng, 128:129]))
                    f1_slot_free = DV.n
                    with em.on("scalar"):
                        em.w(DV, f1_slot_free)
                        if o0 == 0:
                            em.w(DV, htile_free_dv)
                        for och in range(o0, o1):
                            em.i(lambda e, och=och: e.activation(
                                T.htile[:, och, :], T.zti[:, och, 1:129],
                                ACTF.Identity, scale=cwb[:, och:och + 1],
                                bias=cb32[:, och:och + 1]))
                    if prev_grp is not None:
                        sweeps(*prev_grp)
                    prev_grp = (o0, o1, AC.n)
                ldx_pe_wait = PE.n
                sweeps(*prev_grp)
                conv_dv = DV.n
                with em.on("scalar"):
                    em.w(DV, conv_dv)
                    em.fence()
                    em.i(lambda e: e.activation(T.htile[...], T.htile[...],
                                                ACTF.Silu))
                silu_ac = AC.n
                with em.on("vector"):
                    em.w(AC, silu_ac)
                    em.fence()
                    em.i(lambda e: e.tensor_reduce(
                        T.Mp[...], T.htile[...].rearrange("p a b -> p b a"),
                        AX.X, OP.max, apply_absolute_value=True))
                mnmx_dv = DV.n
                with em.on("tensor"):
                    em.w(DV, mnmx_dv)
                    em.i(lambda e: e.transpose(PT4[0:128, 0:128], T.Mp[...],
                                               ident[...]))
                mtr_pe = PE.n
                with em.on("vector"):
                    em.w(PE, mtr_pe)
                    em.fence()
                    em.i(lambda e: e.tensor_reduce(q2st[:, 0:1], PT4[:, 0:128],
                                                   AX.X, OP.max))
                    em.fence()
                    em.i(lambda e: e.tensor_scalar(q2st[:, 0:1], q2st[:, 0:1],
                                                   1e-5, None, OP.max))
                    em.fence()
                    em.i(lambda e: e.reciprocal(q2st[:, 1:2], q2st[:, 0:1]))
                    em.fence()
                    em.i(lambda e: e.tensor_scalar(q2st[:, 1:2], q2st[:, 1:2],
                                                   127.0, None, OP.mult))
                    em.fence()
                    em.i(lambda e: e.tensor_scalar(q2st[:, 2:3], q2st[:, 0:1],
                                                   sw2col[...], None, OP.mult))
                    em.fence()
                    em.i(lambda e: e.tensor_scalar(q2st[:, 2:3], q2st[:, 2:3],
                                                   1.0 / 127.0, None, OP.mult))
                q2s_dv = DV.n
                with em.on("tensor"):
                    em.w(DV, q2s_dv)
                    em.i(lambda e: e.transpose(PT4[0:1, 256:384], q2st[:, 1:2],
                                               ident[...]))
                with em.on("vector"):
                    em.w(PE, PE.n)
                    em.fence()
                    em.i(lambda e: e.tensor_copy(s2row[0:1, 0:128],
                                                 PT4[0:1, 256:384]))
                with em.on("tensor"):
                    em.w(DV, DV.n)
                    em.i(lambda e: e.matmul(PT4[:, 384:512], ones[0:1, :],
                                            s2row[0:1, 0:128],
                                            start=True, stop=True))
                with em.on("vector"):
                    em.w(PE, PE.n)
                    em.fence()
                    em.i(lambda e: e.tensor_copy(T.s2sb[...], PT4[:, 384:512]))
                prev_small = DV.n
                fps = [PT0, PT1][t % 2]
                for hf in range(2):
                    x2b = [T.x2q, T.x2qb][hf]
                    xs = T.zti[:, 0:16, 1:129]
                    cs = T.zti[:, 16:32, 1:129]
                    with em.on("vector"):
                        em.w(PE, q2_pe[hf])
                        em.fence()
                        em.i(lambda e, hf=hf, xs=xs: e.tensor_mul(
                            xs, T.htile[:, 16 * hf:16 * (hf + 1), :],
                            T.s2sb[...].rearrange("p (o b) -> p o b", o=1)
                            .broadcast_to([128, 16, 128])))
                        em.fence()
                        em.i(lambda e, xs=xs: e.tensor_scalar(
                            xs, xs, 1.0, 0.5, OP.mult, OP.add))
                        em.fence()
                        em.i(lambda e, xs=xs: e.tensor_copy(T.zi16[...], xs))
                        em.fence()
                        em.i(lambda e, xs=xs, cs=cs: e.tensor_tensor(
                            cs, T.zi16[...], xs, OP.is_gt))
                        em.fence()
                        em.i(lambda e, x2b=x2b, cs=cs: e.tensor_sub(
                            x2b[...], T.zi16[...], cs))
                    x2q_dv = DV.n
                    with em.on("tensor"):
                        em.w(DV, max(x2q_dv, f2_free_dv))
                        for half in range(2):
                            for kk in range(16):
                                kch = 16 * hf + kk
                                em.i(lambda e, fps=fps, half=half, kk=kk, kch=kch,
                                     x2b=x2b:
                                     e.matmul(fps[:, 512 * half:512 * (half + 1)],
                                              x2b[:, kk, :],
                                              w2qT[:, kch, 512 * half:512 * (half + 1)],
                                              start=(kch == 0),
                                              stop=(kch == 31)),
                                     inc=(1 if (half == 1 and kk == 15) else 0))
                    q2_pe[hf] = PE.n
                f2_pe = PE.n
                htile_free_dv = DV.n
                with em.on("vector"):
                    em.w(PE, f2_pe)
                    em.w(LD, xq_ld)
                    em.w(ST, yst_st.get(t - 1, 0))
                    em.fence()
                    em.i(lambda e, fps=fps: e.tensor_scalar(
                        T.ypre[...], fps[...], q2st[:, 2:3], None, OP.mult))
                    em.fence()
                    em.i(lambda e: e.tensor_add(T.ypre[...], T.ypre[...],
                                                T.uld[...]))
                f2_free_dv = DV.n
                uld_free_dv = DV.n
                ypre_dv = DV.n
                sink = T.zti[:, 8:16, 1:129]
                with em.on("scalar"):
                    em.w(DV, ypre_dv)
                    em.i(lambda e: _load_act_set(e, 6), inc=0)
                    em.fence()
                    em.i(lambda e, sink=sink: e.activation(
                        sink, T.ypre[...], ACTF.Square,
                        accum_out=T.est[:, 1:2]))
                fused_sq_ac = AC.n
                with em.on("vector"):
                    em.w(AC, fused_sq_ac)
                    em.fence()
                    em.i(lambda e, sink=sink: e.tensor_scalar(
                        sink, T.ypre[...], 1.0, None, OP.mult, OP.add,
                        accum_out=T.est[:, 0:1]))
                    em.fence()
                    em.i(lambda e: e.tensor_scalar(T.est[:, 2:3], T.est[:, 0:1],
                                                   -1.0 / 1024.0, None, OP.mult))
                    em.fence()
                    em.i(lambda e: e.tensor_scalar(T.est[:, 3:4], T.est[:, 1:2],
                                                   1.0 / 1024.0, EPS, OP.mult,
                                                   OP.add))
                    em.fence()
                    em.i(lambda e: e.tensor_mul(T.est[:, 4:5], T.est[:, 2:3],
                                                T.est[:, 2:3]))
                    em.fence()
                    em.i(lambda e: e.tensor_sub(T.est[:, 3:4], T.est[:, 3:4],
                                                T.est[:, 4:5]))
                est_dv = DV.n
                with em.on("scalar"):
                    em.w(DV, est_dv)
                    em.fence()
                    em.i(lambda e: e.activation(T.est[:, 4:5], T.est[:, 3:4],
                                                ACTF.Ln))
                    em.fence()
                    em.i(lambda e: e.activation(T.est[:, 5:6], T.est[:, 4:5],
                                                ACTF.Exp, scale=neghalf[...]))
                    em.i(lambda e: _load_act_set(e, 18), inc=0)
                rs_ac = AC.n
                with em.on("vector"):
                    em.w(AC, rs_ac)
                    em.fence()
                    em.i(lambda e: e.tensor_scalar(T.ypre[...], T.ypre[...],
                                                   T.est[:, 2:3], T.est[:, 5:6],
                                                   OP.add, OP.mult))
                with em.on("sync"):
                    em.w(DV, DV.n)
                    em.d(ST, lambda e, t=t: e.dma_start(
                        d_out[128 * t:128 * (t + 1), :], T.ypre[...]))
                yst_st[t] = ST.n
            with em.on("sync"):
                em.w(ST, ST.n)
                em.w(L2, L2.n)

        with nc.Block() as blk:
            @blk.sync
            def _(e):
                program(Em("sync", e))

            @blk.tensor
            def _(e):
                program(Em("tensor", e))

            @blk.vector
            def _(e):
                program(Em("vector", e))

            @blk.scalar
            def _(e):
                _load_act_set(e, 6)
                program(Em("scalar", e))

            @blk.gpsimd
            def _(e):
                program(Em("gpsimd", e))
    return nc


_NC_CACHE = {}


def _get_program():
    if "nc" not in _NC_CACHE:
        _NC_CACHE["nc"] = build_program()
    return _NC_CACHE["nc"]


def _stage_core(x, c, consts):
    n, half = c // 2, c % 2
    own = x[n, 2048 * half:2048 * half + 2048]
    if half == 0:
        foreign = x[n, 2048:2048 + 256]
        maskL, maskR = 0.0, 1.0
    else:
        foreign = x[n, 2048 - 256:2048]
        maskL, maskR = 1.0, 0.0
    win = np.concatenate([own, foreign], axis=0)
    m = dict(consts)
    m["xw_tm"] = np.ascontiguousarray(win.reshape(NT, 128, 1024))
    m["xw_T"] = np.ascontiguousarray(win.T.reshape(8, 128, TW).transpose(1, 0, 2))
    m["maskLR"] = np.broadcast_to(np.array([maskL, maskR], np.float32),
                                  (128, 2)).copy()
    return m


def kernel(L0_attn_X_embed, L0_attn_Wq, L0_attn_bq, L0_attn_Wk, L0_attn_bk,
           L0_attn_Wv, L0_attn_bv, L0_attn_Wo, L0_attn_bo, L0_ffn_w1,
           L0_ffn_conv_w, L0_ffn_conv_b, L0_ffn_w2):
    x = np.ascontiguousarray(np.asarray(L0_attn_X_embed, np.float32))
    f32 = lambda a: np.ascontiguousarray(np.asarray(a, np.float32))

    def chunk_major(w):
        w = f32(w)
        return np.ascontiguousarray(w.reshape(8, 128, w.shape[1]).transpose(1, 0, 2))

    consts = {
        "wq": chunk_major(L0_attn_Wq), "wk": chunk_major(L0_attn_Wk),
        "wv": chunk_major(L0_attn_Wv), "wo": chunk_major(L0_attn_Wo),
        "w1T": chunk_major(f32(L0_ffn_w1).T),
        "w2T": np.ascontiguousarray(
            f32(L0_ffn_w2).T.reshape(32, 128, 1024).transpose(1, 0, 2)),
        "bq_col": np.ascontiguousarray(f32(L0_attn_bq).reshape(8, 128).T),
        "bk_col": np.ascontiguousarray(f32(L0_attn_bk).reshape(8, 128).T),
        "bv_col": np.ascontiguousarray(f32(L0_attn_bv).reshape(8, 128).T),
        "bo_row": f32(L0_attn_bo).reshape(1, 1024),
        "cw3a": np.ascontiguousarray(f32(L0_ffn_conv_w)[:, 0, 0].reshape(32, 128).T),
        "cw3b": np.ascontiguousarray(f32(L0_ffn_conv_w)[:, 0, 1].reshape(32, 128).T),
        "cw3c": np.ascontiguousarray(f32(L0_ffn_conv_w)[:, 0, 2].reshape(32, 128).T),
        "cb32": np.ascontiguousarray(f32(L0_ffn_conv_b).reshape(32, 128).T),
        "ones128": np.ones((128, 128), np.float32),
        "ones_row": np.ones((1, 384), np.float32),
        "ident": np.eye(128, dtype=np.float32),
        "neghalf": np.full((128, 1), -0.5, np.float32),
    }
    in_maps = [_stage_core(x, c, consts) for c in range(8)]
    nc = _get_program()
    res = run_bass_kernel_spmd(nc, in_maps, core_ids=list(range(8)))
    out = np.empty((4, 4096, 1024), np.float32)
    for c in range(8):
        n, half = c // 2, c % 2
        out[n, 2048 * half:2048 * half + 2048] = res.results[c]["out"]
    return out


# version 23
# speedup vs baseline: 1.1957x; 1.0066x over previous
import contextlib
import numpy as np
import ml_dtypes

import concourse.bass as bass
import concourse.mybir as mybir
from concourse.bass_utils import run_bass_kernel_spmd

F32 = mybir.dt.float32
BF16 = mybir.dt.bfloat16
I32 = mybir.dt.int32
AX = mybir.AxisListType
OP = mybir.AluOpType
ACTF = mybir.ActivationFunctionType

NT = 18
NOWN = 16
TW = NT * 128
NHEAD = 9
INV32 = 1.0 / 32.0
EPS = 1e-5


class Ctr:
    __slots__ = ("h", "n")
    def __init__(self, h):
        self.h = h
        self.n = 0


def _load_act_set(e, idx):
    return e.add_instruction(mybir.InstLoadActFuncSet(
        name=e.bass.get_next_instruction_name(),
        act_func_set_id=idx, ins=[], outs=[]))


class Em:
    def __init__(self, live, e):
        self.live = live
        self.e = e
        self.ctrs = None
        self.cur = None
        self.mute = False

    @contextlib.contextmanager
    def on(self, eng):
        prev = self.cur
        self.cur = eng
        yield
        self.cur = prev

    def w(self, ctr, val):
        if self.mute:
            return
        if self.cur == self.live and val > 0:
            self.e.wait_ge(ctr.h, val)

    def i(self, fn, inc=1):
        if self.mute:
            return
        c = self.ctrs[self.cur]
        if self.cur == self.live:
            inst = fn(self.e)
            if inc:
                inst.then_inc(c.h, inc)
        c.n += inc

    def d(self, ctr, fn):
        if self.mute:
            return
        if self.cur == self.live and ctr.n > 0:
            self.e.wait_ge(ctr.h, ctr.n)
        if self.cur == self.live:
            fn(self.e).then_inc(ctr.h, 16)
        ctr.n += 16

    def fence(self):
        c = self.ctrs[self.cur]
        self.w(c, c.n)


def build_program(phases="ABCDE", nheads=NHEAD, noload=False):
    nc = bass.Bass("TRN2", target_bir_lowering=False)

    d_xtm = nc.dram_tensor("xw_tm", [NT, 128, 1024], F32, kind="ExternalInput").ap()
    d_xT = nc.dram_tensor("xw_T", [128, 8, TW], F32, kind="ExternalInput").ap()
    d_wq = nc.dram_tensor("wq", [128, 8, 1024], F32, kind="ExternalInput").ap()
    d_wk = nc.dram_tensor("wk", [128, 8, 1024], F32, kind="ExternalInput").ap()
    d_wv = nc.dram_tensor("wv", [128, 8, 1024], F32, kind="ExternalInput").ap()
    d_wo = nc.dram_tensor("wo", [128, 8, 1024], F32, kind="ExternalInput").ap()
    d_w1T = nc.dram_tensor("w1T", [128, 8, 4096], F32, kind="ExternalInput").ap()
    d_w2T = nc.dram_tensor("w2T", [128, 32, 1024], F32, kind="ExternalInput").ap()
    d_bq = nc.dram_tensor("bq_col", [128, 8], F32, kind="ExternalInput").ap()
    d_bk = nc.dram_tensor("bk_col", [128, 8], F32, kind="ExternalInput").ap()
    d_bv = nc.dram_tensor("bv_col", [128, 8], F32, kind="ExternalInput").ap()
    d_borow = nc.dram_tensor("bo_row", [1, 1024], F32, kind="ExternalInput").ap()
    d_cwa = nc.dram_tensor("cw3a", [128, 32], F32, kind="ExternalInput").ap()
    d_cwb = nc.dram_tensor("cw3b", [128, 32], F32, kind="ExternalInput").ap()
    d_cwc = nc.dram_tensor("cw3c", [128, 32], F32, kind="ExternalInput").ap()
    d_cb = nc.dram_tensor("cb32", [128, 32], F32, kind="ExternalInput").ap()
    d_ones = nc.dram_tensor("ones128", [128, 128], F32, kind="ExternalInput").ap()
    d_onesw = nc.dram_tensor("ones_row", [1, 384], F32, kind="ExternalInput").ap()
    d_ident = nc.dram_tensor("ident", [128, 128], F32, kind="ExternalInput").ap()
    d_nh = nc.dram_tensor("neghalf", [128, 1], F32, kind="ExternalInput").ap()
    d_mask = nc.dram_tensor("maskLR", [128, 2], F32, kind="ExternalInput").ap()
    d_out = nc.dram_tensor("out", [NOWN * 128, 1024], F32, kind="ExternalOutput").ap()

    d_qs = nc.dram_tensor("q_scr", [128, 8, TW], BF16, kind="Internal").ap()
    d_ks = nc.dram_tensor("k_scr", [128, 8, TW], BF16, kind="Internal").ap()
    d_vs = nc.dram_tensor("v_scr", [128, 8, TW], BF16, kind="Internal").ap()
    d_us = nc.dram_tensor("u_scr", [NT, 128, 1024], F32, kind="Internal").ap()
    d_xqs = nc.dram_tensor("xqT_scr", [NT, 128, 8, 128], BF16, kind="Internal").ap()
    d_oss = nc.dram_tensor("os_scr", [NHEAD, 1, 4096], F32, kind="Internal").ap()
    d_ys = nc.dram_tensor("y_scr", [NOWN, 128, 1024], F32, kind="Internal").ap()

    with contextlib.ExitStack() as stk:
        SBT = lambda st, name, shape, dt=F32: st.enter_context(
            nc.sbuf_tensor(name, shape, dt))
        sb = lambda name, shape, dt=F32: SBT(stk, name, shape, dt)
        psum = lambda name, shape: stk.enter_context(nc.psum_tensor(name, shape, F32))

        ones = sb("ones", [128, 128])
        onesw = sb("onesw", [1, 384])
        ident = sb("ident_sb", [128, 128])
        cwa = sb("cwa", [128, 32]); cwb = sb("cwb", [128, 32])
        cwc = sb("cwc", [128, 32]); cb32 = sb("cb32sb", [128, 32])
        neghalf = sb("neghalfsb", [128, 1])
        masklr = sb("masklrsb", [128, 2])
        sw1 = sb("sw1", [1, 1]); sw2 = sb("sw2", [1, 1])
        sw2col = sb("sw2col", [128, 1])
        thr1 = sb("thr1", [128, 1]); nthr1 = sb("nthr1", [128, 1])
        thr2 = sb("thr2", [128, 1]); nthr2 = sb("nthr2", [128, 1])
        wsum = sb("wsum", [128, 1])
        stat = sb("stat", [128, 8])
        q2st = sb("q2st", [128, 4])
        zq4 = sb("zq4", [128, 8])
        Zcol = sb("Zcol", [128, 1])
        rcol = sb("rcol", [128, 1])
        rb16 = sb("rb16", [128, 2], BF16)
        m1row = sb("m1row", [1, TW])
        borow = sb("borow", [1, 1024])
        s2row = sb("s2row", [1, 129])
        scal = sb("scal", [1, 8])
        mhrow = sb("mhrow", [1, 2])
        sw1row = sb("sw1row", [1, 128])
        w1qT = sb("w1qT", [128, 8, 4096], BF16)
        w2qT = sb("w2qT", [128, 32, 1024], BF16)

        PT0 = psum("PT0", [128, 1024])
        PT1 = psum("PT1", [128, 1024])
        PT2 = psum("PT2", [128, 1024])
        PT3 = psum("PT3", [128, 512])
        PT4 = psum("PT4", [128, 512])

        h_ld = stk.enter_context(nc.semaphore("s_ld"))
        h_st = stk.enter_context(nc.semaphore("s_st"))
        h_pe = stk.enter_context(nc.semaphore("s_pe"))
        h_ac = stk.enter_context(nc.semaphore("s_ac"))
        h_dv = stk.enter_context(nc.semaphore("s_dv"))
        h_l2 = stk.enter_context(nc.semaphore("s_l2"))

        ten = {}
        with contextlib.ExitStack() as sA:
            with contextlib.ExitStack() as sWQ:
                ten["big0"] = SBT(sWQ, "big0", [128, 4096])
                ten["c1b"] = SBT(sWQ, "c1b", [128, 1024])
                ten["c2b"] = SBT(sWQ, "c2b", [128, 1024])
            ten["wb"] = SBT(sA, "wb", [128, 8, 1024], BF16)
            ten["wb2"] = SBT(sA, "wb2", [128, 8, 1024], BF16)
            ten["wld"] = SBT(sA, "wld", [128, 1024])
            ten["xtf"] = SBT(sA, "xtf", [128, 8, 384])
            ten["xts0"] = SBT(sA, "xts0", [128, 8, 384], BF16)
            ten["xts1"] = SBT(sA, "xts1", [128, 8, 384], BF16)
            for i in range(2):
                ten[f"stg{i}"] = SBT(sA, f"stg{i}", [128, 512], BF16)
            ten["bqr"] = SBT(sA, "bqr", [128, 8])
            ten["bkr"] = SBT(sA, "bkr", [128, 8])
            ten["bvr"] = SBT(sA, "bvr", [128, 8])
        with contextlib.ExitStack() as sBC:
            ten["wob"] = SBT(sBC, "wob", [128, 8, 1024], BF16)
            with contextlib.ExitStack() as sB:
                with contextlib.ExitStack() as sWo:
                    ten["wotmp"] = SBT(sWo, "wotmp", [128, 1024])
                ten["Ebuf0"] = SBT(sB, "Ebuf0", [128, 4096], BF16)
                ten["Ebuf1"] = SBT(sB, "Ebuf1", [128, 4096], BF16)
                ten["Ebuf2"] = SBT(sB, "Ebuf2", [128, 4096], BF16)
                ten["K2"] = SBT(sB, "K2", [128, 4096], BF16)
                ten["Qh0"] = SBT(sB, "Qh0", [128, 8, 256], BF16)
                ten["Qh1"] = SBT(sB, "Qh1", [128, 8, 256], BF16)
                ten["ossb"] = SBT(sB, "ossb", [128, 1024])
            with contextlib.ExitStack() as sC:
                ten["osrow"] = SBT(sC, "osrow", [1, 4096])
                ten["Xt0"] = SBT(sC, "Xt0", [128, 1024])
                ten["lnin"] = SBT(sC, "lnin", [128, 1024])
                ten["utile"] = SBT(sC, "utile", [128, 1024])
                ten["qsc"] = SBT(sC, "qsc", [128, 1024])
                ten["zich"] = SBT(sC, "zich", [128, 512], I32)
                ten["Vt"] = SBT(sC, "Vt", [128, 8, 128], BF16)
                ten["Ot"] = SBT(sC, "Ot", [128, 8, 128], BF16)
                ten["xqtm"] = SBT(sC, "xqtm", [128, 1024])
                ten["xqTt"] = SBT(sC, "xqTt", [128, 8, 128], BF16)
        with contextlib.ExitStack() as sD:
            ten["zti"] = SBT(sD, "zti", [128, 32, 130])
            ten["htile"] = SBT(sD, "htile", [128, 32, 128])
            ten["x2q"] = SBT(sD, "x2q", [128, 16, 128], BF16)
            ten["x2qb"] = SBT(sD, "x2qb", [128, 16, 128], BF16)
            ten["zi16"] = SBT(sD, "zi16", [128, 16, 128], I32)
            ten["uld"] = SBT(sD, "uld", [128, 1024])
            ten["ypre"] = SBT(sD, "ypre", [128, 1024])
            ten["xqt0"] = SBT(sD, "xqt0", [128, 8, 132], BF16)
            ten["xqt1"] = SBT(sD, "xqt1", [128, 8, 132], BF16)
            ten["xqth"] = SBT(sD, "xqth", [128, 8, 1], BF16)
            ten["m12sb"] = SBT(sD, "m12sb", [128, 129])
            ten["s2sb"] = SBT(sD, "s2sb", [128, 128])
            ten["Mp"] = SBT(sD, "Mp", [128, 128])
            ten["mhsb"] = SBT(sD, "mhsb", [128, 1])
            ten["zedge"] = SBT(sD, "zedge", [128, 32, 1])
            ten["est"] = SBT(sD, "est", [128, 8])

        class T:
            pass
        for k, v in ten.items():
            setattr(T, k, v)

        def program(em):
            LD = Ctr(h_ld); ST = Ctr(h_st)
            PE = Ctr(h_pe); AC = Ctr(h_ac); DV = Ctr(h_dv)
            L2 = Ctr(h_l2)
            em.ctrs = {"tensor": PE, "vector": DV, "scalar": AC, "sync": None,
                       "gpsimd": None}

            def barrier():
                for eng in ("sync", "tensor", "vector", "scalar", "gpsimd"):
                    with em.on(eng):
                        for c in (LD, ST, PE, AC, DV, L2):
                            em.w(c, c.n)

            em.mute = "A" not in phases
            with em.on("sync"):
                for dst, src in ((ones, d_ones), (onesw, d_onesw),
                                 (ident, d_ident),
                                 (borow, d_borow), (cwa, d_cwa), (cwb, d_cwb),
                                 (cwc, d_cwc), (cb32, d_cb), (neghalf, d_nh),
                                 (masklr, d_mask)):
                    em.d(LD, lambda e, dst=dst, src=src: e.dma_start(dst[...], src))
            consts_ld = LD.n

            def wq_quant(dram, si, sw, thr, nthr, dstq, swcol=None):
                big0 = T.big0; c1b = T.c1b; c2b = T.c2b
                halves = [(big0[:, 0:2048], LD, "sync"),
                          (big0[:, 2048:4096], L2, "gpsimd")]

                def load_half(j):
                    buf, _, _ = halves[j % 2]
                    if dram is d_w1T:
                        return lambda e, j=j, buf=buf: e.dma_start(
                            buf, dram[:, j // 2,
                                      2048 * (j % 2):2048 * (j % 2) + 2048])
                    return lambda e, j=j, buf=buf: e.dma_start(
                        buf.rearrange("p (a b) -> p a b", a=2),
                        dram[:, 2 * j:2 * (j + 1), :])

                with em.on("vector"):
                    em.fence()
                    em.i(lambda e: e.memset(wsum[...], 0.0))
                buf_dv = [DV.n, DV.n]
                for j in range(16):
                    buf, ctr, eng = halves[j % 2]
                    with em.on(eng):
                        em.w(DV, buf_dv[j % 2])
                        em.d(ctr, load_half(j))
                    with em.on("vector"):
                        em.w(ctr, ctr.n)
                        em.fence()
                        em.i(lambda e, buf=buf: e.tensor_reduce(
                            c1b[:, 0:1], buf, AX.X, OP.add,
                            apply_absolute_value=True))
                        em.fence()
                        em.i(lambda e: e.tensor_add(wsum[...], wsum[...],
                                                    c1b[:, 0:1]))
                    buf_dv[j % 2] = DV.n
                with em.on("tensor"):
                    em.w(DV, DV.n)
                    em.i(lambda e: e.matmul(PT3[0:1, si:si + 1], ones[:, 0:1],
                                            wsum[...], start=True, stop=True))
                with em.on("vector"):
                    em.w(PE, PE.n)
                    em.fence()
                    em.i(lambda e: e.tensor_scalar(scal[0:1, si:si + 1],
                                                   PT3[0:1, si:si + 1],
                                                   1.0 / (4096.0 * 1024.0), 1e-5,
                                                   OP.mult, OP.max))
                    em.fence()
                    em.i(lambda e: e.tensor_copy(sw[...], scal[0:1, si:si + 1]))
                    em.fence()
                    em.i(lambda e: e.tensor_scalar(scal[0:1, si + 1:si + 2],
                                                   scal[0:1, si:si + 1], 0.5, None,
                                                   OP.mult))
                with em.on("tensor"):
                    em.w(DV, DV.n)
                    em.i(lambda e: e.matmul(PT3[:, 8 + si:9 + si], ones[0:1, :],
                                            scal[0:1, si + 1:si + 2],
                                            start=True, stop=True))
                with em.on("vector"):
                    em.w(PE, PE.n)
                    em.fence()
                    em.i(lambda e: e.tensor_copy(thr[...], PT3[:, 8 + si:9 + si]))
                    em.fence()
                    em.i(lambda e: e.tensor_scalar(nthr[...], thr[...], -1.0, None,
                                                   OP.mult))
                    if swcol is not None:
                        em.fence()
                        em.i(lambda e: e.tensor_scalar(swcol[...], thr[...], 2.0,
                                                       None, OP.mult))
                for j in range(16):
                    buf, ctr, eng = halves[j % 2]
                    with em.on(eng):
                        em.w(DV, buf_dv[j % 2])
                        em.d(ctr, load_half(j))
                    with em.on("vector"):
                        em.w(ctr, ctr.n)
                        for qq in range(2):
                            sl = slice(1024 * qq, 1024 * (qq + 1))
                            em.fence()
                            em.i(lambda e, sl=sl, buf=buf: e.tensor_scalar(
                                c1b[...], buf[:, sl], thr[...], None, OP.is_gt))
                            em.fence()
                            em.i(lambda e, sl=sl, buf=buf: e.tensor_scalar(
                                c2b[...], buf[:, sl], nthr[...], None, OP.is_lt))
                            em.fence()
                            if dstq is w1qT:
                                off = 2048 * (j % 2) + 1024 * qq
                                em.i(lambda e, j=j, off=off: e.tensor_sub(
                                    w1qT[:, j // 2, off:off + 1024],
                                    c1b[...], c2b[...]))
                            else:
                                g = 2 * j + qq
                                em.i(lambda e, g=g: e.tensor_sub(
                                    w2qT[...].rearrange("p a b -> p (a b)")
                                    [:, 1024 * g:1024 * (g + 1)],
                                    c1b[...], c2b[...]))
                    buf_dv[j % 2] = DV.n

            wq_quant(d_w1T, 0, sw1, thr1, nthr1, w1qT)
            wq_quant(d_w2T, 2, sw2, thr2, nthr2, w2qT, swcol=sw2col)
            with em.on("vector"):
                em.fence()
                em.i(lambda e: e.tensor_scalar(sw1row[...], ones[0:1, :], sw1[...],
                                               None, OP.mult))
            barrier()

            for dsrc, dst, scale in ((d_bq, T.bqr, INV32), (d_bk, T.bkr, INV32),
                                     (d_bv, T.bvr, 1.0)):
                with em.on("sync"):
                    em.w(DV, DV.n)
                    em.d(LD, lambda e, dsrc=dsrc, dst=dst:
                         e.dma_start(dst[...], dsrc))
                with em.on("vector"):
                    em.w(LD, LD.n)
                    em.fence()
                    em.i(lambda e, dst=dst, scale=scale:
                         e.tensor_scalar(dst[...], dst[...], scale, None, OP.mult))

            nslab = TW // 128
            psl = [PT0[:, 0:512], PT0[:, 512:1024], PT1[:, 0:512], PT1[:, 512:1024],
                   PT2[:, 0:512], PT2[:, 512:1024]]
            psl_free = [0] * 6
            stg_free = [0] * 2
            slot_i = 0
            stg_i = 0
            stgs = [ten["stg0"], ten["stg1"]]
            xtss = [T.xts0, T.xts1]
            wbs = [T.wb, T.wb2]
            wqkv_pe = [0, 0]
            for wi, (dwsrc, bias, dscr, scale) in enumerate((
                    (d_wq, T.bqr, d_qs, INV32), (d_wk, T.bkr, d_ks, INV32),
                    (d_wv, T.bvr, d_vs, 1.0))):
                wbb = wbs[wi % 2]
                for kch in range(8):
                    with em.on("sync"):
                        em.w(DV, DV.n)
                        if kch == 0:
                            em.w(PE, wqkv_pe[wi % 2])
                        em.d(LD, lambda e, dwsrc=dwsrc, kch=kch:
                             e.dma_start(T.wld[...], dwsrc[:, kch, :]))
                    with em.on("vector"):
                        em.w(LD, LD.n)
                        em.fence()
                        em.i(lambda e, kch=kch, scale=scale, wbb=wbb:
                             e.tensor_scalar(wbb[:, kch, :], T.wld[...], scale,
                                             None, OP.mult))
                wcast = DV.n
                batches = [(384 * i, 384) for i in range(6)]
                slab_pe = [0] * len(batches)
                slab_cast = [0, 0]
                for si, (s0, wdt) in enumerate(batches):
                    xb = xtss[si % 2]
                    with em.on("gpsimd"):
                        em.w(DV, slab_cast[si % 2])
                        if si >= 1:
                            em.w(PE, slab_pe[si - 1])
                        em.d(L2, lambda e, s0=s0, wdt=wdt: e.dma_start(
                            T.xtf[:, :, 0:wdt], d_xT[:, :, s0:s0 + wdt]))
                    with em.on("vector"):
                        em.w(L2, L2.n)
                        em.fence()
                        em.i(lambda e, xb=xb, wdt=wdt: e.tensor_copy(
                            xb[:, :, 0:wdt], T.xtf[:, :, 0:wdt]))
                    slab_cast[si % 2] = DV.n
                    xcast = DV.n
                    for mch in range(8):
                        slot = slot_i % 6
                        slot_i += 1
                        pslot = psl[slot]
                        with em.on("tensor"):
                            em.w(DV, max(xcast, wcast))
                            em.w(AC, psl_free[slot])
                            for kch in range(8):
                                em.i(lambda e, pslot=pslot, kch=kch, mch=mch,
                                     xb=xb, wdt=wdt, wbb=wbb:
                                     e.matmul(pslot[:, 0:wdt],
                                              wbb[:, kch, 128 * mch:128 * (mch + 1)],
                                              xb[:, kch, 0:wdt],
                                              start=(kch == 0), stop=(kch == 7)),
                                     inc=(1 if kch == 7 else 0))

                        sgi = stg_i % 2
                        sgb = stgs[sgi]
                        stg_i += 1
                        with em.on("scalar"):
                            em.w(PE, PE.n)
                            em.w(ST, stg_free[sgi])
                            em.fence()
                            em.i(lambda e, sgb=sgb, pslot=pslot, wdt=wdt,
                                 bias=bias, mch=mch:
                                 e.activation(sgb[:, 0:wdt], pslot[:, 0:wdt],
                                              ACTF.Identity,
                                              bias=bias[:, mch:mch + 1]))
                        psl_free[slot] = AC.n
                        with em.on("sync"):
                            em.w(AC, AC.n)
                            em.d(ST, lambda e, sgb=sgb, dscr=dscr, mch=mch,
                                 s0=s0, wdt=wdt:
                                 e.dma_start(dscr[:, mch, s0:s0 + wdt],
                                             sgb[:, 0:wdt]))
                        stg_free[sgi] = ST.n
                    slab_pe[si] = PE.n
                wqkv_pe[wi % 2] = PE.n
            barrier()
            for kch in range(8):
                with em.on("sync"):
                    em.w(DV, DV.n)
                    em.d(LD, lambda e, kch=kch: e.dma_start(T.wotmp[...],
                                                            d_wo[:, kch, :]))
                with em.on("vector"):
                    em.w(LD, LD.n)
                    em.fence()
                    em.i(lambda e, kch=kch: e.tensor_copy(T.wob[:, kch, :],
                                                          T.wotmp[...]))
            barrier()
            em.mute = False
            if "B" not in phases:
                return
            bi = phases.index("B")
            BL = int(phases[bi + 1]) if (bi + 1 < len(phases)
                                         and phases[bi + 1].isdigit()) else 4

            slot_ps = [PT0, PT1, PT2]
            slot_free_ac = [0, 0, 0]
            ebuf_free_pe = [0, 0, 0]
            zq4_free_dv = [0, 0]
            mv_evac_dv = DV.n
            Ebufs = [T.Ebuf0, T.Ebuf1, T.Ebuf2]
            Qhs = [T.Qh0, T.Qh1]
            osst_st = [0] * NHEAD
            prev_mv_pe = 0
            prev_r_dv = 0
            prev_osst = 0
            head_score_pe = 0
            def k2_load_half(hh, half):
                t0 = 256 * hh
                if half == 0:
                    em.d(LD, lambda e, t0=t0: e.dma_start(
                        T.K2[0:64, 0:2048], d_ks[0:64, :, t0:t0 + 256]))
                    em.d(LD, lambda e, t0=t0: e.dma_start(
                        T.K2[64:128, 0:2048], d_ks[0:64, :, t0:t0 + 256]))
                    em.d(LD, lambda e, t0=t0, hh=hh: e.dma_start(
                        Qhs[hh % 2][...], d_qs[:, :, t0:t0 + 256]))
                else:
                    em.d(LD, lambda e, t0=t0: e.dma_start(
                        T.K2[0:64, 2048:4096], d_ks[64:128, :, t0:t0 + 256]))
                    em.d(LD, lambda e, t0=t0: e.dma_start(
                        T.K2[64:128, 2048:4096], d_ks[64:128, :, t0:t0 + 256]))

            if not noload:
                with em.on("sync"):
                    k2_load_half(0, 0)
            k2_ld = [LD.n, 0]
            for h in range(nheads):
                if not noload:
                    with em.on("sync"):
                        em.w(PE, head_score_pe if h >= 1 else PE.n)
                        k2_load_half(h, 1)
                k2_ld[1] = LD.n
                qh = Qhs[h % 2]

                def emit_chain(jj, dv_gate):
                    ebc = Ebufs[jj % 3]
                    with em.on("tensor"):
                        em.w(DV, dv_gate)
                        if jj == 0:
                            em.w(DV, mv_evac_dv)
                        for q in range(8):
                            pt = PT3 if q < 4 else PT4
                            em.i(lambda e, pt=pt, q=q, ebc=ebc, jj=jj:
                                 e.matmul(pt[32 * (q % 4):32 * (q % 4) + 1,
                                             0:512],
                                          rb16[:, jj % 2:jj % 2 + 1],
                                          ebc[:, 512 * q:512 * (q + 1)],
                                          start=(jj == 0), stop=(jj == 31),
                                          tile_position=(0, 32 * (q % 4))),
                                 inc=(1 if q == 7 else 0))
                    return PE.n

                r_dv_arr = [0] * 32
                for j in range(32 if BL >= 2 else 0):
                    jslots = [(4 * j + g) % 3 for g in range(4)]
                    order = [0, 1, 2, 3] if j % 2 == 0 else [2, 3, 0, 1]
                    first_pos = min(order.index(0), order.index(3))
                    later_g = order[max(order.index(0), order.index(3))]
                    mm_pe = {}
                    ac_start = AC.n

                    def score_mm(g):
                        s = jslots[g]
                        em.w(AC, slot_free_ac[s])
                        if g == later_g:
                            em.w(AC, ac_start + first_pos + 1)
                        par = 0 if g < 2 else 64
                        tp = (0, 0) if g < 2 else (64, 0)
                        base = 1024 * (g % 2)
                        for half in range(2):
                            em.i(lambda e, s=s, half=half, par=par, tp=tp,
                                 base=base, j=j, qh=qh:
                                 e.matmul(
                                     slot_ps[s][:, 512 * half:512 * (half + 1)],
                                     T.K2[par:par + 64, 128 * j:128 * (j + 1)],
                                     qh[par:par + 64, :, :]
                                     .rearrange("p a b -> p (a b)")
                                     [:, base + 512 * half:base + 512 * (half + 1)],
                                     start=True, stop=True, tile_position=tp),
                                 inc=(1 if half == 1 else 0))
                        mm_pe[g] = PE.n

                    with em.on("tensor"):
                        em.w(LD, k2_ld[0] if j < 16 else k2_ld[1])
                        score_mm(order[0])
                        score_mm(order[1])
                    if BL >= 3 and j >= 2:
                        prev_mv_pe = emit_chain(j - 2, r_dv_arr[j - 2])
                        ebuf_free_pe[(j - 2) % 3] = prev_mv_pe
                    with em.on("tensor"):
                        score_mm(order[2])
                        score_mm(order[3])
                    if j == 15 and not noload and h + 1 < nheads:
                        with em.on("sync"):
                            em.w(PE, mm_pe[order[3]])
                            k2_load_half(h + 1, 0)
                        k2_ld[0] = LD.n
                    eb = Ebufs[j % 3]
                    zo = 4 * (j % 2)
                    with em.on("scalar"):
                        em.w(PE, ebuf_free_pe[j % 3])
                        em.w(DV, zq4_free_dv[j % 2])
                        for g in order:
                            em.w(PE, mm_pe[g])
                            kw = ({"accum_out": zq4[:, zo + g:zo + g + 1]}
                                  if g < 2 else {})
                            em.i(lambda e, g=g, eb=eb, s=jslots[g], kw=kw:
                                 e.activation(eb[:, 1024 * g:1024 * (g + 1)],
                                              slot_ps[s][:, :], ACTF.Exp, **kw))
                            slot_free_ac[jslots[g]] = AC.n
                    with em.on("vector"):
                        em.w(AC, AC.n)
                        em.w(PE, prev_mv_pe)
                        for g in range(2, 4):
                            em.fence()
                            em.i(lambda e, g=g, eb=eb, zo=zo: e.tensor_reduce(
                                zq4[:, zo + g:zo + g + 1],
                                eb[:, 1024 * g:1024 * (g + 1)], AX.X, OP.add))
                        em.fence()
                        em.i(lambda e, zo=zo: e.tensor_add(
                            Zcol[...], zq4[:, zo:zo + 1], zq4[:, zo + 1:zo + 2]))
                        em.fence()
                        em.i(lambda e, zo=zo: e.tensor_add(
                            Zcol[...], Zcol[...], zq4[:, zo + 2:zo + 3]))
                        em.fence()
                        em.i(lambda e, zo=zo: e.tensor_add(
                            Zcol[...], Zcol[...], zq4[:, zo + 3:zo + 4]))
                        em.fence()
                        em.i(lambda e: e.reciprocal(rcol[...], Zcol[...]))
                        em.fence()
                        em.i(lambda e, j=j: e.tensor_copy(
                            rb16[:, j % 2:j % 2 + 1], rcol[...]))
                    prev_r_dv = DV.n
                    r_dv_arr[j] = DV.n
                    zq4_free_dv[j % 2] = DV.n
                    if j == 31:
                        head_score_pe = mm_pe[order[3]]
                if BL >= 3:
                    prev_mv_pe = emit_chain(30, r_dv_arr[30])
                    ebuf_free_pe[30 % 3] = prev_mv_pe
                    prev_mv_pe = emit_chain(31, r_dv_arr[31])
                    ebuf_free_pe[31 % 3] = prev_mv_pe
                if BL >= 4:
                    with em.on("vector"):
                        em.w(PE, PE.n)
                        em.w(ST, prev_osst)
                        for q in range(8):
                            pt = PT3 if q < 4 else PT4
                            em.fence()
                            em.i(lambda e, pt=pt, q=q: e.tensor_copy(
                                T.ossb[32 * (q % 4):32 * (q % 4) + 1,
                                       512 * (q // 4):512 * (q // 4) + 512],
                                pt[32 * (q % 4):32 * (q % 4) + 1, 0:512]))
                    mv_evac_dv = DV.n
                    with em.on("sync"):
                        em.w(DV, mv_evac_dv)
                        for q in range(8):
                            em.d(ST, lambda e, h=h, q=q: e.dma_start(
                                d_oss[h, 0:1, 512 * q:512 * (q + 1)],
                                T.ossb[32 * (q % 4):32 * (q % 4) + 1,
                                       512 * (q // 4):512 * (q // 4) + 512]))
                osst_st[h] = ST.n
                prev_osst = ST.n
            barrier()
            if "C" not in phases:
                return

            wo_free_dv = [DV.n, DV.n]
            wt_slot_free = [0, 0]
            prev_vsc = 0
            prev_lnin = [0, 0]
            prev_sq_ac = 0
            prev_m1tr = 0
            prev_ust = 0
            prev_xqst = 0
            prev_wbc = 0
            osld = 0
            for t in range(NT):
                h = t // 2
                hb = t % 2
                if hb == 0:
                    with em.on("sync"):
                        em.w(ST, osst_st[h])
                        em.w(PE, prev_wbc)
                        em.d(LD, lambda e, h=h: e.dma_start(T.osrow[...],
                                                            d_oss[h, :, :]))
                    osld = LD.n
                with em.on("gpsimd"):
                    em.w(DV, max(prev_vsc, prev_lnin[0]))
                    em.d(L2, lambda e, t=t: e.dma_start(
                        T.Vt[...], d_vs[:, :, 128 * t:128 * (t + 1)]))
                    em.d(L2, lambda e, t=t: e.dma_start(T.Xt0[...],
                                                        d_xtm[t, :, :]))
                vx_ld = L2.n
                os3 = T.osrow[0:1, :].rearrange("o (q c) -> o q c", c=256)
                with em.on("tensor"):
                    em.w(LD, osld)
                    em.w(DV, prev_vsc)
                    for b4 in range(2):
                        em.i(lambda e, b4=b4, hb=hb, os3=os3:
                             e.matmul(PT2[0:64, 512 * b4:512 * b4 + 512],
                                      ones[0:1, 0:64],
                                      os3[0:1, 4 * b4:4 * b4 + 4,
                                          128 * hb:128 * hb + 128],
                                      start=True, stop=True,
                                      tile_position=(0, 0)),
                             inc=0)
                    for b4 in range(2):
                        em.i(lambda e, b4=b4, hb=hb, os3=os3:
                             e.matmul(PT2[64:128, 512 * b4:512 * b4 + 512],
                                      ones[0:1, 0:64],
                                      os3[0:1, 8 + 4 * b4:8 + 4 * b4 + 4,
                                          128 * hb:128 * hb + 128],
                                      start=True, stop=True,
                                      tile_position=(0, 64)),
                             inc=(1 if b4 == 1 else 0))
                with em.on("vector"):
                    em.w(PE, PE.n)
                    em.w(L2, vx_ld)
                    em.fence()
                    em.i(lambda e: e.tensor_mul(
                        T.Ot[...], T.Vt[...],
                        PT2[:, 0:1024].rearrange("p (a b) -> p a b", b=128)))
                prev_wbc = PE.n
                prev_vsc = DV.n
                wps = [PT0, PT1][t % 2]
                with em.on("tensor"):
                    em.w(DV, max(prev_vsc, wo_free_dv[t % 2]))
                    for half in range(2):
                        for kch in range(8):
                            em.i(lambda e, wps=wps, half=half, kch=kch:
                                 e.matmul(wps[:, 512 * half:512 * (half + 1)],
                                          T.Ot[:, kch, :],
                                          T.wob[:, kch, 512 * half:512 * (half + 1)],
                                          start=(kch == 0), stop=False),
                                 inc=0)
                        em.i(lambda e, wps=wps, half=half:
                             e.matmul(wps[:, 512 * half:512 * (half + 1)],
                                      ones[0:1, :],
                                      borow[0:1, 512 * half:512 * (half + 1)],
                                      start=False, stop=True),
                             inc=(1 if half == 1 else 0))
                wo_done = PE.n
                with em.on("vector"):
                    em.w(PE, wo_done)
                    em.w(AC, prev_sq_ac)
                    em.fence()
                    em.i(lambda e, wps=wps: e.tensor_add(
                        T.lnin[...], wps[...], T.Xt0[...]))
                wo_free_dv[t % 2] = DV.n
                prev_lnin[0] = DV.n
                lnin_dv = DV.n
                with em.on("scalar"):
                    em.w(DV, lnin_dv)
                    em.fence()
                    em.i(lambda e: e.activation(T.qsc[...], T.lnin[...], ACTF.Square,
                                                accum_out=stat[:, 1:2]))
                prev_sq_ac = AC.n
                with em.on("vector"):
                    em.w(AC, AC.n)
                    em.w(PE, prev_m1tr)
                    em.fence()
                    em.i(lambda e: e.tensor_scalar(T.qsc[...], T.lnin[...], 1.0,
                                                   None, OP.mult, OP.add,
                                                   accum_out=stat[:, 0:1]))
                    em.fence()
                    em.i(lambda e: e.tensor_scalar(stat[:, 2:3], stat[:, 0:1],
                                                   -1.0 / 1024.0, None, OP.mult))
                    em.fence()
                    em.i(lambda e: e.tensor_scalar(stat[:, 3:4], stat[:, 1:2],
                                                   1.0 / 1024.0, EPS, OP.mult,
                                                   OP.add))
                    em.fence()
                    em.i(lambda e: e.tensor_mul(stat[:, 4:5], stat[:, 2:3],
                                                stat[:, 2:3]))
                    em.fence()
                    em.i(lambda e: e.tensor_sub(stat[:, 3:4], stat[:, 3:4],
                                                stat[:, 4:5]))
                with em.on("scalar"):
                    em.w(DV, DV.n)
                    em.fence()
                    em.i(lambda e: e.activation(stat[:, 4:5], stat[:, 3:4], ACTF.Ln))
                    em.fence()
                    em.i(lambda e: e.activation(stat[:, 5:6], stat[:, 4:5],
                                                ACTF.Exp, scale=neghalf[...]))
                rstd_ac = AC.n
                with em.on("vector"):
                    em.w(AC, rstd_ac)
                    em.w(ST, prev_ust)
                    em.fence()
                    em.i(lambda e: e.tensor_scalar(T.utile[...], T.lnin[...],
                                                   stat[:, 2:3], stat[:, 5:6],
                                                   OP.add, OP.mult))
                    em.fence()
                    em.i(lambda e: e.tensor_reduce(stat[:, 6:7], T.utile[...],
                                                   AX.X, OP.max,
                                                   apply_absolute_value=True))
                    em.fence()
                    em.i(lambda e: e.tensor_scalar(stat[:, 6:7], stat[:, 6:7], 1e-5,
                                                   None, OP.max))
                    em.fence()
                    em.i(lambda e: e.reciprocal(stat[:, 7:8], stat[:, 6:7]))
                    em.fence()
                    em.i(lambda e: e.tensor_scalar(stat[:, 7:8], stat[:, 7:8],
                                                   127.0, None, OP.mult))
                    em.fence()
                    em.i(lambda e: e.tensor_scalar(stat[:, 6:7], stat[:, 6:7],
                                                   1.0 / 127.0, None, OP.mult))
                ufin_dv = DV.n
                with em.on("sync"):
                    em.w(DV, ufin_dv)
                    em.d(ST, lambda e, t=t: e.dma_start(d_us[t, :, :], T.utile[...]))
                prev_ust = ST.n
                with em.on("tensor"):
                    em.w(DV, ufin_dv)
                    em.i(lambda e: e.transpose(PT3[0:1, 0:128], stat[:, 6:7],
                                               ident[...]))
                prev_m1tr = PE.n
                with em.on("vector"):
                    em.w(PE, prev_m1tr)
                    em.fence()
                    em.i(lambda e, t=t: e.tensor_copy(
                        m1row[0:1, 128 * t:128 * (t + 1)], PT3[0:1, 0:128]))
                with em.on("vector"):
                    zv = T.qsc[:, 0:512]
                    cv = T.qsc[:, 512:1024]
                    for hf in range(2):
                        sl = slice(512 * hf, 512 * (hf + 1))
                        em.fence()
                        em.i(lambda e, sl=sl, zv=zv: e.tensor_scalar(
                            zv, T.utile[:, sl], stat[:, 7:8], 0.5,
                            OP.mult, OP.add))
                        em.fence()
                        em.i(lambda e, zv=zv: e.tensor_copy(T.zich[...], zv))
                        em.fence()
                        em.i(lambda e, zv=zv, cv=cv: e.tensor_tensor(
                            cv, T.zich[...], zv, OP.is_gt))
                        em.fence()
                        em.i(lambda e, sl=sl, cv=cv: e.tensor_sub(
                            T.xqtm[:, sl], T.zich[...], cv))
                q1_dv = DV.n
                with em.on("tensor"):
                    em.w(DV, q1_dv)
                    for ch in range(8):
                        em.i(lambda e, ch=ch, wps=wps: e.transpose(
                            wps[:, 128 * ch:128 * (ch + 1)],
                            T.xqtm[:, 128 * ch:128 * (ch + 1)], ident[...]))
                xqtr_pe = PE.n
                with em.on("vector"):
                    em.w(PE, xqtr_pe)
                    em.w(ST, prev_xqst)
                    em.fence()
                    em.i(lambda e, wps=wps: e.tensor_copy(
                        T.xqTt[...],
                        wps[...].rearrange("p (a b) -> p a b", b=128)))
                wo_free_dv[t % 2] = DV.n
                with em.on("sync"):
                    em.w(DV, DV.n)
                    em.d(ST, lambda e, t=t: e.dma_start(d_xqs[t, :, :, :],
                                                        T.xqTt[...]))
                prev_xqst = ST.n
            barrier()
            if "D" not in phases:
                return

            with em.on("scalar"):
                em.i(lambda e: _load_act_set(e, 18), inc=0)
            xqts = [T.xqt0, T.xqt1]
            with em.on("vector"):
                em.fence()
                em.i(lambda e: e.tensor_copy(mhrow[0:1, 0:1],
                                             m1row[0:1, 2303:2304]))
                em.fence()
                em.i(lambda e: e.tensor_scalar(mhrow[0:1, 0:1], mhrow[0:1, 0:1],
                                               sw1[...], None, OP.mult))
            m12h_dv = DV.n
            with em.on("sync"):
                em.d(LD, lambda e: e.dma_start(T.xqt0[:, :, 0:128],
                                               d_xqs[17, :, :, :]))
            halo_ld = LD.n
            with em.on("vector"):
                em.w(LD, halo_ld)
                em.fence()
                em.i(lambda e: e.tensor_copy(T.xqth[...], T.xqt0[:, :, 127:128]))
            halo_dv = DV.n
            with em.on("tensor"):
                em.w(DV, halo_dv)
                for och in range(32):
                    for kch in range(8):
                        em.i(lambda e, och=och, kch=kch:
                             e.matmul(PT4[:, och:och + 1],
                                      w1qT[:, kch, 128 * och:128 * (och + 1)],
                                      T.xqth[:, kch, 0:1],
                                      start=(kch == 0), stop=(kch == 7)),
                             inc=(1 if (och == 31 and kch == 7) else 0))
            with em.on("tensor"):
                em.w(DV, m12h_dv)
                em.i(lambda e: e.matmul(PT3[:, 0:1], ones[0:1, :], mhrow[0:1, 0:1],
                                        start=True, stop=True))
            mh_pe = PE.n
            with em.on("vector"):
                em.w(PE, mh_pe)
                em.fence()
                em.i(lambda e: e.tensor_copy(T.mhsb[...], PT3[:, 0:1]))
                em.fence()
                em.i(lambda e: e.tensor_mul(
                    T.zedge[...],
                    PT4[:, 0:32].rearrange("p (a b) -> p a b", b=1),
                    T.mhsb[...].rearrange("p (o b) -> p o b", o=1)
                    .broadcast_to([128, 32, 1])))
                em.fence()
                em.i(lambda e: e.tensor_mul(
                    T.zedge[...], T.zedge[...],
                    masklr[:, 0:1].rearrange("p (o b) -> p o b", o=1)
                    .broadcast_to([128, 32, 1])))
            zedge_dv = DV.n

            f1_slot_free = zedge_dv
            f2_free_dv = DV.n
            htile_free_dv = 0
            ldx_pe_wait = 0
            yst_st = {}
            uld_free_dv = 0
            q2_pe = [0, 0]
            prev_small = DV.n
            fused_sq_ac = 0
            ogroups = [(0, 3), (3, 6), (6, 9), (9, 12), (12, 15), (15, 18),
                       (18, 21), (21, 24), (24, 27), (27, 30), (30, 32)]
            with em.on("sync"):
                em.w(DV, halo_dv)
                em.d(LD, lambda e: e.dma_start(xqts[0][:, :, 0:128],
                                               d_xqs[0, :, :, :]))
            xq_ld_t = [LD.n] + [0] * NT
            for t in range(NOWN):
                with em.on("sync"):
                    if t >= 1:
                        em.w(PE, ldx_pe_wait)
                    em.w(DV, uld_free_dv)
                    em.d(LD, lambda e, t=t: e.dma_start(
                        xqts[(t + 1) % 2][:, :, 0:128], d_xqs[t + 1, :, :, :]))
                    em.d(LD, lambda e, t=t: e.dma_start(T.uld[...], d_us[t, :, :]))
                xq_ld_t[t + 1] = LD.n
                xq_ld = LD.n
                with em.on("vector"):
                    em.w(LD, xq_ld)
                    em.fence()
                    em.i(lambda e, t=t: e.tensor_copy(
                        xqts[t % 2][:, :, 128:129], xqts[(t + 1) % 2][:, :, 0:1]))
                xcol_dv = DV.n
                with em.on("tensor"):
                    em.w(DV, max(m12h_dv, prev_small))
                    em.i(lambda e, t=t: e.matmul(PT2[:, 0:129], sw1row[...],
                                                 m1row[0:1, 128 * t:128 * t + 129],
                                                 start=True, stop=True))
                with em.on("vector"):
                    em.w(PE, PE.n)
                    em.fence()
                    em.i(lambda e: e.tensor_copy(T.m12sb[...], PT2[:, 0:129]))
                m12t_dv = DV.n
                with em.on("vector"):
                    em.w(AC, fused_sq_ac)
                    em.fence()
                    em.i(lambda e: e.tensor_copy(T.zti[:, :, 0:1], T.zedge[...]))
                def sweeps(p0, p1, pac):
                    with em.on("vector"):
                        em.w(AC, pac)
                        em.fence()
                        for och in range(p0, p1):
                            em.i(lambda e, och=och: e.scalar_tensor_tensor(
                                T.htile[:, och, :], T.zti[:, och, 0:128],
                                cwa[:, och:och + 1], T.htile[:, och, :],
                                OP.mult, OP.add))
                        em.fence()
                        for och in range(p0, p1):
                            em.i(lambda e, och=och: e.scalar_tensor_tensor(
                                T.htile[:, och, :], T.zti[:, och, 2:130],
                                cwc[:, och:och + 1], T.htile[:, och, :],
                                OP.mult, OP.add))

                prev_grp = None
                for (o0, o1) in ogroups:
                    ng = o1 - o0
                    with em.on("tensor"):
                        em.w(LD, xq_ld)
                        em.w(DV, max(f1_slot_free, xcol_dv))
                        for oo in range(ng):
                            och = o0 + oo
                            for kch in range(8):
                                em.i(lambda e, oo=oo, och=och, kch=kch, t=t:
                                     e.matmul(PT3[:, 129 * oo:129 * oo + 129],
                                              w1qT[:, kch, 128 * och:128 * (och + 1)],
                                              xqts[t % 2][:, kch, 0:129],
                                              start=(kch == 0), stop=(kch == 7)),
                                     inc=(1 if (oo == ng - 1 and kch == 7) else 0))
                    with em.on("vector"):
                        em.w(PE, PE.n)
                        em.w(DV, m12t_dv)
                        em.fence()
                        em.i(lambda e, o0=o0, ng=ng: e.tensor_mul(
                            T.zti[:, o0:o0 + ng, 1:130],
                            PT3[:, 0:129 * ng].rearrange("p (a b) -> p a b", b=129),
                            T.m12sb[...].rearrange("p (o b) -> p o b", o=1)
                            .broadcast_to([128, ng, 129])))
                        if t == NOWN - 1:
                            em.fence()
                            em.i(lambda e, o0=o0, ng=ng: e.tensor_mul(
                                T.zti[:, o0:o0 + ng, 129:130],
                                T.zti[:, o0:o0 + ng, 129:130],
                                masklr[:, 1:2].rearrange("p (o b) -> p o b", o=1)
                                .broadcast_to([128, ng, 1])))
                        em.fence()
                        em.i(lambda e, o0=o0, ng=ng: e.tensor_copy(
                            T.zedge[:, o0:o0 + ng, :],
                            T.zti[:, o0:o0 + ng, 128:129]))
                    f1_slot_free = DV.n
                    with em.on("scalar"):
                        em.w(DV, f1_slot_free)
                        if o0 == 0:
                            em.w(DV, htile_free_dv)
                        for och in range(o0, o1):
                            em.i(lambda e, och=och: e.activation(
                                T.htile[:, och, :], T.zti[:, och, 1:129],
                                ACTF.Identity, scale=cwb[:, och:och + 1],
                                bias=cb32[:, och:och + 1]))
                    if prev_grp is not None:
                        sweeps(*prev_grp)
                    prev_grp = (o0, o1, AC.n)
                ldx_pe_wait = PE.n
                sweeps(*prev_grp)
                conv_dv = DV.n
                with em.on("scalar"):
                    em.w(DV, conv_dv)
                    em.fence()
                    em.i(lambda e: e.activation(T.htile[...], T.htile[...],
                                                ACTF.Silu))
                silu_ac = AC.n
                with em.on("vector"):
                    em.w(AC, silu_ac)
                    em.fence()
                    em.i(lambda e: e.tensor_reduce(
                        T.Mp[...], T.htile[...].rearrange("p a b -> p b a"),
                        AX.X, OP.max, apply_absolute_value=True))
                mnmx_dv = DV.n
                with em.on("tensor"):
                    em.w(DV, mnmx_dv)
                    em.i(lambda e: e.transpose(PT4[0:128, 0:128], T.Mp[...],
                                               ident[...]))
                mtr_pe = PE.n
                with em.on("vector"):
                    em.w(PE, mtr_pe)
                    em.fence()
                    em.i(lambda e: e.tensor_reduce(q2st[:, 0:1], PT4[:, 0:128],
                                                   AX.X, OP.max))
                    em.fence()
                    em.i(lambda e: e.tensor_scalar(q2st[:, 0:1], q2st[:, 0:1],
                                                   1e-5, None, OP.max))
                    em.fence()
                    em.i(lambda e: e.reciprocal(q2st[:, 1:2], q2st[:, 0:1]))
                    em.fence()
                    em.i(lambda e: e.tensor_scalar(q2st[:, 1:2], q2st[:, 1:2],
                                                   127.0, None, OP.mult))
                    em.fence()
                    em.i(lambda e: e.tensor_scalar(q2st[:, 2:3], q2st[:, 0:1],
                                                   sw2col[...], None, OP.mult))
                    em.fence()
                    em.i(lambda e: e.tensor_scalar(q2st[:, 2:3], q2st[:, 2:3],
                                                   1.0 / 127.0, None, OP.mult))
                q2s_dv = DV.n
                with em.on("tensor"):
                    em.w(DV, q2s_dv)
                    em.i(lambda e: e.transpose(PT4[0:1, 256:384], q2st[:, 1:2],
                                               ident[...]))
                with em.on("vector"):
                    em.w(PE, PE.n)
                    em.fence()
                    em.i(lambda e: e.tensor_copy(s2row[0:1, 0:128],
                                                 PT4[0:1, 256:384]))
                with em.on("tensor"):
                    em.w(DV, DV.n)
                    em.i(lambda e: e.matmul(PT4[:, 384:512], ones[0:1, :],
                                            s2row[0:1, 0:128],
                                            start=True, stop=True))
                with em.on("vector"):
                    em.w(PE, PE.n)
                    em.fence()
                    em.i(lambda e: e.tensor_copy(T.s2sb[...], PT4[:, 384:512]))
                prev_small = DV.n
                fps = [PT0, PT1][t % 2]
                for hf in range(2):
                    x2b = [T.x2q, T.x2qb][hf]
                    xs = T.zti[:, 0:16, 1:129]
                    cs = T.zti[:, 16:32, 1:129]
                    with em.on("vector"):
                        em.w(PE, q2_pe[hf])
                        em.fence()
                        em.i(lambda e, hf=hf, xs=xs: e.tensor_mul(
                            xs, T.htile[:, 16 * hf:16 * (hf + 1), :],
                            T.s2sb[...].rearrange("p (o b) -> p o b", o=1)
                            .broadcast_to([128, 16, 128])))
                        em.fence()
                        em.i(lambda e, xs=xs: e.tensor_scalar(
                            xs, xs, 1.0, 0.5, OP.mult, OP.add))
                        em.fence()
                        em.i(lambda e, xs=xs: e.tensor_copy(T.zi16[...], xs))
                        em.fence()
                        em.i(lambda e, xs=xs, cs=cs: e.tensor_tensor(
                            cs, T.zi16[...], xs, OP.is_gt))
                        em.fence()
                        em.i(lambda e, x2b=x2b, cs=cs: e.tensor_sub(
                            x2b[...], T.zi16[...], cs))
                    x2q_dv = DV.n
                    with em.on("tensor"):
                        em.w(DV, max(x2q_dv, f2_free_dv))
                        for half in range(2):
                            for kk in range(16):
                                kch = 16 * hf + kk
                                em.i(lambda e, fps=fps, half=half, kk=kk, kch=kch,
                                     x2b=x2b:
                                     e.matmul(fps[:, 512 * half:512 * (half + 1)],
                                              x2b[:, kk, :],
                                              w2qT[:, kch, 512 * half:512 * (half + 1)],
                                              start=(kch == 0),
                                              stop=(kch == 31)),
                                     inc=(1 if (half == 1 and kk == 15) else 0))
                    q2_pe[hf] = PE.n
                f2_pe = PE.n
                htile_free_dv = DV.n
                with em.on("vector"):
                    em.w(PE, f2_pe)
                    em.w(LD, xq_ld)
                    em.w(ST, yst_st.get(t - 1, 0))
                    em.fence()
                    em.i(lambda e, fps=fps: e.tensor_scalar(
                        T.ypre[...], fps[...], q2st[:, 2:3], None, OP.mult))
                    em.fence()
                    em.i(lambda e: e.tensor_add(T.ypre[...], T.ypre[...],
                                                T.uld[...]))
                f2_free_dv = DV.n
                uld_free_dv = DV.n
                ypre_dv = DV.n
                sink = T.zti[:, 8:16, 1:129]
                with em.on("scalar"):
                    em.w(DV, ypre_dv)
                    em.i(lambda e: _load_act_set(e, 6), inc=0)
                    em.fence()
                    em.i(lambda e, sink=sink: e.activation(
                        sink, T.ypre[...], ACTF.Square,
                        accum_out=T.est[:, 1:2]))
                fused_sq_ac = AC.n
                with em.on("vector"):
                    em.w(AC, fused_sq_ac)
                    em.fence()
                    em.i(lambda e, sink=sink: e.tensor_scalar(
                        sink, T.ypre[...], 1.0, None, OP.mult, OP.add,
                        accum_out=T.est[:, 0:1]))
                    em.fence()
                    em.i(lambda e: e.tensor_scalar(T.est[:, 2:3], T.est[:, 0:1],
                                                   -1.0 / 1024.0, None, OP.mult))
                    em.fence()
                    em.i(lambda e: e.tensor_scalar(T.est[:, 3:4], T.est[:, 1:2],
                                                   1.0 / 1024.0, EPS, OP.mult,
                                                   OP.add))
                    em.fence()
                    em.i(lambda e: e.tensor_mul(T.est[:, 4:5], T.est[:, 2:3],
                                                T.est[:, 2:3]))
                    em.fence()
                    em.i(lambda e: e.tensor_sub(T.est[:, 3:4], T.est[:, 3:4],
                                                T.est[:, 4:5]))
                est_dv = DV.n
                with em.on("scalar"):
                    em.w(DV, est_dv)
                    em.fence()
                    em.i(lambda e: e.activation(T.est[:, 4:5], T.est[:, 3:4],
                                                ACTF.Ln))
                    em.fence()
                    em.i(lambda e: e.activation(T.est[:, 5:6], T.est[:, 4:5],
                                                ACTF.Exp, scale=neghalf[...]))
                    em.i(lambda e: _load_act_set(e, 18), inc=0)
                rs_ac = AC.n
                with em.on("vector"):
                    em.w(AC, rs_ac)
                    em.fence()
                    em.i(lambda e: e.tensor_scalar(T.ypre[...], T.ypre[...],
                                                   T.est[:, 2:3], T.est[:, 5:6],
                                                   OP.add, OP.mult))
                with em.on("sync"):
                    em.w(DV, DV.n)
                    em.d(ST, lambda e, t=t: e.dma_start(
                        d_out[128 * t:128 * (t + 1), :], T.ypre[...]))
                yst_st[t] = ST.n
            with em.on("sync"):
                em.w(ST, ST.n)
                em.w(L2, L2.n)

        with nc.Block() as blk:
            @blk.sync
            def _(e):
                program(Em("sync", e))

            @blk.tensor
            def _(e):
                program(Em("tensor", e))

            @blk.vector
            def _(e):
                program(Em("vector", e))

            @blk.scalar
            def _(e):
                _load_act_set(e, 6)
                program(Em("scalar", e))

            @blk.gpsimd
            def _(e):
                program(Em("gpsimd", e))
    return nc


_NC_CACHE = {}


def _get_program():
    if "nc" not in _NC_CACHE:
        _NC_CACHE["nc"] = build_program()
    return _NC_CACHE["nc"]


def _stage_core(x, c, consts):
    n, half = c // 2, c % 2
    own = x[n, 2048 * half:2048 * half + 2048]
    if half == 0:
        foreign = x[n, 2048:2048 + 256]
        maskL, maskR = 0.0, 1.0
    else:
        foreign = x[n, 2048 - 256:2048]
        maskL, maskR = 1.0, 0.0
    win = np.concatenate([own, foreign], axis=0)
    m = dict(consts)
    m["xw_tm"] = np.ascontiguousarray(win.reshape(NT, 128, 1024))
    m["xw_T"] = np.ascontiguousarray(win.T.reshape(8, 128, TW).transpose(1, 0, 2))
    m["maskLR"] = np.broadcast_to(np.array([maskL, maskR], np.float32),
                                  (128, 2)).copy()
    return m


def kernel(L0_attn_X_embed, L0_attn_Wq, L0_attn_bq, L0_attn_Wk, L0_attn_bk,
           L0_attn_Wv, L0_attn_bv, L0_attn_Wo, L0_attn_bo, L0_ffn_w1,
           L0_ffn_conv_w, L0_ffn_conv_b, L0_ffn_w2):
    x = np.ascontiguousarray(np.asarray(L0_attn_X_embed, np.float32))
    f32 = lambda a: np.ascontiguousarray(np.asarray(a, np.float32))

    def chunk_major(w):
        w = f32(w)
        return np.ascontiguousarray(w.reshape(8, 128, w.shape[1]).transpose(1, 0, 2))

    consts = {
        "wq": chunk_major(L0_attn_Wq), "wk": chunk_major(L0_attn_Wk),
        "wv": chunk_major(L0_attn_Wv), "wo": chunk_major(L0_attn_Wo),
        "w1T": chunk_major(f32(L0_ffn_w1).T),
        "w2T": np.ascontiguousarray(
            f32(L0_ffn_w2).T.reshape(32, 128, 1024).transpose(1, 0, 2)),
        "bq_col": np.ascontiguousarray(f32(L0_attn_bq).reshape(8, 128).T),
        "bk_col": np.ascontiguousarray(f32(L0_attn_bk).reshape(8, 128).T),
        "bv_col": np.ascontiguousarray(f32(L0_attn_bv).reshape(8, 128).T),
        "bo_row": f32(L0_attn_bo).reshape(1, 1024),
        "cw3a": np.ascontiguousarray(f32(L0_ffn_conv_w)[:, 0, 0].reshape(32, 128).T),
        "cw3b": np.ascontiguousarray(f32(L0_ffn_conv_w)[:, 0, 1].reshape(32, 128).T),
        "cw3c": np.ascontiguousarray(f32(L0_ffn_conv_w)[:, 0, 2].reshape(32, 128).T),
        "cb32": np.ascontiguousarray(f32(L0_ffn_conv_b).reshape(32, 128).T),
        "ones128": np.ones((128, 128), np.float32),
        "ones_row": np.ones((1, 384), np.float32),
        "ident": np.eye(128, dtype=np.float32),
        "neghalf": np.full((128, 1), -0.5, np.float32),
    }
    in_maps = [_stage_core(x, c, consts) for c in range(8)]
    nc = _get_program()
    res = run_bass_kernel_spmd(nc, in_maps, core_ids=list(range(8)))
    out = np.empty((4, 4096, 1024), np.float32)
    for c in range(8):
        n, half = c // 2, c % 2
        out[n, 2048 * half:2048 * half + 2048] = res.results[c]["out"]
    return out
